# version 13
# baseline (speedup 1.0000x reference)
import numpy as np
import ml_dtypes
from contextlib import ExitStack

import concourse.bass as bass
import concourse.mybir as mybir
from concourse.bass_utils import run_bass_kernel_spmd
from concourse.hw_specs import get_activation_tables

F32 = mybir.dt.float32
BF16 = mybir.dt.bfloat16
AF = mybir.ActivationFunctionType
OP = mybir.AluOpType
AX = mybir.AxisListType

B = 8
D = 1024
T = 4096
FC = 8
TC = 512
NCH = T // TC
SUB = 128
NB2 = 64
NEXP = 8
HDIM = 4096
HC = 512
MF = 64
INNER = 16

LND_CLAMP = -85.0

_AF = mybir.ActivationFunctionType
ACT_NEUTRAL = {_AF.Identity, _AF.Copy, _AF.Square, _AF.Relu, _AF.Abs, _AF.Sign, _AF.MemsetZero}
ACT_FUNC_SET = {_AF.Exp: "nlexp", _AF.Ln: "nlexp", _AF.Sigmoid: "sigmoid",
                _AF.Gelu: "gelu", _AF.Tanh: "gelu", _AF.Silu: "silu"}
ACT_SET_NAMES = {"nlexp": "natural_log_exp_and_others", "sigmoid": "sigmoid_and_others",
                 "gelu": "gelu_and_others", "silu": "silu_and_others"}
ACT_SET_REP = {"nlexp": _AF.Exp, "sigmoid": _AF.Sigmoid, "gelu": _AF.Gelu,
               "silu": _AF.Silu}


class Buf:
    __slots__ = ("w", "r", "dsem")

    def __init__(self):
        self.w = {}
        self.r = {}
        self.dsem = None


class Prog:
    ENGS = ("pe", "ve", "se", "po", "sy")

    def __init__(self, nc):
        self.nc = nc
        self.es = ExitStack()
        self.E = {"pe": nc.tensor, "ve": nc.vector, "se": nc.scalar,
                  "po": nc.gpsimd, "sy": nc.sync}
        self.sem = {}
        self.cnt = {}
        for s in ("pe", "ve", "se", "po", "qsy", "qse", "qpo"):
            self.sem[s] = self.es.enter_context(nc.semaphore("sem_" + s))
            self.cnt[s] = 0
        self.waited = {}
        self.engsem = {"pe": "pe", "ve": "ve", "se": "se", "po": "po"}
        self.drained = {"ve": 0, "se": 0, "po": 0}
        self.drain_marks = {"ve": [], "se": [], "po": []}
        self.default_scope = None
        self.extra_dep = None
        self.uid = 0

    def name(self, p):
        self.uid += 1
        return f"{p}_{self.uid}"

    def S(self, p, shape, dt, scope=None):
        side = "left" if scope is None else "right"
        t = (scope or self.default_scope or self.es).enter_context(
            self.nc.sbuf_tensor(self.name(p), shape, dt, side=side))
        return t, Buf()

    def cast_staging(self):
        if not hasattr(self, "_cstg"):
            sc = ExitStack()
            self.es.callback(sc.close)
            stgf, _ = self.S("cstgf", [128, 2 * 512], F32, scope=sc)
            stgb, _ = self.S("cstgb", [128, 2 * 512], BF16, scope=sc)
            self._cstg = (stgf, stgb, [(Buf(), Buf()), (Buf(), Buf())], [0])
            self._cast_pend = None
        return self._cstg

    def cast_tile(self, src_ap, dst_ap, dst_buf, src_buf=None):
        stgf, stgb, slotbufs, slot = self.cast_staging()
        sl = slot[0]
        slot[0] = 1 - slot[0]
        fb, bb = slotbufs[sl]
        fap = stgf[:, sl * 512:(sl + 1) * 512]
        bap = stgb[:, sl * 512:(sl + 1) * 512]
        self.dma("po", fap, src_ap, reads=(src_buf,) if src_buf is not None else (),
                 writes=(fb,))
        pend = self._cast_pend
        self._cast_pend = (fap, bap, fb, bb, dst_ap, dst_buf)
        if pend is not None:
            self._cast_emit(pend)

    def _cast_emit(self, pend):
        fap, bap, fb, bb, dst_ap, dst_buf = pend
        self.op("po", lambda e, bap=bap, fap=fap: e.tensor_copy(bap, fap),
                reads=(fb,), writes=(bb,))
        self.dma("po", dst_ap, bap, reads=(bb,), writes=(dst_buf,))

    def cast_flush(self):
        if getattr(self, "_cast_pend", None) is not None:
            self._cast_emit(self._cast_pend)
            self._cast_pend = None

    def barrier(self):
        snap = dict(self.cnt)
        for e in self.ENGS:
            for s, v in snap.items():
                if v > 0:
                    self._wait(e, s, v)

    def PS(self, p, shape, dt=F32, scope=None):
        t = (scope or self.default_scope or self.es).enter_context(
            self.nc.psum_tensor(self.name(p), shape, dt))
        return t, Buf()

    def DR(self, p, shape, dt, kind="Internal"):
        t = self.nc.dram_tensor(self.name(p) if kind == "Internal" else p, shape, dt, kind=kind)
        return t, Buf()

    CLEAN_OP_EDGES = {("se", "pe"), ("ve", "pe"), ("po", "pe"), ("po", "ve"),
                      ("ve", "se")}

    def _wait(self, eng, semname, val, skip_same=True, need_drain=True, dma=False):
        if skip_same and self.engsem.get(eng) == semname:
            return
        if need_drain and not dma and (semname, eng) in self.CLEAN_OP_EDGES:
            need_drain = False
        if need_drain and semname in self.drain_marks and eng != semname:
            cover = None
            for covered, dcnt in self.drain_marks[semname]:
                if covered >= val:
                    cover = dcnt
                    break
            if cover is None:
                covered = self.cnt[semname]
                dinst = self.E[semname].drain()
                self.cnt[semname] += 1
                dinst.then_inc(self.sem[semname], 1)
                cover = self.cnt[semname]
                self.drain_marks[semname].append((covered, cover))
                self.drained[semname] = cover
            val = cover
        key = (eng, semname)
        if self.waited.get(key, 0) >= val:
            return
        self.E[eng].wait_ge(self.sem[semname], val)
        self.waited[key] = val

    def _deps(self, eng, reads, writes, skip_same=True, need_drain=True, dma=False):
        need = {}
        if self.extra_dep is not None:
            for s, v in self.extra_dep.items():
                if need.get(s, 0) < v:
                    need[s] = v
        for b in reads:
            for s, v in b.w.items():
                if need.get(s, 0) < v:
                    need[s] = v
        for b in writes:
            for s, v in b.w.items():
                if need.get(s, 0) < v:
                    need[s] = v
            for s, v in b.r.items():
                if need.get(s, 0) < v:
                    need[s] = v
        for s, v in need.items():
            self._wait(eng, s, v, skip_same=skip_same, need_drain=need_drain, dma=dma)

    def _finish(self, sname, val, reads, writes):
        for b in reads:
            if b.r.get(sname, 0) < val:
                b.r[sname] = val
        for b in writes:
            if b.w.get(sname, 0) < val:
                b.w[sname] = val
            b.r = {}

    def fence(self, eng):
        covered = self.cnt[eng]
        inst = self.E[eng].drain()
        self.cnt[eng] += 1
        inst.then_inc(self.sem[eng], 1)
        if eng in self.drained:
            self.drained[eng] = self.cnt[eng]
            self.drain_marks[eng].append((covered, self.cnt[eng]))
        return (eng, self.cnt[eng])

    def op(self, eng, fn, reads=(), writes=()):
        self._deps(eng, reads, writes, skip_same=(eng == "pe"),
                   need_drain=(eng != "pe"))
        inst = fn(self.E[eng])
        self.cnt[eng] += 1
        inst.then_inc(self.sem[eng], 1)
        self._finish(eng, self.cnt[eng], reads, writes)
        return (eng, self.cnt[eng])

    def dma(self, q, out, in_, reads=(), writes=(), **kw):
        self._deps(q, reads, writes, skip_same=False, dma=True)
        inst = self.E[q].dma_start(out=out, in_=in_, **kw)
        db = writes[0] if writes else None
        if db is not None:
            if db.dsem is None:
                nm = self.name("dsem")
                self.sem[nm] = self.es.enter_context(self.nc.semaphore(nm))
                self.cnt[nm] = 0
                db.dsem = nm
            qs = db.dsem
        else:
            qs = "q" + q
        self.cnt[qs] += 16
        inst.then_inc(self.sem[qs], 16)
        self._finish(qs, self.cnt[qs], reads, writes)
        return (qs, self.cnt[qs])

    def setup_act_guard(self):
        tabs = list(get_activation_tables(self.nc.m.arch).keys())
        self.set_ids = {k: tabs.index(v) for k, v in ACT_SET_NAMES.items()}
        self.cur_set = None

    def ensure_set(self, s):
        if self.cur_set == s:
            return
        self.cur_set = s
        self.E["se"].add_instruction(mybir.InstLoadActFuncSet(
            name=self.nc.get_next_instruction_name(),
            act_func_set_id=self.set_ids[s], ins=[], outs=[]))

    def act(self, fn_func, out, in_, bias=0.0, scale=1.0, reads=(), writes=()):
        s = ACT_FUNC_SET.get(fn_func)
        if s is not None:
            self.ensure_set(s)
        return self.op("se", lambda e: e.activation(out, in_, fn_func, bias=bias, scale=scale),
                       reads=reads, writes=writes)

    def dump(self, name, ap, reads, shape=None, dt=None):
        t = self.nc.dram_tensor("dbg_" + name, shape or list(ap.shape), dt or ap.dtype, kind="ExternalOutput")
        self.dma("po", t.ap()[:, :], ap, reads=reads, writes=(Buf(),))

    def final_wait(self):
        snap = dict(self.cnt)
        for s, v in snap.items():
            if v > 0:
                self._wait("sy", s, v)


def _consts_np():
    i = np.arange(128)
    bf = ml_dtypes.bfloat16
    c = {}
    c["ident_f32"] = np.eye(128, dtype=np.float32)
    c["ident_bf"] = np.eye(128, dtype=np.float32).astype(bf)
    c["ones_col"] = np.ones((128, 1), dtype=np.float32)
    c["ones_row"] = np.ones((1, 128), dtype=np.float32)
    c["ones_row_bfs"] = np.ones((1, 128), dtype=np.float32).astype(bf)
    c["iotaE"] = (i[None, :] - i[:, None] - 1).astype(np.float32)
    c["trimask"] = (i[None, :] > i[:, None]).astype(np.float32)
    c["iotarow"] = i[None, :].astype(np.float32)
    c["revcol"] = (127 - i)[:, None].astype(np.float32)
    return c


CONST_SPECS = (("ident_f32", [128, 128], F32), ("ident_bf", [128, 128], BF16),
               ("ones_col", [128, 1], F32), ("ones_row", [1, 128], F32),
               ("ones_row_bfs", [1, 128], BF16), ("iotaE", [128, 128], F32),
               ("trimask", [128, 128], F32), ("iotarow", [1, 128], F32),
               ("revcol", [128, 1], F32))


DBG = False
STAGE = 99


def build_p1(nc, p, dram):

    def ext_in(nm, shape, dt=F32):
        return nc.dram_tensor(nm, shape, dt, kind="ExternalInput").ap(), Buf()

    xT, xT_b = ext_in("xT", [D, T])
    basis, basis_b = ext_in("L0_attn_basis", [D, NB2])
    coef_in = {}
    for nm in ("q", "k", "v", "o"):
        coef_in[nm] = ext_in(f"L0_attn_{nm}_coeffs", [128, NB2])
    decay, decay_b = ext_in("L0_attn_decay", [1, 1])
    oscale, oscale_b = ext_in("L0_attn_out_scale", [1, 1])
    Win, Win_b = ext_in("L0_ffn_Win", [D, MF])
    binv, binv_b = ext_in("L0_ffn_bin_", [1, MF])
    W1f, W1f_b = ext_in("L0_ffn_W1", [MF, INNER])
    b1f, b1f_b = ext_in("L0_ffn_b1", [MF, INNER])
    W2f, W2f_b = ext_in("L0_ffn_W2", [MF, INNER])
    b2f, b2f_b = ext_in("L0_ffn_b2", [1, MF])
    Wout, Wout_b = ext_in("L0_ffn_Wout", [MF, D])
    boutv, boutv_b = ext_in("L0_ffn_bout", [1, D])
    l1w = {}
    for nm in ("W_in", "W_gx", "W_ga", "W_out"):
        l1w[nm] = ext_in("L1_attn_" + nm, [D, D])
    l1b = {}
    for nm in ("b_in", "b_gx", "b_ga", "b_out", "a_param"):
        l1b[nm] = ext_in("L1_attn_" + nm, [1, D])
    Wg, Wg_b = ext_in("L1_ffn_Wg", [D, NEXP])
    bg, bg_b = ext_in("L1_ffn_bg", [1, NEXP])
    cst = {}
    for nm, shape, dt_ in CONST_SPECS:
        cst[nm] = ext_in(nm, shape, dt_)

    x1scr, x1scr_b = dram["x1scr"]
    s2scr, s2scr_b = dram["s2scr"]
    xn1scr, xn1scr_b = dram["xn1scr"]
    ohscr, ohscr_b = dram["ohscr"]
    x1ap, s2ap, xn1ap, ohap = x1scr.ap(), s2scr.ap(), xn1scr.ap(), ohscr.ap()

    l1wbf = {}
    l1wbf_bufs = {}
    for nm in ("W_in", "W_gx", "W_ga", "W_out"):
        t_, b_ = p.DR("l1wbfT_" + nm, [128, FC * D], BF16)
        l1wbf[nm] = t_
        l1wbf_bufs[nm] = b_
        sap, sbuf = l1w[nm]
        for ic in range(FC):
            for cc in range(2):
                p.cast_tile(sap[ic * 128:(ic + 1) * 128, cc * 512:(cc + 1) * 512],
                            t_.ap()[:, ic * D + cc * 512: ic * D + (cc + 1) * 512],
                            b_, src_buf=sbuf)
    p.cast_flush()

    PBt = [p.PS(f"pb{i}", [128, 512]) for i in range(6)]
    PTb, PTb_b = p.PS("ptb", [128, 1024], BF16)
    PSm, PSm_b = p.PS("psm", [128, 512])

    pre_es = ExitStack()

    def Sld(nm, shape, dt, src, src_b, q="sy", scope=None):
        t, b = p.S(nm, shape, dt, scope=scope)
        p.dma(q, t[:, :], src, reads=(src_b,), writes=(b,))
        return t, b

    idf, idf_b = Sld("idf", [128, 128], F32, cst["ident_f32"][0][:, :], cst["ident_f32"][1])
    idb, idb_b = Sld("idb", [128, 128], BF16, cst["ident_bf"][0][:, :], cst["ident_bf"][1], scope=pre_es)
    onc, onc_b = Sld("onc", [128, 1], F32, cst["ones_col"][0][:, :], cst["ones_col"][1])
    onr, onr_b = Sld("onr", [1, 128], F32, cst["ones_row"][0][:, :], cst["ones_row"][1])
    iotaE, iotaE_b = Sld("iotaE", [128, 128], F32, cst["iotaE"][0][:, :], cst["iotaE"][1], scope=pre_es)
    trim, trim_b = Sld("trim", [128, 128], F32, cst["trimask"][0][:, :], cst["trimask"][1], scope=pre_es)
    iotar, iotar_b = Sld("iotar", [1, 128], F32, cst["iotarow"][0][:, :], cst["iotarow"][1], scope=pre_es)
    revc, revc_b = Sld("revc", [128, 1], F32, cst["revcol"][0][:, :], cst["revcol"][1], scope=pre_es)

    bas, bas_b = p.S("bas", [128, FC * NB2], F32, scope=pre_es)
    for ic in range(FC):
        p.dma("sy", bas[:, ic * NB2:(ic + 1) * NB2], basis[ic * 128:(ic + 1) * 128, :],
              reads=(basis_b,), writes=(bas_b,))
    coef = {}
    for nm in ("q", "k", "v", "o"):
        src, srcb = coef_in[nm]
        coef[nm] = Sld("coef" + nm, [128, NB2], F32, src[:, :], srcb, scope=pre_es)
    dscal, dscal_b = Sld("dscal", [1, 1], F32, decay[:, :], decay_b, scope=pre_es)
    osc, osc_b = Sld("osc", [1, 1], F32, oscale[:, :], oscale_b, scope=pre_es)
    winf, winf_b = p.S("winf", [128, FC * MF], F32, scope=pre_es)
    for ic in range(FC):
        p.dma("sy", winf[:, ic * MF:(ic + 1) * MF], Win[ic * 128:(ic + 1) * 128, :],
              reads=(Win_b,), writes=(winf_b,))
    woutf, woutf_b = Sld("woutf", [MF, D], F32, Wout[:, :], Wout_b, scope=pre_es)
    w1c, w1c_b = Sld("w1c", [MF, INNER], F32, W1f[:, :], W1f_b)
    b1c, b1c_b = Sld("b1c", [MF, INNER], F32, b1f[:, :], b1f_b)
    w2c, w2c_b = Sld("w2c", [MF, INNER], F32, W2f[:, :], W2f_b)
    binr, binr_b = Sld("binr", [1, MF], F32, binv[:, :], binv_b, scope=pre_es)
    b2r, b2r_b = Sld("b2r", [1, MF], F32, b2f[:, :], b2f_b, scope=pre_es)
    boutr, boutr_b = Sld("boutr", [1, D], F32, boutv[:, :], boutv_b, scope=pre_es)
    l1br = {}
    for nm in ("b_in", "b_gx", "b_ga", "b_out", "a_param"):
        l1br[nm] = Sld("r" + nm, [1, D], F32, l1b[nm][0][:, :], l1b[nm][1], scope=pre_es)
    wgt, wgt_b = p.S("wgt", [128, FC * NEXP], F32)
    p.dma("sy", wgt[:, :].rearrange("p (ic e) -> p ic e", ic=FC),
          Wg.rearrange("(ic p) e -> p ic e", p=128), reads=(Wg_b,), writes=(wgt_b,))
    bgr, bgr_b = Sld("bgr", [1, NEXP], F32, bg[:, :], bg_b, scope=pre_es)

    def row_to_cols(row_t, row_b, ncols, name, scope=None):
        colt, colb = p.S(name, [128, max(1, ncols // 128)], F32, scope=scope)
        for j in range(max(1, ncols // 128)):
            w = min(128, ncols - j * 128)
            p.op("pe", lambda e, j=j, w=w: e.transpose(PBt[4][0][0:w, 0:1],
                                                       row_t[0:1, j * 128:j * 128 + w], idf[0:1, 0:1]),
                 reads=(row_b, idf_b), writes=(PBt[4][1],))
            p.op("ve", lambda e, j=j, w=w: e.tensor_copy(colt[0:w, j:j + 1], PBt[4][0][0:w, 0:1]),
                 reads=(PBt[4][1],), writes=(colb,))
        return colt, colb

    binc, binc_b = row_to_cols(binr, binr_b, MF, "binc")
    b2cc, b2cc_b = row_to_cols(b2r, b2r_b, MF, "b2cc")
    boutT, boutT_b = row_to_cols(boutr, boutr_b, D, "boutT")
    biasT = {}
    for nm in ("b_in", "b_gx", "b_ga", "b_out"):
        rt_, rb_ = l1br[nm]
        biasT[nm] = row_to_cols(rt_, rb_, D, "bT" + nm)
    apT, apT_b = row_to_cols(l1br["a_param"][0], l1br["a_param"][1], D, "apT", scope=pre_es)
    bgc, bgc_b = row_to_cols(bgr, bgr_b, NEXP, "bgc")

    sp8, sp8_b = p.S("sp8", [128, FC], F32)
    tpb, tpb_b = p.S("pre_t1", [128, FC], F32, scope=pre_es)
    tpc, tpc_b = p.S("pre_t2", [128, FC], F32, scope=pre_es)
    p.act(AF.Abs, PBt[4][0][0:128, 0:FC], apT[:, :], reads=(apT_b,), writes=(PBt[4][1],))
    p.act(AF.Exp, PBt[5][0][0:128, 0:FC], PBt[4][0][0:128, 0:FC], scale=-1.0,
          reads=(PBt[4][1],), writes=(PBt[5][1],))
    p.act(AF.Ln, PBt[4][0][0:128, 0:FC], PBt[5][0][0:128, 0:FC], bias=1.0,
          reads=(PBt[5][1],), writes=(PBt[4][1],))
    p.op("ve", lambda e: e.tensor_scalar_max(tpb[:, :], apT[:, :], 0.0), reads=(apT_b,), writes=(tpb_b,))
    p.fence("ve")
    p.op("ve", lambda e: e.tensor_tensor(tpc[:, :], tpb[:, :], PBt[4][0][0:128, 0:FC], OP.add),
         reads=(tpb_b, PBt[4][1]), writes=(tpc_b,))
    p.fence("ve")
    p.op("ve", lambda e: e.tensor_scalar_mul(sp8[:, :], tpc[:, :], -8.0), reads=(tpc_b,), writes=(sp8_b,))
    p.fence("ve")

    dsig, dsig_b = p.S("dsig", [1, 2], F32, scope=pre_es)
    p.act(AF.Exp, PBt[4][0][0:1, 0:1], dscal[:, :], scale=-1.0, reads=(dscal_b,), writes=(PBt[4][1],))
    p.op("ve", lambda e: e.tensor_scalar_add(dsig[:, 0:1], PBt[4][0][0:1, 0:1], 1.0),
         reads=(PBt[4][1],), writes=(dsig_b,))
    p.fence("ve")
    p.op("ve", lambda e: e.reciprocal(dsig[:, 1:2], dsig[:, 0:1]), reads=(dsig_b,), writes=(dsig_b,))
    p.fence("ve")
    lnd, lnd_b = p.S("lnd", [1, 1], F32, scope=pre_es)
    p.act(AF.Ln, PBt[4][0][0:1, 0:1], dsig[:, 1:2], reads=(dsig_b,), writes=(PBt[4][1],))
    p.op("ve", lambda e: e.tensor_scalar_max(lnd[:, :], PBt[4][0][0:1, 0:1], LND_CLAMP),
         reads=(PBt[4][1],), writes=(lnd_b,))
    p.fence("ve")
    lndc, lndc_b = p.S("lndc", [128, 1], F32, scope=pre_es)
    p.op("pe", lambda e: e.matmul(PBt[4][0][0:128, 0:1], onr[0:1, :], lnd[0:1, 0:1],
                                  start=True, stop=True), reads=(onr_b, lnd_b), writes=(PBt[4][1],))
    p.op("ve", lambda e: e.tensor_copy(lndc[:, 0:1], PBt[4][0][0:128, 0:1]),
         reads=(PBt[4][1],), writes=(lndc_b,))
    p.fence("ve")
    Dmask, Dmask_b = p.S("Dmask", [128, 128], F32)
    p.op("ve", lambda e: e.tensor_scalar_mul(Dmask[:, :], iotaE[:, :], lndc[:, 0:1]),
         reads=(iotaE_b, lndc_b), writes=(Dmask_b,))
    p.act(AF.Exp, PBt[4][0][:, 0:128], Dmask[:, :], reads=(Dmask_b,), writes=(PBt[4][1],))
    p.op("ve", lambda e: e.tensor_tensor(Dmask[:, :], PBt[4][0][:, 0:128], trim[:, :], OP.mult),
         reads=(PBt[4][1], trim_b), writes=(Dmask_b,))
    drow, drow_b = p.S("drow", [1, 128], F32, scope=pre_es)
    p.op("ve", lambda e: e.tensor_scalar_mul(drow[:, :], iotar[:, :], lnd[0:1, 0:1]),
         reads=(iotar_b, lnd_b), writes=(drow_b,))
    p.act(AF.Exp, PBt[5][0][0:1, 0:128], drow[:, :], reads=(drow_b,), writes=(PBt[5][1],))
    p.op("ve", lambda e: e.tensor_copy(drow[:, :], PBt[5][0][0:1, 0:128]),
         reads=(PBt[5][1],), writes=(drow_b,))
    dprow, dprow_b = p.S("dprow", [128, 128], F32)
    p.op("pe", lambda e: e.matmul(PBt[4][0][0:128, 0:128], onr[0:1, :], drow[0:1, :],
                                  start=True, stop=True), reads=(onr_b, drow_b), writes=(PBt[4][1],))
    p.op("ve", lambda e: e.tensor_copy(dprow[:, :], PBt[4][0][0:128, 0:128]),
         reads=(PBt[4][1],), writes=(dprow_b,))
    dpcol, dpcol_b = p.S("dpcol", [128, 1], F32)
    p.op("ve", lambda e: e.tensor_scalar_mul(dpcol[:, :], revc[:, :], lndc[:, 0:1]),
         reads=(revc_b, lndc_b), writes=(dpcol_b,))
    p.act(AF.Exp, PBt[5][0][0:128, 0:1], dpcol[:, :], reads=(dpcol_b,), writes=(PBt[5][1],))
    p.op("ve", lambda e: e.tensor_copy(dpcol[:, :], PBt[5][0][0:128, 0:1]),
         reads=(PBt[5][1],), writes=(dpcol_b,))
    dTc, dTc_b = p.S("dTc", [128, 1], F32)
    p.op("ve", lambda e: e.tensor_scalar_mul(dTc[:, :], lndc[:, 0:1], 128.0),
         reads=(lndc_b,), writes=(dTc_b,))
    p.act(AF.Exp, PBt[5][0][0:128, 0:1], dTc[:, :], reads=(dTc_b,), writes=(PBt[5][1],))
    p.op("ve", lambda e: e.tensor_copy(dTc[:, :], PBt[5][0][0:128, 0:1]),
         reads=(PBt[5][1],), writes=(dTc_b,))

    basT, basT_b = p.S("basT", [NB2, D], F32, scope=pre_es)
    for ic in range(FC):
        p.op("pe", lambda e, ic=ic: e.transpose(PBt[5][0][0:NB2, 0:128],
                                                bas[:, ic * NB2:(ic + 1) * NB2], idf[:, :]),
             reads=(bas_b, idf_b), writes=(PBt[5][1],))
        p.op("ve", lambda e, ic=ic: e.tensor_copy(basT[:, ic * 128:(ic + 1) * 128], PBt[5][0][0:NB2, 0:128]),
             reads=(PBt[5][1],), writes=(basT_b,))

    wnat = {}
    owT = owT_b = None
    for nm in ("q", "k", "v", "o"):
        ct, cb = coef[nm]
        cT, cTb = p.S("cT" + nm, [NB2, 128], F32, scope=pre_es)
        p.op("pe", lambda e, ct=ct: e.transpose(PBt[5][0][0:NB2, 0:128], ct[:, :], idf[:, :]),
             reads=(cb, idf_b), writes=(PBt[5][1],))
        p.op("ve", lambda e, cT=cT: e.tensor_copy(cT[:, :], PBt[5][0][0:NB2, 0:128]),
             reads=(PBt[5][1],), writes=(cTb,))
        wT, wTb = p.S("wT" + nm, [128, D], F32, scope=pre_es)
        for hh in range(2):
            p.op("pe", lambda e, hh=hh, cT=cT: e.matmul(PBt[hh][0][:, :], cT[:, :],
                                                        basT[0:NB2, hh * 512:(hh + 1) * 512],
                                                        start=True, stop=True),
                 reads=(cTb, basT_b), writes=(PBt[hh][1],))
            if nm == "o":
                p.op("ve", lambda e, hh=hh, wT=wT: e.tensor_copy(wT[:, hh * 512:(hh + 1) * 512],
                                                                 PBt[hh][0][:, :]),
                     reads=(PBt[hh][1],), writes=(wTb,))
        if nm == "o":
            oscb, oscb_b = p.S("oscb", [128, 1], F32, scope=pre_es)
            p.op("pe", lambda e: e.matmul(PBt[4][0][0:128, 0:1], onr[0:1, :], osc[0:1, 0:1],
                                          start=True, stop=True), reads=(onr_b, osc_b), writes=(PBt[4][1],))
            p.op("ve", lambda e: e.tensor_copy(oscb[:, 0:1], PBt[4][0][0:128, 0:1]),
                 reads=(PBt[4][1],), writes=(oscb_b,))
            owT, owT_b = p.S("owT", [128, D], BF16)
            p.op("se", lambda e: e.activation(owT[:, :], wT[:, :], AF.Identity, scale=oscb[:, 0:1]),
                 reads=(wTb, oscb_b), writes=(owT_b,))
        else:
            mx, mx_b = p.S("mxw", [128, 4], F32, scope=pre_es)
            p.op("ve", lambda e: e.reduce_max(mx[:, 0:1], PBt[0][0][:, :], axis=AX.X),
                 reads=(PBt[0][1],), writes=(mx_b,))
            p.op("ve", lambda e: e.reduce_max(mx[:, 1:2], PBt[1][0][:, :], axis=AX.X),
                 reads=(PBt[1][1],), writes=(mx_b,))
            p.fence("ve")
            p.op("ve", lambda e: e.tensor_max(mx[:, 2:3], mx[:, 0:1], mx[:, 1:2]), reads=(mx_b,), writes=(mx_b,))
            p.op("se", lambda e: e.mul(mx[:, 3:4], mx[:, 2:3], -1.0), reads=(mx_b,), writes=(mx_b,))
            if DBG and nm == "q":
                p.dump("nmx", mx[:, 3:4], (mx_b,))
            for hh in range(2):
                p.act(AF.Exp, PBt[2 + hh][0][:, :], PBt[hh][0][:, :], bias=mx[:, 3:4],
                      reads=(PBt[hh][1], mx_b), writes=(PBt[2 + hh][1],))
            sm, sm_b = p.S("smw", [128, 4], F32, scope=pre_es)
            p.op("ve", lambda e: e.reduce_sum(sm[:, 0:1], PBt[2][0][:, :], axis=AX.X),
                 reads=(PBt[2][1],), writes=(sm_b,))
            p.op("ve", lambda e: e.reduce_sum(sm[:, 1:2], PBt[3][0][:, :], axis=AX.X),
                 reads=(PBt[3][1],), writes=(sm_b,))
            p.fence("ve")
            p.op("ve", lambda e: e.tensor_add(sm[:, 2:3], sm[:, 0:1], sm[:, 1:2]), reads=(sm_b,), writes=(sm_b,))
            p.fence("ve")
            p.op("ve", lambda e: e.reciprocal(sm[:, 3:4], sm[:, 2:3]), reads=(sm_b,), writes=(sm_b,))
            p.fence("ve")
            if DBG and nm == "q":
                p.dump("smq", sm[:, :], (sm_b,))
                p.dump("mxq", mx[:, :], (mx_b,))
            wTb16, wTb16_b = p.S("wTb" + nm, [128, D], BF16, scope=pre_es)
            for hh in range(2):
                p.op("ve", lambda e, hh=hh: e.tensor_scalar_mul(wTb16[:, hh * 512:(hh + 1) * 512],
                                                                PBt[2 + hh][0][:, :], sm[:, 3:4]),
                     reads=(PBt[2 + hh][1], sm_b), writes=(wTb16_b,))
            nat, nat_b = p.S("wnat" + nm, [128, FC * 128], BF16)
            for ic in range(FC):
                p.op("pe", lambda e, ic=ic: e.transpose(PTb[0:128, 0:128],
                                                        wTb16[:, ic * 128:(ic + 1) * 128], idb[:, :]),
                     reads=(wTb16_b, idb_b), writes=(PTb_b,))
                p.op("ve", lambda e, ic=ic: e.tensor_copy(nat[:, ic * 128:(ic + 1) * 128], PTb[0:128, 0:128]),
                     reads=(PTb_b,), writes=(nat_b,))
            wnat[nm] = (nat, nat_b)
            if DBG and nm == "q":
                p.dump("basT", basT[:, :], (basT_b,))
                p.dump("cTq", cT[:, :], (cTb,))
                p.dump("wTq", wT[:, :], (wTb,))
                p.dump("wTb16q", wTb16[:, :], (wTb16_b,))
                p.dump("wnatq", nat[:, :], (nat_b,))

    winb, winb_b = p.S("winb", [128, FC * MF], BF16)
    p.op("ve", lambda e: e.tensor_copy(winb[:, :], winf[:, :]), reads=(winf_b,), writes=(winb_b,))
    woutb, woutb_b = p.S("woutb", [MF, D], BF16)
    p.op("ve", lambda e: e.tensor_copy(woutb[:, :], woutf[:, :]), reads=(woutf_b,), writes=(woutb_b,))

    pre_es.close()
    p.barrier()

    Mst, Mst_b = p.S("Mst", [128, 128], F32)
    Mbf, Mbf_b = p.S("Mbf", [128, 128], BF16)
    p.op("ve", lambda e: e.memset(Mst[:, :], 0.0), writes=(Mst_b,))
    p.op("ve", lambda e: e.memset(Mbf[:, :], 0.0), writes=(Mbf_b,))
    hcar, hcar_b = p.S("hcar", [128, FC], F32)
    p.op("ve", lambda e: e.memset(hcar[:, :], 0.0), writes=(hcar_b,))

    xab = [p.S("xa", [128, FC * TC], F32)]
    x1buf, x1buf_b = p.S("x1buf", [128, FC * TC], F32)
    xn, xn_b = p.S("xn", [128, FC * TC], BF16)
    s0, s0_b = p.S("s0", [128, FC * TC], F32)
    qT, qT_b = p.S("qT", [128, TC], BF16)
    kT, kT_b = p.S("kT", [128, TC], BF16)
    ktok, ktok_b = p.S("ktok", [128, TC], BF16)
    vtok, vtok_b = p.S("vtok", [128, TC], BF16)
    qtil, qtil_b = p.S("qtil", [128, 128], BF16)
    ktil, ktil_b = p.S("ktil", [128, 128], BF16)
    SpT, SpT_b = p.S("SpT", [128, 128], BF16)
    ret, ret_b = p.S("ret", [128, TC], BF16)
    Mtmp, Mtmp_b = p.S("Mtmp", [128, 128], F32)
    sF, sF_b = p.S("sF", [MF, TC], F32)
    gaccs = [p.S(f"gacc{i}", [MF, TC], F32) for i in range(4)]
    gbf, gbf_b = p.S("gbf", [MF, TC], BF16)
    xi_bf, xi_bf_b = p.S("xibf", [128, FC * TC], BF16)
    pool16a = p.S("p16a", [128, FC * TC], F32)
    pool16b = p.S("p16b", [128, FC * TC], F32)
    h_bf, h_bf_b = p.S("hbf", [128, FC * TC], BF16)
    tp = [p.S(f"tp{i}", [128, TC], F32) for i in range(6)]
    tpi = [0]

    def TP():
        t = tp[tpi[0] % len(tp)]
        tpi[0] += 1
        return t

    rows_stats = p.S("rstat", [1, 4 * TC], F32)
    lg, lg_b = p.S("lg", [NEXP, TC], F32)
    ohb, ohb_b = p.S("ohb", [NEXP, TC], BF16)
    ltok, ltok_b = p.S("ltok", [128, 4 * NEXP], F32)
    mxc, mxc_b = p.S("mxc", [128, 4], F32)
    mxr, mxr_b = p.S("mxr", [1, TC], F32)
    wstr = [p.S("wstrA", [128, FC * D], BF16), p.S("wstrB", [128, FC * D], BF16)]

    def ln_apply(xa, xa_b, out_t, out_b, also_f32_cb=None):
        rs, rs_b = rows_stats
        for ic in range(FC):
            sq = TP()
            p.op("se", lambda e, ic=ic, sq=sq: e.square(sq[0][:, :], xa[:, ic * TC:(ic + 1) * TC]),
                 reads=(xa_b,), writes=(sq[1],))
            p.op("pe", lambda e, ic=ic: e.matmul(PBt[2][0][0:1, :], onc[:, 0:1], xa[:, ic * TC:(ic + 1) * TC],
                                                 start=(ic == 0), stop=(ic == FC - 1)),
                 reads=(xa_b, onc_b), writes=(PBt[2][1],))
            p.op("pe", lambda e, ic=ic, sq=sq: e.matmul(PBt[3][0][0:1, :], onc[:, 0:1], sq[0][:, :],
                                                        start=(ic == 0), stop=(ic == FC - 1)),
                 reads=(sq[1], onc_b), writes=(PBt[3][1],))
        p.op("ve", lambda e: e.tensor_scalar_mul(rs[0:1, 0:TC], PBt[2][0][0:1, :], 1.0 / D),
             reads=(PBt[2][1],), writes=(rs_b,))
        p.op("ve", lambda e: e.tensor_scalar_mul(rs[0:1, TC:2 * TC], PBt[3][0][0:1, :], 1.0 / D),
             reads=(PBt[3][1],), writes=(rs_b,))
        p.op("ve", lambda e: e.tensor_tensor(rs[0:1, 2 * TC:3 * TC], rs[0:1, 0:TC], rs[0:1, 0:TC], OP.mult),
             reads=(rs_b,), writes=(rs_b,))
        p.op("ve", lambda e: e.tensor_sub(rs[0:1, 3 * TC:4 * TC], rs[0:1, TC:2 * TC], rs[0:1, 2 * TC:3 * TC]),
             reads=(rs_b,), writes=(rs_b,))
        p.op("ve", lambda e: e.tensor_scalar_add(rs[0:1, 2 * TC:3 * TC], rs[0:1, 3 * TC:4 * TC], 1e-5),
             reads=(rs_b,), writes=(rs_b,))
        p.act(AF.Ln, PBt[2][0][0:1, :], rs[0:1, 2 * TC:3 * TC], reads=(rs_b,), writes=(PBt[2][1],))
        p.act(AF.Exp, PBt[3][0][0:1, :], PBt[2][0][0:1, :], scale=-0.5,
              reads=(PBt[2][1],), writes=(PBt[3][1],))
        p.op("ve", lambda e: e.tensor_copy(rs[0:1, 3 * TC:4 * TC], PBt[3][0][0:1, :]),
             reads=(PBt[3][1],), writes=(rs_b,))
        p.op("pe", lambda e: e.matmul(PBt[2][0][:, :], onr[0:1, :], rs[0:1, 0:TC],
                                      start=True, stop=True), reads=(rs_b, onr_b), writes=(PBt[2][1],))
        p.op("pe", lambda e: e.matmul(PBt[3][0][:, :], onr[0:1, :], rs[0:1, 3 * TC:4 * TC],
                                      start=True, stop=True), reads=(rs_b, onr_b), writes=(PBt[3][1],))
        for ic in range(FC):
            t1 = TP()
            p.op("ve", lambda e, ic=ic, t1=t1: e.tensor_tensor(t1[0][:, :], xa[:, ic * TC:(ic + 1) * TC],
                                                               PBt[2][0][:, :], OP.subtract),
                 reads=(xa_b, PBt[2][1]), writes=(t1[1],))
            if also_f32_cb is None:
                p.op("ve", lambda e, ic=ic, t1=t1: e.tensor_tensor(out_t[:, ic * TC:(ic + 1) * TC],
                                                                   t1[0][:, :], PBt[3][0][:, :], OP.mult),
                     reads=(t1[1], PBt[3][1]), writes=(out_b,))
            else:
                tf = TP()
                p.op("ve", lambda e, t1=t1, tf=tf: e.tensor_tensor(tf[0][:, :], t1[0][:, :],
                                                                   PBt[3][0][:, :], OP.mult),
                     reads=(t1[1], PBt[3][1]), writes=(tf[1],))
                p.op("ve", lambda e, ic=ic, tf=tf: e.tensor_copy(out_t[:, ic * TC:(ic + 1) * TC], tf[0][:, :]),
                     reads=(tf[1],), writes=(out_b,))
                also_f32_cb(ic, tf)

    def load_x(c, slot):
        xt, xb = xab[slot]
        p.dma("sy", xt[:, :].rearrange("p (fc t) -> p fc t", fc=FC),
              xT.rearrange("(fc p) t -> p fc t", p=128)[:, :, c * TC:(c + 1) * TC],
              reads=(xT_b,), writes=(xb,))

    def load_w(nm, slot):
        wt, wb = wstr[slot]
        p.dma("sy", wt[:, :], l1wbf[nm].ap()[:, :],
              reads=(l1wbf_bufs[nm],), writes=(wb,))
        return wt, wb

    load_x(0, 0)
    for c in range(NCH):
        xa, xa_b = xab[0]

        ln_apply(xa, xa_b, xn, xn_b)
        if DBG and c == 0:
            p.dump("xa0", xa[:, :], (xa_b,))
            p.dump("xn0", xn[:, :], (xn_b,))

        if STAGE < 2:
            continue
        for dst, dstb, nm, ev in ((qT, qT_b, "q", "ve"), (kT, kT_b, "k", "se")):
            nat, nat_b = wnat[nm]
            for ic in range(FC):
                p.op("pe", lambda e, ic=ic, nat=nat: e.matmul(PBt[0][0][:, :], nat[:, ic * 128:(ic + 1) * 128],
                                                              xn[:, ic * TC:(ic + 1) * TC],
                                                              start=(ic == 0), stop=(ic == FC - 1)),
                     reads=(nat_b, xn_b), writes=(PBt[0][1],))
            if ev == "ve":
                p.op("ve", lambda e, dst=dst: e.tensor_copy(dst[:, :], PBt[0][0][:, :]),
                     reads=(PBt[0][1],), writes=(dstb,))
            else:
                p.op("se", lambda e, dst=dst: e.copy(dst[:, :], PBt[0][0][:, :]),
                     reads=(PBt[0][1],), writes=(dstb,))
        for dst, dstb, nm, ev in ((ktok, ktok_b, "k", "ve"), (vtok, vtok_b, "v", "se")):
            nat, nat_b = wnat[nm]
            for s in range(TC // SUB):
                pb = PBt[s % 2]
                for ic in range(FC):
                    p.op("pe", lambda e, s=s, ic=ic, nat=nat, pb=pb: e.matmul(
                        pb[0][:, 0:128], xn[:, ic * TC + s * SUB: ic * TC + (s + 1) * SUB],
                        nat[:, ic * 128:(ic + 1) * 128],
                        start=(ic == 0), stop=(ic == FC - 1)),
                        reads=(nat_b, xn_b), writes=(pb[1],))
                if ev == "ve":
                    p.op("ve", lambda e, s=s, dst=dst, pb=pb: e.tensor_copy(dst[:, s * SUB:(s + 1) * SUB],
                                                                            pb[0][:, 0:128]),
                         reads=(pb[1],), writes=(dstb,))
                else:
                    p.op("se", lambda e, s=s, dst=dst, pb=pb: e.copy(dst[:, s * SUB:(s + 1) * SUB],
                                                                     pb[0][:, 0:128]),
                         reads=(pb[1],), writes=(dstb,))
        if DBG and c == 0:
            p.dump("qT0", qT[:, :], (qT_b,))
            p.dump("kT0", kT[:, :], (kT_b,))
            p.dump("ktok0", ktok[:, :], (ktok_b,))
            p.dump("vtok0", vtok[:, :], (vtok_b,))
        if STAGE < 3:
            continue
        for s in range(TC // SUB):
            sl = slice(s * SUB, (s + 1) * SUB)
            p.op("pe", lambda e, sl=sl: e.matmul(PBt[4][0][:, 0:128], kT[:, sl], qT[:, sl],
                                                 start=True, stop=True),
                 reads=(kT_b, qT_b), writes=(PBt[4][1],))
            p.op("ve", lambda e: e.tensor_tensor(SpT[:, :], PBt[4][0][:, 0:128], Dmask[:, :], OP.mult),
                 reads=(PBt[4][1], Dmask_b), writes=(SpT_b,))
            p.op("ve", lambda e, sl=sl: e.tensor_tensor(qtil[:, :], qT[:, sl], dprow[:, :], OP.mult),
                 reads=(qT_b, dprow_b), writes=(qtil_b,))
            p.op("pe", lambda e, sl=sl: e.matmul(PBt[5][0][:, 0:128], vtok[:, sl], SpT[:, :],
                                                 start=True, stop=False),
                 reads=(vtok_b, SpT_b), writes=(PBt[5][1],))
            p.op("pe", lambda e: e.matmul(PBt[5][0][:, 0:128], Mbf[:, :], qtil[:, :],
                                          start=False, stop=True),
                 reads=(Mbf_b, qtil_b), writes=(PBt[5][1],))
            p.op("se", lambda e, sl=sl: e.copy(ret[:, sl], PBt[5][0][:, 0:128]),
                 reads=(PBt[5][1],), writes=(ret_b,))
            p.op("ve", lambda e, sl=sl: e.tensor_scalar_mul(ktil[:, :], ktok[:, sl], dpcol[:, 0:1]),
                 reads=(ktok_b, dpcol_b), writes=(ktil_b,))
            p.op("pe", lambda e, sl=sl: e.matmul(PBt[4][0][:, 0:128], ktil[:, :], vtok[:, sl],
                                                 start=True, stop=True),
                 reads=(ktil_b, vtok_b), writes=(PBt[4][1],))
            p.op("ve", lambda e: e.tensor_scalar_mul(Mtmp[:, :], Mst[:, :], dTc[:, 0:1]),
                 reads=(Mst_b, dTc_b), writes=(Mtmp_b,))
            p.op("ve", lambda e: e.tensor_tensor(Mst[:, :], Mtmp[:, :], PBt[4][0][:, 0:128], OP.add),
                 reads=(Mtmp_b, PBt[4][1]), writes=(Mst_b,))
            p.op("se", lambda e: e.copy(Mbf[:, :], Mst[:, :]), reads=(Mst_b,), writes=(Mbf_b,))
        if STAGE < 4:
            continue
        for vcB in range(FC):
            pb = PBt[vcB % 2]
            p.op("pe", lambda e, vcB=vcB, pb=pb: e.matmul(pb[0][:, :], owT[:, vcB * 128:(vcB + 1) * 128],
                                                          ret[:, :], start=True, stop=True),
                 reads=(owT_b, ret_b), writes=(pb[1],))
            p.op("ve", lambda e, vcB=vcB, pb=pb: e.tensor_copy(s0[:, vcB * TC:(vcB + 1) * TC], pb[0][:, :]),
                 reads=(pb[1],), writes=(s0_b,))

        if DBG and c == 0:
            p.dump("ret0", ret[:, :], (ret_b,))
            p.dump("s00", s0[:, :], (s0_b,))
            p.dump("Dmask", Dmask[:, :], (Dmask_b,))
            p.dump("dprow", dprow[:, :], (dprow_b,))
        if STAGE < 5:
            continue
        for ic in range(FC):
            p.op("pe", lambda e, ic=ic: e.matmul(PBt[0][0][0:MF, :], winb[:, ic * MF:(ic + 1) * MF],
                                                 xn[:, ic * TC:(ic + 1) * TC],
                                                 start=(ic == 0), stop=(ic == FC - 1)),
                 reads=(winb_b, xn_b), writes=(PBt[0][1],))
        p.op("se", lambda e: e.activation(sF[:, :], PBt[0][0][0:MF, :], AF.Identity, bias=binc[0:MF, 0:1]),
             reads=(PBt[0][1], binc_b), writes=(sF_b,))
        if STAGE < 4.2:
            continue
        for h in range(INNER):
            rp = PBt[2 + h % 2]
            par = h % 2
            k = h // 2
            ga_o, ga_ob = gaccs[(k % 2) * 2 + par]
            ga_i, ga_ib = gaccs[((k + 1) % 2) * 2 + par]
            p.act(AF.Relu, rp[0][0:MF, :], sF[:, :], bias=b1c[:, h:h + 1], scale=w1c[:, h:h + 1],
                  reads=(sF_b, b1c_b, w1c_b), writes=(rp[1],))
            if h < 2:
                p.op("ve", lambda e, h=h, rp=rp, ga_o=ga_o: e.tensor_scalar_mul(ga_o[:, :], rp[0][0:MF, :],
                                                                                w2c[:, h:h + 1]),
                     reads=(rp[1], w2c_b), writes=(ga_ob,))
            else:
                p.op("ve", lambda e, h=h, rp=rp, ga_o=ga_o, ga_i=ga_i: e.scalar_tensor_tensor(
                    ga_o[:, :], rp[0][0:MF, :], w2c[:, h:h + 1], ga_i[:, :], OP.mult, OP.add),
                     reads=(rp[1], w2c_b, ga_ib), writes=(ga_ob,))
        if STAGE < 4.3:
            continue
        ke = (INNER - 2) // 2
        ko = (INNER - 1) // 2
        geven, geven_b = gaccs[(ke % 2) * 2]
        godd, godd_b = gaccs[(ko % 2) * 2 + 1]
        gfin, gfin_b = gaccs[((ke + 1) % 2) * 2]
        p.op("ve", lambda e: e.tensor_tensor(gfin[:, :], geven[:, :], godd[:, :], OP.add),
             reads=(geven_b, godd_b), writes=(gfin_b,))
        p.act(AF.Identity, gbf[:, :], gfin[:, :], bias=b2cc[0:MF, 0:1],
              reads=(gfin_b, b2cc_b), writes=(gbf_b,))
        if STAGE < 4.4:
            continue
        for dc in range(FC):
            pb = PBt[dc % 2]
            p.op("pe", lambda e, dc=dc, pb=pb: e.matmul(pb[0][:, :], woutb[:, dc * 128:(dc + 1) * 128],
                                                        gbf[:, :], start=True, stop=True),
                 reads=(woutb_b, gbf_b), writes=(pb[1],))
            sg = PBt[2 + dc % 2]
            p.act(AF.Silu, sg[0][:, :], pb[0][:, :], bias=boutT[:, dc:dc + 1],
                  reads=(pb[1], boutT_b), writes=(sg[1],))
            tm = TP()
            p.op("ve", lambda e, dc=dc, sg=sg, tm=tm: e.tensor_tensor(tm[0][:, :], s0[:, dc * TC:(dc + 1) * TC],
                                                                      sg[0][:, :], OP.mult),
                 reads=(s0_b, sg[1]), writes=(tm[1],))
            p.op("ve", lambda e, dc=dc, tm=tm: e.tensor_tensor(x1buf[:, dc * TC:(dc + 1) * TC],
                                                               xa[:, dc * TC:(dc + 1) * TC], tm[0][:, :], OP.add),
                 reads=(tm[1], xa_b), writes=(x1buf_b,))
        if STAGE < 4.5:
            continue
        p.dma("se", x1ap.rearrange("(fc p) t -> p fc t", p=128)[:, :, c * TC:(c + 1) * TC],
              x1buf[:, :].rearrange("p (fc t) -> p fc t", fc=FC), reads=(x1buf_b,), writes=(x1scr_b,))
        if c + 1 < NCH:
            load_x(c + 1, 0)
        if DBG and c == 0:
            p.dump("sF0", sF[:, :], (sF_b,))
            p.dump("gacc0", gfin[:, :], (gfin_b,))
            p.dump("x1c0", x1buf[:, :], (x1buf_b,))

        if STAGE < 6:
            continue
        def logits_cb(ic, tf):
            p.op("pe", lambda e, ic=ic, tf=tf: e.matmul(PSm[0:NEXP, :], wgt[:, ic * NEXP:(ic + 1) * NEXP],
                                                        tf[0][:, :], start=(ic == 0), stop=(ic == FC - 1)),
                 reads=(wgt_b, tf[1]), writes=(PSm_b,))

        ln_apply(x1buf, x1buf_b, xn, xn_b, also_f32_cb=logits_cb)
        p.op("ve", lambda e: e.tensor_scalar(lg[:, :], PSm[0:NEXP, :], bgc[0:NEXP, 0:1], None, OP.add),
             reads=(PSm_b, bgc_b), writes=(lg_b,))
        p.dma("se", xn1ap.rearrange("(fc p) t -> p fc t", p=128)[:, :, c * TC:(c + 1) * TC],
              xn[:, :].rearrange("p (fc t) -> p fc t", fc=FC), reads=(xn_b,), writes=(xn1scr_b,))

        if STAGE < 7:
            continue
        for s in range(TC // SUB):
            p.op("pe", lambda e, s=s: e.transpose(PBt[4][0][0:128, 0:NEXP],
                                                  lg[:, s * SUB:(s + 1) * SUB], idf[0:NEXP, 0:NEXP]),
                 reads=(lg_b, idf_b), writes=(PBt[4][1],))
            p.op("ve", lambda e, s=s: e.tensor_copy(ltok[:, s * NEXP:(s + 1) * NEXP], PBt[4][0][0:128, 0:NEXP]),
                 reads=(PBt[4][1],), writes=(ltok_b,))
            p.op("ve", lambda e, s=s: e.reduce_max(mxc[:, s:s + 1], ltok[:, s * NEXP:(s + 1) * NEXP], axis=AX.X),
                 reads=(ltok_b,), writes=(mxc_b,))
            p.op("pe", lambda e, s=s: e.transpose(PBt[4][0][0:1, 0:128], mxc[:, s:s + 1], idf[:, :]),
                 reads=(mxc_b, idf_b), writes=(PBt[4][1],))
            p.op("ve", lambda e, s=s: e.tensor_copy(mxr[0:1, s * SUB:(s + 1) * SUB], PBt[4][0][0:1, 0:128]),
                 reads=(PBt[4][1],), writes=(mxr_b,))
        p.op("pe", lambda e: e.matmul(PBt[4][0][0:NEXP, :], onr[0:1, 0:NEXP], mxr[0:1, :],
                                      start=True, stop=True), reads=(onr_b, mxr_b), writes=(PBt[4][1],))
        p.op("ve", lambda e: e.tensor_tensor(ohb[:, :], lg[:, :], PBt[4][0][0:NEXP, :], OP.is_ge),
             reads=(lg_b, PBt[4][1]), writes=(ohb_b,))
        p.dma("se", ohap[:, c * TC:(c + 1) * TC], ohb[:, :], reads=(ohb_b,), writes=(ohscr_b,))

        if STAGE < 8:
            continue
        xi_f32, xi_f32_b = pool16a
        xg_all, xg_all_b = pool16b
        wt, wb = load_w("W_in", 0)
        for oc2 in range(FC):
            pb = PBt[oc2 % 2]
            for ic in range(FC):
                p.op("pe", lambda e, oc2=oc2, ic=ic, pb=pb, wt=wt: e.matmul(
                    pb[0][:, :], wt[:, ic * D + oc2 * 128: ic * D + (oc2 + 1) * 128],
                    xn[:, ic * TC:(ic + 1) * TC], start=(ic == 0), stop=(ic == FC - 1)),
                    reads=(wb, xn_b), writes=(pb[1],))
            p.op("ve", lambda e, oc2=oc2, pb=pb: e.tensor_scalar(xi_f32[:, oc2 * TC:(oc2 + 1) * TC],
                                                                 pb[0][:, :], biasT["b_in"][0][:, oc2:oc2 + 1],
                                                                 None, OP.add),
                 reads=(pb[1], biasT["b_in"][1]), writes=(xi_f32_b,))
            p.op("ve", lambda e, oc2=oc2: e.tensor_copy(xi_bf[:, oc2 * TC:(oc2 + 1) * TC],
                                                        xi_f32[:, oc2 * TC:(oc2 + 1) * TC]),
                 reads=(xi_f32_b,), writes=(xi_bf_b,))
        if STAGE < 9:
            continue
        wt2, wb2 = load_w("W_gx", 1)
        for oc2 in range(FC):
            pb = PBt[oc2 % 2]
            for ic in range(FC):
                p.op("pe", lambda e, oc2=oc2, ic=ic, pb=pb, wt2=wt2: e.matmul(
                    pb[0][:, :], wt2[:, ic * D + oc2 * 128: ic * D + (oc2 + 1) * 128],
                    xi_bf[:, ic * TC:(ic + 1) * TC], start=(ic == 0), stop=(ic == FC - 1)),
                    reads=(wb2, xi_bf_b), writes=(pb[1],))
            sg = PBt[2 + oc2 % 2]
            p.act(AF.Sigmoid, sg[0][:, :], pb[0][:, :], bias=biasT["b_gx"][0][:, oc2:oc2 + 1],
                  reads=(pb[1], biasT["b_gx"][1]), writes=(sg[1],))
            p.op("ve", lambda e, oc2=oc2, sg=sg: e.tensor_tensor(xg_all[:, oc2 * TC:(oc2 + 1) * TC],
                                                                 sg[0][:, :],
                                                                 xi_f32[:, oc2 * TC:(oc2 + 1) * TC], OP.mult),
                 reads=(sg[1], xi_f32_b), writes=(xg_all_b,))
        if STAGE < 10:
            continue
        a_all, a_all_b = pool16a
        ga_all, ga_all_b = a_all, a_all_b
        wt3, wb3 = load_w("W_ga", 0)
        for oc2 in range(FC):
            pb = PBt[oc2 % 2]
            for ic in range(FC):
                p.op("pe", lambda e, oc2=oc2, ic=ic, pb=pb, wt3=wt3: e.matmul(
                    pb[0][:, :], wt3[:, ic * D + oc2 * 128: ic * D + (oc2 + 1) * 128],
                    xi_bf[:, ic * TC:(ic + 1) * TC], start=(ic == 0), stop=(ic == FC - 1)),
                    reads=(wb3, xi_bf_b), writes=(pb[1],))
            sg = PBt[2 + oc2 % 2]
            p.act(AF.Sigmoid, sg[0][:, :], pb[0][:, :], bias=biasT["b_ga"][0][:, oc2:oc2 + 1],
                  reads=(pb[1], biasT["b_ga"][1]), writes=(sg[1],))
            p.op("ve", lambda e, oc2=oc2, sg=sg: e.tensor_scalar(ga_all[:, oc2 * TC:(oc2 + 1) * TC],
                                                                 sg[0][:, :], sp8[:, oc2:oc2 + 1],
                                                                 None, OP.mult),
                 reads=(sg[1], sp8_b), writes=(ga_all_b,))
        if STAGE < 11:
            continue
        for oc2 in range(FC):
            osl = slice(oc2 * TC, (oc2 + 1) * TC)
            ap_ = PBt[2 + oc2 % 2]
            p.act(AF.Exp, ap_[0][:, :], ga_all[:, osl], reads=(ga_all_b,), writes=(ap_[1],))
            e2a = TP()
            e2b = TP()
            p.act(AF.Exp, PSm[:, :], ga_all[:, osl], scale=2.0, reads=(ga_all_b,), writes=(PSm_b,))
            p.op("ve", lambda e, e2a=e2a: e.tensor_scalar(e2a[0][:, :], PSm[:, :], -1.0, 1.0, OP.mult, OP.add),
                 reads=(PSm_b,), writes=(e2a[1],))
            p.op("ve", lambda e, e2a=e2a, e2b=e2b: e.tensor_scalar_max(e2b[0][:, :], e2a[0][:, :], 1e-38),
                 reads=(e2a[1],), writes=(e2b[1],))
            lnt = TP()
            p.act(AF.Ln, lnt[0][:, :], e2b[0][:, :], reads=(e2b[1],), writes=(lnt[1],))
            p.act(AF.Exp, PSm[:, :], lnt[0][:, :], scale=0.5, reads=(lnt[1],), writes=(PSm_b,))
            p.op("ve", lambda e, osl=osl: e.tensor_tensor(ga_all[:, osl], PSm[:, :],
                                                          xg_all[:, osl], OP.mult),
                 reads=(PSm_b, xg_all_b), writes=(ga_all_b,))
            hf = TP()
            p.op("ve", lambda e, osl=osl, hf=hf, ap_=ap_, oc2=oc2: e.tensor_tensor_scan(
                hf[0][:, :], ap_[0][:, :], ga_all[:, osl],
                hcar[:, oc2:oc2 + 1], OP.mult, OP.add),
                reads=(ap_[1], ga_all_b, hcar_b), writes=(hf[1],))
            p.op("ve", lambda e, oc2=oc2, hf=hf: e.tensor_copy(hcar[:, oc2:oc2 + 1], hf[0][:, TC - 1:TC]),
                 reads=(hf[1],), writes=(hcar_b,))
            p.op("se", lambda e, oc2=oc2, hf=hf: e.copy(h_bf[:, oc2 * TC:(oc2 + 1) * TC], hf[0][:, :]),
                 reads=(hf[1],), writes=(h_bf_b,))
        if STAGE < 12:
            continue
        wt4, wb4 = load_w("W_out", 1)
        for oc2 in range(FC):
            pb = PBt[oc2 % 2]
            for ic in range(FC):
                p.op("pe", lambda e, oc2=oc2, ic=ic, pb=pb, wt4=wt4: e.matmul(
                    pb[0][:, :], wt4[:, ic * D + oc2 * 128: ic * D + (oc2 + 1) * 128],
                    h_bf[:, ic * TC:(ic + 1) * TC], start=(ic == 0), stop=(ic == FC - 1)),
                    reads=(wb4, h_bf_b), writes=(pb[1],))
            s2t = TP()
            p.op("ve", lambda e, oc2=oc2, pb=pb, s2t=s2t: e.tensor_scalar(s2t[0][:, :], pb[0][:, :],
                                                                          biasT["b_out"][0][:, oc2:oc2 + 1],
                                                                          None, OP.add),
                 reads=(pb[1], biasT["b_out"][1]), writes=(s2t[1],))
            p.dma("se", s2ap[oc2 * 128:(oc2 + 1) * 128, c * TC:(c + 1) * TC], s2t[0][:, :],
                  reads=(s2t[1],), writes=(s2scr_b,))


def build_p2_cast(nc, p, w1, w2):
    w1bf_t, _ = p.DR("w1bfT", [NEXP, 128, (HDIM // 512) * FC * 512], BF16)
    w2bf_t, _ = p.DR("w2bfT", [NEXP, 128, 32 * D], BF16)
    w1bf, w2bf = w1bf_t.ap(), w2bf_t.ap()
    w1bufs = [Buf() for _ in range(NEXP)]
    w2bufs = [Buf() for _ in range(NEXP)]
    cast_tile = p.cast_tile

    for e_ in range(NEXP):
        for r in range(D // 128):
            for cc in range(HDIM // 512):
                dst = w1bf[e_, :, (cc * FC + r) * 512:(cc * FC + r + 1) * 512]
                cast_tile(w1[e_, r * 128:(r + 1) * 128, cc * 512:(cc + 1) * 512],
                          dst, w1bufs[e_])
        for r in range(HDIM // 128):
            for cc in range(D // 512):
                dst = w2bf[e_, :, r * D + cc * 512:r * D + (cc + 1) * 512]
                cast_tile(w2[e_, r * 128:(r + 1) * 128, cc * 512:(cc + 1) * 512],
                          dst, w2bufs[e_])
    p.cast_flush()
    return w1bf, w1bufs, w2bf, w2bufs


def build_p2(nc, p, dram, moebf=None, ext_w=None):

    def ext_in(nm, shape, dt=F32):
        return nc.dram_tensor(nm, shape, dt, kind="ExternalInput").ap(), Buf()

    x1ap, x1b = dram["x1scr"][0].ap(), dram["x1scr"][1]
    s2ap, s2b = dram["s2scr"][0].ap(), dram["s2scr"][1]
    xn1ap, xn1b = dram["xn1scr"][0].ap(), dram["xn1scr"][1]
    ohap, oh_b = dram["ohscr"][0].ap(), dram["ohscr"][1]

    if ext_w is None:
        w1, w1_b = ext_in("L1_ffn_W1", [NEXP, D, HDIM])
        w2, w2_b = ext_in("L1_ffn_W2", [NEXP, HDIM, D])
    else:
        w1, w2 = ext_w
        w1_b, w2_b = Buf(), Buf()
    out, out_b = nc.dram_tensor("xT2", [D, T], F32, kind="ExternalOutput").ap(), Buf()

    if moebf is None:
        w1bf, w1bufs, w2bf, w2bufs = build_p2_cast(nc, p, w1, w2)
    else:
        w1bf, w1bufs, w2bf, w2bufs = moebf

    onrb2, onrb2_b = p.S("onrb2", [128, 128], BF16)
    orow = nc.dram_tensor("ones_row_bf2", [1, 128], BF16, kind="ExternalInput").ap()
    p.dma("sy", onrb2[0:1, :], orow[:, :], writes=(onrb2_b,))
    p.dma("sy", onrb2[64:65, :], orow[:, :], writes=(onrb2_b,))

    PB = [p.PS(f"p2pb{i}", [128, 512]) for i in range(6)]

    w2sb, w2sb_b = p.S("w2sb", [128, 32 * D], BF16)
    ybufs = [p.S("ybufA", [128, FC * (T // 2)], BF16), p.S("ybufB", [128, FC * (T // 2)], BF16)]
    h_fm, h_fm_b = p.S("hfm", [128, 32 * 512], BF16)
    w1t = [p.S("w1tA", [128, FC * HC], BF16), p.S("w1tB", [128, FC * HC], BF16)]
    xn1s, xn1s_b = p.S("xn1s", [128, FC * 512], BF16)
    xnm, xnm_b = p.S("xnm", [128, FC * 512], BF16)
    mrowt, _mb = p.S("mrow", [128, 512], BF16)
    mrow = [(mrowt[0:1, :], Buf()), (mrowt[64:65, :], Buf())]
    x1t = [p.S("x1tA", [128, 512], F32)]
    s2t = [p.S("s2tA", [128, 512], F32)]
    ctp = [p.S(f"ctp{i}", [128, 512], F32) for i in range(2)]

    NTS = T // 512
    units = [(th, e_, ts) for th in range(2) for e_ in range(NEXP)
             for ts in range(th * (NTS // 2), (th + 1) * (NTS // 2))]

    def emit_w2load(e_):
        for qq in range(4):
            p.dma("se", w2sb[:, qq * 8 * D:(qq + 1) * 8 * D],
                  w2bf[e_, :, qq * 8 * D:(qq + 1) * 8 * D],
                  reads=(w2bufs[e_],), writes=(w2sb_b,))

    def emit_front(u):
        th, e_, ts = units[u]
        tsl = slice(ts * 512, (ts + 1) * 512)
        p.dma("sy", xn1s[:, :].rearrange("p (fc t) -> p fc t", fc=FC),
              xn1ap.rearrange("(fc p) t -> p fc t", p=128)[:, :, tsl],
              reads=(xn1b,), writes=(xn1s_b,))
        mr, mr_b = mrow[u % 2]
        p.dma("sy", mr[:, :], ohap[e_:e_ + 1, tsl], reads=(oh_b,), writes=(mr_b,))
        obase = 64 * (u % 2)
        p.op("pe", lambda e, mr=mr, obase=obase: e.matmul(
            PB[4][0][:, :], onrb2[obase:obase + 1, :], mr[0:1, :],
            start=True, stop=True),
             reads=(onrb2_b, mr_b), writes=(PB[4][1],))
        for ic in range(FC):
            p.op("ve", lambda e, ic=ic: e.tensor_tensor(xnm[:, ic * 512:(ic + 1) * 512],
                                                        xn1s[:, ic * 512:(ic + 1) * 512],
                                                        PB[4][0][:, :], OP.mult),
                 reads=(xn1s_b, PB[4][1]), writes=(xnm_b,))
        wt, wb = w1t[0]
        p.dma("se", wt[:, :], w1bf[e_, :, 0:FC * 512], reads=(w1bufs[e_],), writes=(wb,))

    for u, (th, e_, ts) in enumerate(units):
        tsl = slice(ts * 512, (ts + 1) * 512)
        if u == 0:
            emit_front(0)
        if ts == th * (NTS // 2):
            emit_w2load(e_)
        for hc in range(HDIM // HC):
            wt, wb = w1t[hc % 2]
            if hc + 1 < HDIM // HC:
                nwt, nwb = w1t[(hc + 1) % 2]
                p.dma("se", nwt[:, :], w1bf[e_, :, (hc + 1) * FC * 512:(hc + 2) * FC * 512],
                      reads=(w1bufs[e_],), writes=(nwb,))
            for hs in range(HC // 128):
                pb = PB[(hc * 4 + hs) % 2]
                for ic in range(FC):
                    p.op("pe", lambda e, hc=hc, hs=hs, ic=ic, pb=pb, wt=wt: e.matmul(
                        pb[0][:, :], wt[:, ic * HC + hs * 128: ic * HC + (hs + 1) * 128],
                        xnm[:, ic * 512:(ic + 1) * 512],
                        start=(ic == 0), stop=(ic == FC - 1)),
                        reads=(wb, xnm_b), writes=(pb[1],))
                p.act(AF.Gelu, h_fm[:, (hc * 4 + hs) * 512:(hc * 4 + hs + 1) * 512], pb[0][:, :],
                      reads=(pb[1],), writes=(h_fm_b,))
        if u + 1 < len(units):
            emit_front(u + 1)
        for dc in range(FC):
            pb = PB[2 + dc % 2]
            for ht in range(32):
                p.op("pe", lambda e, dc=dc, ht=ht, pb=pb: e.matmul(
                    pb[0][:, :], w2sb[:, ht * D + dc * 128: ht * D + (dc + 1) * 128],
                    h_fm[:, ht * 512:(ht + 1) * 512],
                    start=(ht == 0), stop=(ht == 31)),
                    reads=(w2sb_b, h_fm_b), writes=(pb[1],))
            yofs = dc * (T // 2) + (ts % (NTS // 2)) * 512
            ydst, ydst_b = ybufs[e_ % 2]
            ysrc, ysrc_b = ybufs[(e_ + 1) % 2]
            ysl = ydst[:, yofs: yofs + 512]
            if e_ == 0:
                p.op("ve", lambda e, ysl=ysl, pb=pb: e.tensor_copy(ysl, pb[0][:, :]),
                     reads=(pb[1],), writes=(ydst_b,))
            else:
                p.op("ve", lambda e, ysl=ysl, pb=pb, ysrc=ysrc, yofs=yofs: e.tensor_tensor(
                    ysl, pb[0][:, :], ysrc[:, yofs: yofs + 512], OP.add),
                     reads=(pb[1], ysrc_b), writes=(ydst_b,))
            if e_ == NEXP - 1:
                xt, xtb = x1t[0]
                st, stb = s2t[0]
                p.dma("sy", xt[:, :], x1ap[dc * 128:(dc + 1) * 128, tsl], reads=(x1b,), writes=(xtb,))
                p.dma("sy", st[:, :], s2ap[dc * 128:(dc + 1) * 128, tsl], reads=(s2b,), writes=(stb,))
                c0 = ctp[0]
                c1 = ctp[1]
                p.act(AF.Tanh, PB[5][0][:, :], ysl, scale=0.5, reads=(ydst_b,), writes=(PB[5][1],))
                p.op("ve", lambda e, ysl=ysl, c0=c0, st=st: e.tensor_tensor(c0[0][:, :], st[:, :],
                                                                            ysl, OP.mult),
                     reads=(ydst_b, stb), writes=(c0[1],))
                p.op("ve", lambda e, c0=c0, c1=c1: e.tensor_tensor(c1[0][:, :], c0[0][:, :],
                                                                   PB[5][0][:, :], OP.mult),
                     reads=(c0[1], PB[5][1]), writes=(c1[1],))
                p.op("ve", lambda e, c0=c0, st=st, xt=xt: e.scalar_tensor_tensor(st[:, :], c0[0][:, :],
                                                                                 0.5, xt[:, :],
                                                                                 OP.mult, OP.add),
                     reads=(c0[1], xtb), writes=(stb,))
                p.op("ve", lambda e, c1=c1, st=st, c0=c0: e.scalar_tensor_tensor(c0[0][:, :], c1[0][:, :],
                                                                                 0.5, st[:, :],
                                                                                 OP.mult, OP.add),
                     reads=(c1[1], stb), writes=(c0[1],))
                p.dma("se", out[dc * 128:(dc + 1) * 128, tsl], c0[0][:, :],
                      reads=(c0[1],), writes=(out_b,))


def _mk_dram(p, external):
    kind = "ExternalOutput" if external == "out" else ("ExternalInput" if external == "in" else "Internal")
    dram = {}
    specs = (("x1scr", [D, T], F32), ("s2scr", [D, T], F32),
             ("xn1scr", [D, T], BF16), ("ohscr", [NEXP, T], BF16))
    for nm, shape, dt_ in specs:
        if external == "internal":
            t, b = p.DR(nm, shape, dt_, kind="Internal")
        else:
            t = p.nc.dram_tensor(nm, shape, dt_, kind=kind)
            b = Buf()
        dram[nm] = (t, b)
    return dram


def _build(mode):
    nc = bass.Bass("TRN2", target_bir_lowering=False)
    p = Prog(nc)
    p.setup_act_guard()
    if mode == "p1":
        dram = _mk_dram(p, "out")
        build_p1(nc, p, dram)
    elif mode == "p2":
        dram = _mk_dram(p, "in")
        build_p2(nc, p, dram)
    else:
        dram = _mk_dram(p, "internal")
        scope1 = ExitStack()
        p.default_scope = scope1
        build_p1(nc, p, dram)
        p.default_scope = None
        p.fence("po")
        snap = {s: v for s, v in p.cnt.items() if v > 0}
        for e in Prog.ENGS:
            if e == "po":
                continue
            for s, v in snap.items():
                p._wait(e, s, v)
        w1e = nc.dram_tensor("L1_ffn_W1", [NEXP, D, HDIM], F32, kind="ExternalInput").ap()
        w2e = nc.dram_tensor("L1_ffn_W2", [NEXP, HDIM, D], F32, kind="ExternalInput").ap()
        moebf = build_p2_cast(nc, p, w1e, w2e)
        scope1.close()
        p.extra_dep = dict(snap)
        build_p2(nc, p, dram, moebf=moebf, ext_w=(w1e, w2e))
    p.final_wait()
    p.es.close()
    return nc


P1_KEYS = ("xT", "L0_attn_basis", "L0_attn_q_coeffs", "L0_attn_k_coeffs",
           "L0_attn_v_coeffs", "L0_attn_o_coeffs", "L0_attn_decay",
           "L0_attn_out_scale", "L0_ffn_Win", "L0_ffn_bin_", "L0_ffn_W1",
           "L0_ffn_b1", "L0_ffn_W2", "L0_ffn_b2", "L0_ffn_Wout", "L0_ffn_bout",
           "L1_attn_W_in", "L1_attn_b_in", "L1_attn_W_gx", "L1_attn_b_gx",
           "L1_attn_W_ga", "L1_attn_b_ga", "L1_attn_a_param", "L1_attn_W_out",
           "L1_attn_b_out", "L1_ffn_Wg", "L1_ffn_bg",
           "ident_f32", "ident_bf", "ones_col", "ones_row", "ones_row_bfs",
           "iotaE", "trimask", "iotarow", "revcol")
P2_KEYS = ("L1_ffn_W1", "L1_ffn_W2", "ones_row_bf2")


def _input_maps(inputs):
    consts = _consts_np()
    x = np.asarray(inputs["L0_attn_x"])
    base = {}
    for k, v in inputs.items():
        if k == "L0_attn_x":
            continue
        a = np.asarray(v, dtype=np.float32)
        if k in ("L0_attn_decay", "L0_attn_out_scale"):
            a = a.reshape(1, 1)
        if a.ndim == 1:
            a = a.reshape(1, -1)
        base[k] = np.ascontiguousarray(a)
    for k, v in consts.items():
        base[k] = v
    base["ones_row_bf2"] = consts["ones_row_bfs"]
    maps = []
    for b in range(B):
        m = dict(base)
        m["xT"] = np.ascontiguousarray(x[b].T.astype(np.float32))
        maps.append(m)
    return maps


FUSED = True


def kernel(**inputs):
    maps = _input_maps(inputs)
    if FUSED:
        nc = _build("full")
        keys = tuple(set(P1_KEYS) | set(P2_KEYS))
        inm = [{k: m[k] for k in keys} for m in maps]
        rr = run_bass_kernel_spmd(nc, inm, core_ids=list(range(B)))
        outs = []
        for b in range(B):
            xt2 = np.asarray(rr.results[b]["xT2"], dtype=np.float32)
            outs.append(np.ascontiguousarray(xt2.T))
        return np.stack(outs, axis=0)
    nc1 = _build("p1")
    in1 = [{k: m[k] for k in P1_KEYS} for m in maps]
    r1 = run_bass_kernel_spmd(nc1, in1, core_ids=list(range(B)))
    r1m = [{k: np.array(v) for k, v in r.items()} for r in r1.results]
    nc2 = _build("p2")
    in2 = []
    for b in range(B):
        m = {k: maps[b][k] for k in P2_KEYS}
        for nm in ("x1scr", "s2scr", "xn1scr", "ohscr"):
            m[nm] = r1m[b][nm]
        in2.append(m)
    r2 = run_bass_kernel_spmd(nc2, in2, core_ids=list(range(B)))
    outs = []
    for b in range(B):
        xt2 = np.asarray(r2.results[b]["xT2"], dtype=np.float32)
        outs.append(np.ascontiguousarray(xt2.T))
    return np.stack(outs, axis=0)


# version 14
# speedup vs baseline: 1.0121x; 1.0121x over previous
import numpy as np
import ml_dtypes
from contextlib import ExitStack

import concourse.bass as bass
import concourse.mybir as mybir
from concourse.bass_utils import run_bass_kernel_spmd
from concourse.hw_specs import get_activation_tables

F32 = mybir.dt.float32
BF16 = mybir.dt.bfloat16
AF = mybir.ActivationFunctionType
OP = mybir.AluOpType
AX = mybir.AxisListType

B = 8
D = 1024
T = 4096
FC = 8
TC = 512
NCH = T // TC
SUB = 128
NB2 = 64
NEXP = 8
HDIM = 4096
HC = 512
MF = 64
INNER = 16

LND_CLAMP = -85.0

_AF = mybir.ActivationFunctionType
ACT_NEUTRAL = {_AF.Identity, _AF.Copy, _AF.Square, _AF.Relu, _AF.Abs, _AF.Sign, _AF.MemsetZero}
ACT_FUNC_SET = {_AF.Exp: "nlexp", _AF.Ln: "nlexp", _AF.Sigmoid: "sigmoid",
                _AF.Gelu: "gelu", _AF.Tanh: "gelu", _AF.Silu: "silu"}
ACT_SET_NAMES = {"nlexp": "natural_log_exp_and_others", "sigmoid": "sigmoid_and_others",
                 "gelu": "gelu_and_others", "silu": "silu_and_others"}
ACT_SET_REP = {"nlexp": _AF.Exp, "sigmoid": _AF.Sigmoid, "gelu": _AF.Gelu,
               "silu": _AF.Silu}


class Buf:
    __slots__ = ("w", "r", "dsem")

    def __init__(self):
        self.w = {}
        self.r = {}
        self.dsem = None


class Prog:
    ENGS = ("pe", "ve", "se", "po", "sy")

    def __init__(self, nc):
        self.nc = nc
        self.es = ExitStack()
        self.E = {"pe": nc.tensor, "ve": nc.vector, "se": nc.scalar,
                  "po": nc.gpsimd, "sy": nc.sync}
        self.sem = {}
        self.cnt = {}
        for s in ("pe", "ve", "se", "po", "qsy", "qse", "qpo"):
            self.sem[s] = self.es.enter_context(nc.semaphore("sem_" + s))
            self.cnt[s] = 0
        self.waited = {}
        self.engsem = {"pe": "pe", "ve": "ve", "se": "se", "po": "po"}
        self.drained = {"ve": 0, "se": 0, "po": 0}
        self.drain_marks = {"ve": [], "se": [], "po": []}
        self.default_scope = None
        self.extra_dep = None
        self.uid = 0

    def name(self, p):
        self.uid += 1
        return f"{p}_{self.uid}"

    def S(self, p, shape, dt, scope=None):
        side = "left" if scope is None else "right"
        t = (scope or self.default_scope or self.es).enter_context(
            self.nc.sbuf_tensor(self.name(p), shape, dt, side=side))
        return t, Buf()

    def cast_staging(self):
        if not hasattr(self, "_cstg"):
            sc = ExitStack()
            self.es.callback(sc.close)
            stgf, _ = self.S("cstgf", [128, 2 * 512], F32, scope=sc)
            stgb, _ = self.S("cstgb", [128, 2 * 512], BF16, scope=sc)
            self._cstg = (stgf, stgb, [(Buf(), Buf()), (Buf(), Buf())], [0])
            self._cast_pend = None
        return self._cstg

    def cast_tile(self, src_ap, dst_ap, dst_buf, src_buf=None):
        stgf, stgb, slotbufs, slot = self.cast_staging()
        sl = slot[0]
        slot[0] = 1 - slot[0]
        fb, bb = slotbufs[sl]
        fap = stgf[:, sl * 512:(sl + 1) * 512]
        bap = stgb[:, sl * 512:(sl + 1) * 512]
        self.dma("po", fap, src_ap, reads=(src_buf,) if src_buf is not None else (),
                 writes=(fb,))
        pend = self._cast_pend
        self._cast_pend = (fap, bap, fb, bb, dst_ap, dst_buf)
        if pend is not None:
            self._cast_emit(pend)

    def _cast_emit(self, pend):
        fap, bap, fb, bb, dst_ap, dst_buf = pend
        self.op("po", lambda e, bap=bap, fap=fap: e.tensor_copy(bap, fap),
                reads=(fb,), writes=(bb,))
        self.dma("po", dst_ap, bap, reads=(bb,), writes=(dst_buf,))

    def cast_flush(self):
        if getattr(self, "_cast_pend", None) is not None:
            self._cast_emit(self._cast_pend)
            self._cast_pend = None

    def barrier(self):
        snap = dict(self.cnt)
        for e in self.ENGS:
            for s, v in snap.items():
                if v > 0:
                    self._wait(e, s, v)

    def PS(self, p, shape, dt=F32, scope=None):
        t = (scope or self.default_scope or self.es).enter_context(
            self.nc.psum_tensor(self.name(p), shape, dt))
        return t, Buf()

    def DR(self, p, shape, dt, kind="Internal"):
        t = self.nc.dram_tensor(self.name(p) if kind == "Internal" else p, shape, dt, kind=kind)
        return t, Buf()

    CLEAN_OP_EDGES = {("se", "pe"), ("ve", "pe"), ("po", "pe"), ("po", "ve"),
                      ("ve", "se")}

    def _wait(self, eng, semname, val, skip_same=True, need_drain=True, dma=False):
        if skip_same and self.engsem.get(eng) == semname:
            return
        if need_drain and not dma and (semname, eng) in self.CLEAN_OP_EDGES:
            need_drain = False
        if need_drain and semname in self.drain_marks and eng != semname:
            cover = None
            for covered, dcnt in self.drain_marks[semname]:
                if covered >= val:
                    cover = dcnt
                    break
            if cover is None:
                covered = self.cnt[semname]
                dinst = self.E[semname].drain()
                self.cnt[semname] += 1
                dinst.then_inc(self.sem[semname], 1)
                cover = self.cnt[semname]
                self.drain_marks[semname].append((covered, cover))
                self.drained[semname] = cover
            val = cover
        key = (eng, semname)
        if self.waited.get(key, 0) >= val:
            return
        self.E[eng].wait_ge(self.sem[semname], val)
        self.waited[key] = val

    def _deps(self, eng, reads, writes, skip_same=True, need_drain=True, dma=False):
        need = {}
        if self.extra_dep is not None:
            for s, v in self.extra_dep.items():
                if need.get(s, 0) < v:
                    need[s] = v
        for b in reads:
            for s, v in b.w.items():
                if need.get(s, 0) < v:
                    need[s] = v
        for b in writes:
            for s, v in b.w.items():
                if need.get(s, 0) < v:
                    need[s] = v
            for s, v in b.r.items():
                if need.get(s, 0) < v:
                    need[s] = v
        for s, v in need.items():
            self._wait(eng, s, v, skip_same=skip_same, need_drain=need_drain, dma=dma)

    def _finish(self, sname, val, reads, writes):
        for b in reads:
            if b.r.get(sname, 0) < val:
                b.r[sname] = val
        for b in writes:
            if b.w.get(sname, 0) < val:
                b.w[sname] = val
            b.r = {}

    def fence(self, eng):
        covered = self.cnt[eng]
        inst = self.E[eng].drain()
        self.cnt[eng] += 1
        inst.then_inc(self.sem[eng], 1)
        if eng in self.drained:
            self.drained[eng] = self.cnt[eng]
            self.drain_marks[eng].append((covered, self.cnt[eng]))
        return (eng, self.cnt[eng])

    def op(self, eng, fn, reads=(), writes=()):
        self._deps(eng, reads, writes, skip_same=(eng == "pe"),
                   need_drain=(eng != "pe"))
        inst = fn(self.E[eng])
        self.cnt[eng] += 1
        inst.then_inc(self.sem[eng], 1)
        self._finish(eng, self.cnt[eng], reads, writes)
        return (eng, self.cnt[eng])

    def dma(self, q, out, in_, reads=(), writes=(), **kw):
        self._deps(q, reads, writes, skip_same=False, dma=True)
        inst = self.E[q].dma_start(out=out, in_=in_, **kw)
        db = writes[0] if writes else None
        if db is not None:
            if db.dsem is None:
                nm = self.name("dsem")
                self.sem[nm] = self.es.enter_context(self.nc.semaphore(nm))
                self.cnt[nm] = 0
                db.dsem = nm
            qs = db.dsem
        else:
            qs = "q" + q
        self.cnt[qs] += 16
        inst.then_inc(self.sem[qs], 16)
        self._finish(qs, self.cnt[qs], reads, writes)
        return (qs, self.cnt[qs])

    def setup_act_guard(self):
        tabs = list(get_activation_tables(self.nc.m.arch).keys())
        self.set_ids = {k: tabs.index(v) for k, v in ACT_SET_NAMES.items()}
        self.cur_set = None

    def ensure_set(self, s):
        if self.cur_set == s:
            return
        self.cur_set = s
        self.E["se"].add_instruction(mybir.InstLoadActFuncSet(
            name=self.nc.get_next_instruction_name(),
            act_func_set_id=self.set_ids[s], ins=[], outs=[]))

    def act(self, fn_func, out, in_, bias=0.0, scale=1.0, reads=(), writes=()):
        s = ACT_FUNC_SET.get(fn_func)
        if s is not None:
            self.ensure_set(s)
        return self.op("se", lambda e: e.activation(out, in_, fn_func, bias=bias, scale=scale),
                       reads=reads, writes=writes)

    def dump(self, name, ap, reads, shape=None, dt=None):
        t = self.nc.dram_tensor("dbg_" + name, shape or list(ap.shape), dt or ap.dtype, kind="ExternalOutput")
        self.dma("po", t.ap()[:, :], ap, reads=reads, writes=(Buf(),))

    def final_wait(self):
        snap = dict(self.cnt)
        for s, v in snap.items():
            if v > 0:
                self._wait("sy", s, v)


def _consts_np():
    i = np.arange(128)
    bf = ml_dtypes.bfloat16
    c = {}
    c["ident_f32"] = np.eye(128, dtype=np.float32)
    c["ident_bf"] = np.eye(128, dtype=np.float32).astype(bf)
    c["ones_col"] = np.ones((128, 1), dtype=np.float32)
    c["ones_row"] = np.ones((1, 128), dtype=np.float32)
    c["ones_row_bfs"] = np.ones((1, 128), dtype=np.float32).astype(bf)
    c["iotaE"] = (i[None, :] - i[:, None] - 1).astype(np.float32)
    c["trimask"] = (i[None, :] > i[:, None]).astype(np.float32)
    c["iotarow"] = i[None, :].astype(np.float32)
    c["revcol"] = (127 - i)[:, None].astype(np.float32)
    return c


CONST_SPECS = (("ident_f32", [128, 128], F32), ("ident_bf", [128, 128], BF16),
               ("ones_col", [128, 1], F32), ("ones_row", [1, 128], F32),
               ("ones_row_bfs", [1, 128], BF16), ("iotaE", [128, 128], F32),
               ("trimask", [128, 128], F32), ("iotarow", [1, 128], F32),
               ("revcol", [128, 1], F32))


DBG = False
STAGE = 99


def build_p1(nc, p, dram):

    def ext_in(nm, shape, dt=F32):
        return nc.dram_tensor(nm, shape, dt, kind="ExternalInput").ap(), Buf()

    xT, xT_b = ext_in("xT", [D, T])
    basis, basis_b = ext_in("L0_attn_basis", [D, NB2])
    coef_in = {}
    for nm in ("q", "k", "v", "o"):
        coef_in[nm] = ext_in(f"L0_attn_{nm}_coeffs", [128, NB2])
    decay, decay_b = ext_in("L0_attn_decay", [1, 1])
    oscale, oscale_b = ext_in("L0_attn_out_scale", [1, 1])
    Win, Win_b = ext_in("L0_ffn_Win", [D, MF])
    binv, binv_b = ext_in("L0_ffn_bin_", [1, MF])
    W1f, W1f_b = ext_in("L0_ffn_W1", [MF, INNER])
    b1f, b1f_b = ext_in("L0_ffn_b1", [MF, INNER])
    W2f, W2f_b = ext_in("L0_ffn_W2", [MF, INNER])
    b2f, b2f_b = ext_in("L0_ffn_b2", [1, MF])
    Wout, Wout_b = ext_in("L0_ffn_Wout", [MF, D])
    boutv, boutv_b = ext_in("L0_ffn_bout", [1, D])
    l1w = {}
    for nm in ("W_in", "W_gx", "W_ga", "W_out"):
        l1w[nm] = ext_in("L1_attn_" + nm, [D, D])
    l1b = {}
    for nm in ("b_in", "b_gx", "b_ga", "b_out", "a_param"):
        l1b[nm] = ext_in("L1_attn_" + nm, [1, D])
    Wg, Wg_b = ext_in("L1_ffn_Wg", [D, NEXP])
    bg, bg_b = ext_in("L1_ffn_bg", [1, NEXP])
    cst = {}
    for nm, shape, dt_ in CONST_SPECS:
        cst[nm] = ext_in(nm, shape, dt_)

    x1scr, x1scr_b = dram["x1scr"]
    s2scr, s2scr_b = dram["s2scr"]
    xn1scr, xn1scr_b = dram["xn1scr"]
    ohscr, ohscr_b = dram["ohscr"]
    x1ap, s2ap, xn1ap, ohap = x1scr.ap(), s2scr.ap(), xn1scr.ap(), ohscr.ap()

    l1wbf = {}
    l1wbf_bufs = {}
    for nm in ("W_in", "W_gx", "W_ga", "W_out"):
        t_, b_ = p.DR("l1wbfT_" + nm, [128, FC * D], BF16)
        l1wbf[nm] = t_
        l1wbf_bufs[nm] = b_
        sap, sbuf = l1w[nm]
        for ic in range(FC):
            for cc in range(2):
                p.cast_tile(sap[ic * 128:(ic + 1) * 128, cc * 512:(cc + 1) * 512],
                            t_.ap()[:, ic * D + cc * 512: ic * D + (cc + 1) * 512],
                            b_, src_buf=sbuf)
    p.cast_flush()

    PBt = [p.PS(f"pb{i}", [128, 512]) for i in range(6)]
    PTb, PTb_b = p.PS("ptb", [128, 1024], BF16)
    PSm, PSm_b = p.PS("psm", [128, 512])

    pre_es = ExitStack()

    def Sld(nm, shape, dt, src, src_b, q="sy", scope=None):
        t, b = p.S(nm, shape, dt, scope=scope)
        p.dma(q, t[:, :], src, reads=(src_b,), writes=(b,))
        return t, b

    idf, idf_b = Sld("idf", [128, 128], F32, cst["ident_f32"][0][:, :], cst["ident_f32"][1])
    idb, idb_b = Sld("idb", [128, 128], BF16, cst["ident_bf"][0][:, :], cst["ident_bf"][1], scope=pre_es)
    onc, onc_b = Sld("onc", [128, 1], F32, cst["ones_col"][0][:, :], cst["ones_col"][1])
    onr, onr_b = Sld("onr", [1, 128], F32, cst["ones_row"][0][:, :], cst["ones_row"][1])
    iotaE, iotaE_b = Sld("iotaE", [128, 128], F32, cst["iotaE"][0][:, :], cst["iotaE"][1], scope=pre_es)
    trim, trim_b = Sld("trim", [128, 128], F32, cst["trimask"][0][:, :], cst["trimask"][1], scope=pre_es)
    iotar, iotar_b = Sld("iotar", [1, 128], F32, cst["iotarow"][0][:, :], cst["iotarow"][1], scope=pre_es)
    revc, revc_b = Sld("revc", [128, 1], F32, cst["revcol"][0][:, :], cst["revcol"][1], scope=pre_es)

    bas, bas_b = p.S("bas", [128, FC * NB2], F32, scope=pre_es)
    for ic in range(FC):
        p.dma("sy", bas[:, ic * NB2:(ic + 1) * NB2], basis[ic * 128:(ic + 1) * 128, :],
              reads=(basis_b,), writes=(bas_b,))
    coef = {}
    for nm in ("q", "k", "v", "o"):
        src, srcb = coef_in[nm]
        coef[nm] = Sld("coef" + nm, [128, NB2], F32, src[:, :], srcb, scope=pre_es)
    dscal, dscal_b = Sld("dscal", [1, 1], F32, decay[:, :], decay_b, scope=pre_es)
    osc, osc_b = Sld("osc", [1, 1], F32, oscale[:, :], oscale_b, scope=pre_es)
    winf, winf_b = p.S("winf", [128, FC * MF], F32, scope=pre_es)
    for ic in range(FC):
        p.dma("sy", winf[:, ic * MF:(ic + 1) * MF], Win[ic * 128:(ic + 1) * 128, :],
              reads=(Win_b,), writes=(winf_b,))
    woutf, woutf_b = Sld("woutf", [MF, D], F32, Wout[:, :], Wout_b, scope=pre_es)
    w1c, w1c_b = Sld("w1c", [MF, INNER], F32, W1f[:, :], W1f_b)
    b1c, b1c_b = Sld("b1c", [MF, INNER], F32, b1f[:, :], b1f_b)
    w2c, w2c_b = Sld("w2c", [MF, INNER], F32, W2f[:, :], W2f_b)
    binr, binr_b = Sld("binr", [1, MF], F32, binv[:, :], binv_b, scope=pre_es)
    b2r, b2r_b = Sld("b2r", [1, MF], F32, b2f[:, :], b2f_b, scope=pre_es)
    boutr, boutr_b = Sld("boutr", [1, D], F32, boutv[:, :], boutv_b, scope=pre_es)
    l1br = {}
    for nm in ("b_in", "b_gx", "b_ga", "b_out", "a_param"):
        l1br[nm] = Sld("r" + nm, [1, D], F32, l1b[nm][0][:, :], l1b[nm][1], scope=pre_es)
    wgt, wgt_b = p.S("wgt", [128, FC * NEXP], F32)
    p.dma("sy", wgt[:, :].rearrange("p (ic e) -> p ic e", ic=FC),
          Wg.rearrange("(ic p) e -> p ic e", p=128), reads=(Wg_b,), writes=(wgt_b,))
    bgr, bgr_b = Sld("bgr", [1, NEXP], F32, bg[:, :], bg_b, scope=pre_es)

    def row_to_cols(row_t, row_b, ncols, name, scope=None):
        colt, colb = p.S(name, [128, max(1, ncols // 128)], F32, scope=scope)
        for j in range(max(1, ncols // 128)):
            w = min(128, ncols - j * 128)
            p.op("pe", lambda e, j=j, w=w: e.transpose(PBt[4][0][0:w, 0:1],
                                                       row_t[0:1, j * 128:j * 128 + w], idf[0:1, 0:1]),
                 reads=(row_b, idf_b), writes=(PBt[4][1],))
            p.op("ve", lambda e, j=j, w=w: e.tensor_copy(colt[0:w, j:j + 1], PBt[4][0][0:w, 0:1]),
                 reads=(PBt[4][1],), writes=(colb,))
        return colt, colb

    binc, binc_b = row_to_cols(binr, binr_b, MF, "binc")
    b2cc, b2cc_b = row_to_cols(b2r, b2r_b, MF, "b2cc")
    boutT, boutT_b = row_to_cols(boutr, boutr_b, D, "boutT")
    biasT = {}
    for nm in ("b_in", "b_gx", "b_ga", "b_out"):
        rt_, rb_ = l1br[nm]
        biasT[nm] = row_to_cols(rt_, rb_, D, "bT" + nm)
    apT, apT_b = row_to_cols(l1br["a_param"][0], l1br["a_param"][1], D, "apT", scope=pre_es)
    bgc, bgc_b = row_to_cols(bgr, bgr_b, NEXP, "bgc")

    sp8, sp8_b = p.S("sp8", [128, FC], F32)
    tpb, tpb_b = p.S("pre_t1", [128, FC], F32, scope=pre_es)
    tpc, tpc_b = p.S("pre_t2", [128, FC], F32, scope=pre_es)
    p.act(AF.Abs, PBt[4][0][0:128, 0:FC], apT[:, :], reads=(apT_b,), writes=(PBt[4][1],))
    p.act(AF.Exp, PBt[5][0][0:128, 0:FC], PBt[4][0][0:128, 0:FC], scale=-1.0,
          reads=(PBt[4][1],), writes=(PBt[5][1],))
    p.act(AF.Ln, PBt[4][0][0:128, 0:FC], PBt[5][0][0:128, 0:FC], bias=1.0,
          reads=(PBt[5][1],), writes=(PBt[4][1],))
    p.op("ve", lambda e: e.tensor_scalar_max(tpb[:, :], apT[:, :], 0.0), reads=(apT_b,), writes=(tpb_b,))
    p.fence("ve")
    p.op("ve", lambda e: e.tensor_tensor(tpc[:, :], tpb[:, :], PBt[4][0][0:128, 0:FC], OP.add),
         reads=(tpb_b, PBt[4][1]), writes=(tpc_b,))
    p.fence("ve")
    p.op("ve", lambda e: e.tensor_scalar_mul(sp8[:, :], tpc[:, :], -8.0), reads=(tpc_b,), writes=(sp8_b,))
    p.fence("ve")

    dsig, dsig_b = p.S("dsig", [1, 2], F32, scope=pre_es)
    p.act(AF.Exp, PBt[4][0][0:1, 0:1], dscal[:, :], scale=-1.0, reads=(dscal_b,), writes=(PBt[4][1],))
    p.op("ve", lambda e: e.tensor_scalar_add(dsig[:, 0:1], PBt[4][0][0:1, 0:1], 1.0),
         reads=(PBt[4][1],), writes=(dsig_b,))
    p.fence("ve")
    p.op("ve", lambda e: e.reciprocal(dsig[:, 1:2], dsig[:, 0:1]), reads=(dsig_b,), writes=(dsig_b,))
    p.fence("ve")
    lnd, lnd_b = p.S("lnd", [1, 1], F32, scope=pre_es)
    p.act(AF.Ln, PBt[4][0][0:1, 0:1], dsig[:, 1:2], reads=(dsig_b,), writes=(PBt[4][1],))
    p.op("ve", lambda e: e.tensor_scalar_max(lnd[:, :], PBt[4][0][0:1, 0:1], LND_CLAMP),
         reads=(PBt[4][1],), writes=(lnd_b,))
    p.fence("ve")
    lndc, lndc_b = p.S("lndc", [128, 1], F32, scope=pre_es)
    p.op("pe", lambda e: e.matmul(PBt[4][0][0:128, 0:1], onr[0:1, :], lnd[0:1, 0:1],
                                  start=True, stop=True), reads=(onr_b, lnd_b), writes=(PBt[4][1],))
    p.op("ve", lambda e: e.tensor_copy(lndc[:, 0:1], PBt[4][0][0:128, 0:1]),
         reads=(PBt[4][1],), writes=(lndc_b,))
    p.fence("ve")
    Dmask, Dmask_b = p.S("Dmask", [128, 128], F32)
    p.op("ve", lambda e: e.tensor_scalar_mul(Dmask[:, :], iotaE[:, :], lndc[:, 0:1]),
         reads=(iotaE_b, lndc_b), writes=(Dmask_b,))
    p.act(AF.Exp, PBt[4][0][:, 0:128], Dmask[:, :], reads=(Dmask_b,), writes=(PBt[4][1],))
    p.op("ve", lambda e: e.tensor_tensor(Dmask[:, :], PBt[4][0][:, 0:128], trim[:, :], OP.mult),
         reads=(PBt[4][1], trim_b), writes=(Dmask_b,))
    drow, drow_b = p.S("drow", [1, 128], F32, scope=pre_es)
    p.op("ve", lambda e: e.tensor_scalar_mul(drow[:, :], iotar[:, :], lnd[0:1, 0:1]),
         reads=(iotar_b, lnd_b), writes=(drow_b,))
    p.act(AF.Exp, PBt[5][0][0:1, 0:128], drow[:, :], reads=(drow_b,), writes=(PBt[5][1],))
    p.op("ve", lambda e: e.tensor_copy(drow[:, :], PBt[5][0][0:1, 0:128]),
         reads=(PBt[5][1],), writes=(drow_b,))
    dprow, dprow_b = p.S("dprow", [128, 128], F32)
    p.op("pe", lambda e: e.matmul(PBt[4][0][0:128, 0:128], onr[0:1, :], drow[0:1, :],
                                  start=True, stop=True), reads=(onr_b, drow_b), writes=(PBt[4][1],))
    p.op("ve", lambda e: e.tensor_copy(dprow[:, :], PBt[4][0][0:128, 0:128]),
         reads=(PBt[4][1],), writes=(dprow_b,))
    dpcol, dpcol_b = p.S("dpcol", [128, 1], F32)
    p.op("ve", lambda e: e.tensor_scalar_mul(dpcol[:, :], revc[:, :], lndc[:, 0:1]),
         reads=(revc_b, lndc_b), writes=(dpcol_b,))
    p.act(AF.Exp, PBt[5][0][0:128, 0:1], dpcol[:, :], reads=(dpcol_b,), writes=(PBt[5][1],))
    p.op("ve", lambda e: e.tensor_copy(dpcol[:, :], PBt[5][0][0:128, 0:1]),
         reads=(PBt[5][1],), writes=(dpcol_b,))
    dTc, dTc_b = p.S("dTc", [128, 1], F32)
    p.op("ve", lambda e: e.tensor_scalar_mul(dTc[:, :], lndc[:, 0:1], 128.0),
         reads=(lndc_b,), writes=(dTc_b,))
    p.act(AF.Exp, PBt[5][0][0:128, 0:1], dTc[:, :], reads=(dTc_b,), writes=(PBt[5][1],))
    p.op("ve", lambda e: e.tensor_copy(dTc[:, :], PBt[5][0][0:128, 0:1]),
         reads=(PBt[5][1],), writes=(dTc_b,))

    basT, basT_b = p.S("basT", [NB2, D], F32, scope=pre_es)
    for ic in range(FC):
        p.op("pe", lambda e, ic=ic: e.transpose(PBt[5][0][0:NB2, 0:128],
                                                bas[:, ic * NB2:(ic + 1) * NB2], idf[:, :]),
             reads=(bas_b, idf_b), writes=(PBt[5][1],))
        p.op("ve", lambda e, ic=ic: e.tensor_copy(basT[:, ic * 128:(ic + 1) * 128], PBt[5][0][0:NB2, 0:128]),
             reads=(PBt[5][1],), writes=(basT_b,))

    wnat = {}
    owT = owT_b = None
    for nm in ("q", "k", "v", "o"):
        ct, cb = coef[nm]
        cT, cTb = p.S("cT" + nm, [NB2, 128], F32, scope=pre_es)
        p.op("pe", lambda e, ct=ct: e.transpose(PBt[5][0][0:NB2, 0:128], ct[:, :], idf[:, :]),
             reads=(cb, idf_b), writes=(PBt[5][1],))
        p.op("ve", lambda e, cT=cT: e.tensor_copy(cT[:, :], PBt[5][0][0:NB2, 0:128]),
             reads=(PBt[5][1],), writes=(cTb,))
        wT, wTb = p.S("wT" + nm, [128, D], F32, scope=pre_es)
        for hh in range(2):
            p.op("pe", lambda e, hh=hh, cT=cT: e.matmul(PBt[hh][0][:, :], cT[:, :],
                                                        basT[0:NB2, hh * 512:(hh + 1) * 512],
                                                        start=True, stop=True),
                 reads=(cTb, basT_b), writes=(PBt[hh][1],))
            if nm == "o":
                p.op("ve", lambda e, hh=hh, wT=wT: e.tensor_copy(wT[:, hh * 512:(hh + 1) * 512],
                                                                 PBt[hh][0][:, :]),
                     reads=(PBt[hh][1],), writes=(wTb,))
        if nm == "o":
            oscb, oscb_b = p.S("oscb", [128, 1], F32, scope=pre_es)
            p.op("pe", lambda e: e.matmul(PBt[4][0][0:128, 0:1], onr[0:1, :], osc[0:1, 0:1],
                                          start=True, stop=True), reads=(onr_b, osc_b), writes=(PBt[4][1],))
            p.op("ve", lambda e: e.tensor_copy(oscb[:, 0:1], PBt[4][0][0:128, 0:1]),
                 reads=(PBt[4][1],), writes=(oscb_b,))
            owT, owT_b = p.S("owT", [128, D], BF16)
            p.op("se", lambda e: e.activation(owT[:, :], wT[:, :], AF.Identity, scale=oscb[:, 0:1]),
                 reads=(wTb, oscb_b), writes=(owT_b,))
        else:
            mx, mx_b = p.S("mxw", [128, 4], F32, scope=pre_es)
            p.op("ve", lambda e: e.reduce_max(mx[:, 0:1], PBt[0][0][:, :], axis=AX.X),
                 reads=(PBt[0][1],), writes=(mx_b,))
            p.op("ve", lambda e: e.reduce_max(mx[:, 1:2], PBt[1][0][:, :], axis=AX.X),
                 reads=(PBt[1][1],), writes=(mx_b,))
            p.fence("ve")
            p.op("ve", lambda e: e.tensor_max(mx[:, 2:3], mx[:, 0:1], mx[:, 1:2]), reads=(mx_b,), writes=(mx_b,))
            p.op("se", lambda e: e.mul(mx[:, 3:4], mx[:, 2:3], -1.0), reads=(mx_b,), writes=(mx_b,))
            if DBG and nm == "q":
                p.dump("nmx", mx[:, 3:4], (mx_b,))
            for hh in range(2):
                p.act(AF.Exp, PBt[2 + hh][0][:, :], PBt[hh][0][:, :], bias=mx[:, 3:4],
                      reads=(PBt[hh][1], mx_b), writes=(PBt[2 + hh][1],))
            sm, sm_b = p.S("smw", [128, 4], F32, scope=pre_es)
            p.op("ve", lambda e: e.reduce_sum(sm[:, 0:1], PBt[2][0][:, :], axis=AX.X),
                 reads=(PBt[2][1],), writes=(sm_b,))
            p.op("ve", lambda e: e.reduce_sum(sm[:, 1:2], PBt[3][0][:, :], axis=AX.X),
                 reads=(PBt[3][1],), writes=(sm_b,))
            p.fence("ve")
            p.op("ve", lambda e: e.tensor_add(sm[:, 2:3], sm[:, 0:1], sm[:, 1:2]), reads=(sm_b,), writes=(sm_b,))
            p.fence("ve")
            p.op("ve", lambda e: e.reciprocal(sm[:, 3:4], sm[:, 2:3]), reads=(sm_b,), writes=(sm_b,))
            p.fence("ve")
            if DBG and nm == "q":
                p.dump("smq", sm[:, :], (sm_b,))
                p.dump("mxq", mx[:, :], (mx_b,))
            wTb16, wTb16_b = p.S("wTb" + nm, [128, D], BF16, scope=pre_es)
            for hh in range(2):
                p.op("ve", lambda e, hh=hh: e.tensor_scalar_mul(wTb16[:, hh * 512:(hh + 1) * 512],
                                                                PBt[2 + hh][0][:, :], sm[:, 3:4]),
                     reads=(PBt[2 + hh][1], sm_b), writes=(wTb16_b,))
            nat, nat_b = p.S("wnat" + nm, [128, FC * 128], BF16)
            for ic in range(FC):
                p.op("pe", lambda e, ic=ic: e.transpose(PTb[0:128, 0:128],
                                                        wTb16[:, ic * 128:(ic + 1) * 128], idb[:, :]),
                     reads=(wTb16_b, idb_b), writes=(PTb_b,))
                p.op("ve", lambda e, ic=ic: e.tensor_copy(nat[:, ic * 128:(ic + 1) * 128], PTb[0:128, 0:128]),
                     reads=(PTb_b,), writes=(nat_b,))
            wnat[nm] = (nat, nat_b)
            if DBG and nm == "q":
                p.dump("basT", basT[:, :], (basT_b,))
                p.dump("cTq", cT[:, :], (cTb,))
                p.dump("wTq", wT[:, :], (wTb,))
                p.dump("wTb16q", wTb16[:, :], (wTb16_b,))
                p.dump("wnatq", nat[:, :], (nat_b,))

    winb, winb_b = p.S("winb", [128, FC * MF], BF16)
    p.op("ve", lambda e: e.tensor_copy(winb[:, :], winf[:, :]), reads=(winf_b,), writes=(winb_b,))
    woutb, woutb_b = p.S("woutb", [MF, D], BF16)
    p.op("ve", lambda e: e.tensor_copy(woutb[:, :], woutf[:, :]), reads=(woutf_b,), writes=(woutb_b,))

    pre_es.close()
    p.barrier()

    Mst, Mst_b = p.S("Mst", [128, 128], F32)
    Mbf, Mbf_b = p.S("Mbf", [128, 128], BF16)
    p.op("ve", lambda e: e.memset(Mst[:, :], 0.0), writes=(Mst_b,))
    p.op("ve", lambda e: e.memset(Mbf[:, :], 0.0), writes=(Mbf_b,))
    hcar, hcar_b = p.S("hcar", [128, FC], F32)
    p.op("ve", lambda e: e.memset(hcar[:, :], 0.0), writes=(hcar_b,))

    xab = [p.S("xa", [128, FC * TC], F32)]
    x1buf, x1buf_b = p.S("x1buf", [128, FC * TC], F32)
    xn, xn_b = p.S("xn", [128, FC * TC], BF16)
    s0, s0_b = p.S("s0", [128, FC * TC], F32)
    qT, qT_b = p.S("qT", [128, TC], BF16)
    kT, kT_b = p.S("kT", [128, TC], BF16)
    ktok, ktok_b = p.S("ktok", [128, TC], BF16)
    vtok, vtok_b = p.S("vtok", [128, TC], BF16)
    qtil, qtil_b = p.S("qtil", [128, 128], BF16)
    ktil, ktil_b = p.S("ktil", [128, 128], BF16)
    SpT, SpT_b = p.S("SpT", [128, 128], BF16)
    ret, ret_b = p.S("ret", [128, TC], BF16)
    Mtmp, Mtmp_b = p.S("Mtmp", [128, 128], F32)
    sF, sF_b = p.S("sF", [MF, TC], F32)
    gaccs = [p.S(f"gacc{i}", [MF, TC], F32) for i in range(4)]
    gbf, gbf_b = p.S("gbf", [MF, TC], BF16)
    xi_bf, xi_bf_b = p.S("xibf", [128, FC * TC], BF16)
    pool16a = p.S("p16a", [128, FC * TC], F32)
    pool16b = p.S("p16b", [128, FC * TC], F32)
    h_bf, h_bf_b = p.S("hbf", [128, FC * TC], BF16)
    tp = [p.S(f"tp{i}", [128, TC], F32) for i in range(6)]
    tpi = [0]

    def TP():
        t = tp[tpi[0] % len(tp)]
        tpi[0] += 1
        return t

    rows_stats = p.S("rstat", [1, 4 * TC], F32)
    lg, lg_b = p.S("lg", [NEXP, TC], F32)
    ohb, ohb_b = p.S("ohb", [NEXP, TC], BF16)
    ltok, ltok_b = p.S("ltok", [128, 4 * NEXP], F32)
    mxc, mxc_b = p.S("mxc", [128, 4], F32)
    mxr, mxr_b = p.S("mxr", [1, TC], F32)
    wstr = [p.S("wstrA", [128, FC * D], BF16), p.S("wstrB", [128, FC * D], BF16)]

    def ln_apply(xa, xa_b, out_t, out_b, also_f32_cb=None):
        rs, rs_b = rows_stats
        for ic in range(FC):
            sq = TP()
            p.op("se", lambda e, ic=ic, sq=sq: e.square(sq[0][:, :], xa[:, ic * TC:(ic + 1) * TC]),
                 reads=(xa_b,), writes=(sq[1],))
            p.op("pe", lambda e, ic=ic: e.matmul(PBt[2][0][0:1, :], onc[:, 0:1], xa[:, ic * TC:(ic + 1) * TC],
                                                 start=(ic == 0), stop=(ic == FC - 1)),
                 reads=(xa_b, onc_b), writes=(PBt[2][1],))
            p.op("pe", lambda e, ic=ic, sq=sq: e.matmul(PBt[3][0][0:1, :], onc[:, 0:1], sq[0][:, :],
                                                        start=(ic == 0), stop=(ic == FC - 1)),
                 reads=(sq[1], onc_b), writes=(PBt[3][1],))
        p.op("ve", lambda e: e.tensor_scalar_mul(rs[0:1, 0:TC], PBt[2][0][0:1, :], 1.0 / D),
             reads=(PBt[2][1],), writes=(rs_b,))
        p.op("ve", lambda e: e.tensor_scalar_mul(rs[0:1, TC:2 * TC], PBt[3][0][0:1, :], 1.0 / D),
             reads=(PBt[3][1],), writes=(rs_b,))
        p.op("ve", lambda e: e.tensor_tensor(rs[0:1, 2 * TC:3 * TC], rs[0:1, 0:TC], rs[0:1, 0:TC], OP.mult),
             reads=(rs_b,), writes=(rs_b,))
        p.op("ve", lambda e: e.tensor_sub(rs[0:1, 3 * TC:4 * TC], rs[0:1, TC:2 * TC], rs[0:1, 2 * TC:3 * TC]),
             reads=(rs_b,), writes=(rs_b,))
        p.op("ve", lambda e: e.tensor_scalar_add(rs[0:1, 2 * TC:3 * TC], rs[0:1, 3 * TC:4 * TC], 1e-5),
             reads=(rs_b,), writes=(rs_b,))
        p.act(AF.Ln, PBt[2][0][0:1, :], rs[0:1, 2 * TC:3 * TC], reads=(rs_b,), writes=(PBt[2][1],))
        p.act(AF.Exp, PBt[3][0][0:1, :], PBt[2][0][0:1, :], scale=-0.5,
              reads=(PBt[2][1],), writes=(PBt[3][1],))
        p.op("ve", lambda e: e.tensor_copy(rs[0:1, 3 * TC:4 * TC], PBt[3][0][0:1, :]),
             reads=(PBt[3][1],), writes=(rs_b,))
        p.op("pe", lambda e: e.matmul(PBt[2][0][:, :], onr[0:1, :], rs[0:1, 0:TC],
                                      start=True, stop=True), reads=(rs_b, onr_b), writes=(PBt[2][1],))
        p.op("pe", lambda e: e.matmul(PBt[3][0][:, :], onr[0:1, :], rs[0:1, 3 * TC:4 * TC],
                                      start=True, stop=True), reads=(rs_b, onr_b), writes=(PBt[3][1],))
        for ic in range(FC):
            t1 = TP()
            p.op("ve", lambda e, ic=ic, t1=t1: e.tensor_tensor(t1[0][:, :], xa[:, ic * TC:(ic + 1) * TC],
                                                               PBt[2][0][:, :], OP.subtract),
                 reads=(xa_b, PBt[2][1]), writes=(t1[1],))
            if also_f32_cb is None:
                p.op("ve", lambda e, ic=ic, t1=t1: e.tensor_tensor(out_t[:, ic * TC:(ic + 1) * TC],
                                                                   t1[0][:, :], PBt[3][0][:, :], OP.mult),
                     reads=(t1[1], PBt[3][1]), writes=(out_b,))
            else:
                tf = TP()
                p.op("ve", lambda e, t1=t1, tf=tf: e.tensor_tensor(tf[0][:, :], t1[0][:, :],
                                                                   PBt[3][0][:, :], OP.mult),
                     reads=(t1[1], PBt[3][1]), writes=(tf[1],))
                p.op("ve", lambda e, ic=ic, tf=tf: e.tensor_copy(out_t[:, ic * TC:(ic + 1) * TC], tf[0][:, :]),
                     reads=(tf[1],), writes=(out_b,))
                also_f32_cb(ic, tf)

    def load_x(c, slot):
        xt, xb = xab[slot]
        p.dma("sy", xt[:, :].rearrange("p (fc t) -> p fc t", fc=FC),
              xT.rearrange("(fc p) t -> p fc t", p=128)[:, :, c * TC:(c + 1) * TC],
              reads=(xT_b,), writes=(xb,))

    def load_w(nm, slot):
        wt, wb = wstr[slot]
        p.dma("sy", wt[:, :], l1wbf[nm].ap()[:, :],
              reads=(l1wbf_bufs[nm],), writes=(wb,))
        return wt, wb

    xa, xa_b = xab[0]

    def emit_ln0():
        ln_apply(xa, xa_b, xn, xn_b)

    def emit_qkv():
        for dst, dstb, nm, ev in ((qT, qT_b, "q", "ve"), (kT, kT_b, "k", "se")):
            nat, nat_b = wnat[nm]
            for ic in range(FC):
                p.op("pe", lambda e, ic=ic, nat=nat: e.matmul(PBt[0][0][:, :], nat[:, ic * 128:(ic + 1) * 128],
                                                              xn[:, ic * TC:(ic + 1) * TC],
                                                              start=(ic == 0), stop=(ic == FC - 1)),
                     reads=(nat_b, xn_b), writes=(PBt[0][1],))
            if ev == "ve":
                p.op("ve", lambda e, dst=dst: e.tensor_copy(dst[:, :], PBt[0][0][:, :]),
                     reads=(PBt[0][1],), writes=(dstb,))
            else:
                p.op("se", lambda e, dst=dst: e.copy(dst[:, :], PBt[0][0][:, :]),
                     reads=(PBt[0][1],), writes=(dstb,))
        for dst, dstb, nm, ev in ((ktok, ktok_b, "k", "ve"), (vtok, vtok_b, "v", "se")):
            nat, nat_b = wnat[nm]
            for s in range(TC // SUB):
                pb = PBt[s % 2]
                for ic in range(FC):
                    p.op("pe", lambda e, s=s, ic=ic, nat=nat, pb=pb: e.matmul(
                        pb[0][:, 0:128], xn[:, ic * TC + s * SUB: ic * TC + (s + 1) * SUB],
                        nat[:, ic * 128:(ic + 1) * 128],
                        start=(ic == 0), stop=(ic == FC - 1)),
                        reads=(nat_b, xn_b), writes=(pb[1],))
                if ev == "ve":
                    p.op("ve", lambda e, s=s, dst=dst, pb=pb: e.tensor_copy(dst[:, s * SUB:(s + 1) * SUB],
                                                                            pb[0][:, 0:128]),
                         reads=(pb[1],), writes=(dstb,))
                else:
                    p.op("se", lambda e, s=s, dst=dst, pb=pb: e.copy(dst[:, s * SUB:(s + 1) * SUB],
                                                                     pb[0][:, 0:128]),
                         reads=(pb[1],), writes=(dstb,))

    load_x(0, 0)
    emit_ln0()
    emit_qkv()
    for c in range(NCH):
        if DBG and c == 0:
            p.dump("xa0", xa[:, :], (xa_b,))
            p.dump("xn0", xn[:, :], (xn_b,))
        if STAGE < 2:
            continue
        if DBG and c == 0:
            p.dump("qT0", qT[:, :], (qT_b,))
            p.dump("kT0", kT[:, :], (kT_b,))
            p.dump("ktok0", ktok[:, :], (ktok_b,))
            p.dump("vtok0", vtok[:, :], (vtok_b,))
        if STAGE < 3:
            continue
        for s in range(TC // SUB):
            sl = slice(s * SUB, (s + 1) * SUB)
            p.op("pe", lambda e, sl=sl: e.matmul(PBt[4][0][:, 0:128], kT[:, sl], qT[:, sl],
                                                 start=True, stop=True),
                 reads=(kT_b, qT_b), writes=(PBt[4][1],))
            p.op("ve", lambda e: e.tensor_tensor(SpT[:, :], PBt[4][0][:, 0:128], Dmask[:, :], OP.mult),
                 reads=(PBt[4][1], Dmask_b), writes=(SpT_b,))
            p.op("ve", lambda e, sl=sl: e.tensor_tensor(qtil[:, :], qT[:, sl], dprow[:, :], OP.mult),
                 reads=(qT_b, dprow_b), writes=(qtil_b,))
            p.op("pe", lambda e, sl=sl: e.matmul(PBt[5][0][:, 0:128], vtok[:, sl], SpT[:, :],
                                                 start=True, stop=False),
                 reads=(vtok_b, SpT_b), writes=(PBt[5][1],))
            p.op("pe", lambda e: e.matmul(PBt[5][0][:, 0:128], Mbf[:, :], qtil[:, :],
                                          start=False, stop=True),
                 reads=(Mbf_b, qtil_b), writes=(PBt[5][1],))
            p.op("se", lambda e, sl=sl: e.copy(ret[:, sl], PBt[5][0][:, 0:128]),
                 reads=(PBt[5][1],), writes=(ret_b,))
            p.op("ve", lambda e, sl=sl: e.tensor_scalar_mul(ktil[:, :], ktok[:, sl], dpcol[:, 0:1]),
                 reads=(ktok_b, dpcol_b), writes=(ktil_b,))
            p.op("pe", lambda e, sl=sl: e.matmul(PBt[4][0][:, 0:128], ktil[:, :], vtok[:, sl],
                                                 start=True, stop=True),
                 reads=(ktil_b, vtok_b), writes=(PBt[4][1],))
            p.op("ve", lambda e: e.tensor_scalar_mul(Mtmp[:, :], Mst[:, :], dTc[:, 0:1]),
                 reads=(Mst_b, dTc_b), writes=(Mtmp_b,))
            p.op("ve", lambda e: e.tensor_tensor(Mst[:, :], Mtmp[:, :], PBt[4][0][:, 0:128], OP.add),
                 reads=(Mtmp_b, PBt[4][1]), writes=(Mst_b,))
            p.op("se", lambda e: e.copy(Mbf[:, :], Mst[:, :]), reads=(Mst_b,), writes=(Mbf_b,))
        if STAGE < 4:
            continue
        for vcB in range(FC):
            pb = PBt[vcB % 2]
            p.op("pe", lambda e, vcB=vcB, pb=pb: e.matmul(pb[0][:, :], owT[:, vcB * 128:(vcB + 1) * 128],
                                                          ret[:, :], start=True, stop=True),
                 reads=(owT_b, ret_b), writes=(pb[1],))
            p.op("ve", lambda e, vcB=vcB, pb=pb: e.tensor_copy(s0[:, vcB * TC:(vcB + 1) * TC], pb[0][:, :]),
                 reads=(pb[1],), writes=(s0_b,))

        if DBG and c == 0:
            p.dump("ret0", ret[:, :], (ret_b,))
            p.dump("s00", s0[:, :], (s0_b,))
            p.dump("Dmask", Dmask[:, :], (Dmask_b,))
            p.dump("dprow", dprow[:, :], (dprow_b,))
        if STAGE < 5:
            continue
        for ic in range(FC):
            p.op("pe", lambda e, ic=ic: e.matmul(PBt[0][0][0:MF, :], winb[:, ic * MF:(ic + 1) * MF],
                                                 xn[:, ic * TC:(ic + 1) * TC],
                                                 start=(ic == 0), stop=(ic == FC - 1)),
                 reads=(winb_b, xn_b), writes=(PBt[0][1],))
        p.op("se", lambda e: e.activation(sF[:, :], PBt[0][0][0:MF, :], AF.Identity, bias=binc[0:MF, 0:1]),
             reads=(PBt[0][1], binc_b), writes=(sF_b,))
        if STAGE < 4.2:
            continue
        for h in range(INNER):
            rp = PBt[2 + h % 2]
            par = h % 2
            k = h // 2
            ga_o, ga_ob = gaccs[(k % 2) * 2 + par]
            ga_i, ga_ib = gaccs[((k + 1) % 2) * 2 + par]
            p.act(AF.Relu, rp[0][0:MF, :], sF[:, :], bias=b1c[:, h:h + 1], scale=w1c[:, h:h + 1],
                  reads=(sF_b, b1c_b, w1c_b), writes=(rp[1],))
            if h < 2:
                p.op("ve", lambda e, h=h, rp=rp, ga_o=ga_o: e.tensor_scalar_mul(ga_o[:, :], rp[0][0:MF, :],
                                                                                w2c[:, h:h + 1]),
                     reads=(rp[1], w2c_b), writes=(ga_ob,))
            else:
                p.op("ve", lambda e, h=h, rp=rp, ga_o=ga_o, ga_i=ga_i: e.scalar_tensor_tensor(
                    ga_o[:, :], rp[0][0:MF, :], w2c[:, h:h + 1], ga_i[:, :], OP.mult, OP.add),
                     reads=(rp[1], w2c_b, ga_ib), writes=(ga_ob,))
        if STAGE < 4.3:
            continue
        ke = (INNER - 2) // 2
        ko = (INNER - 1) // 2
        geven, geven_b = gaccs[(ke % 2) * 2]
        godd, godd_b = gaccs[(ko % 2) * 2 + 1]
        gfin, gfin_b = gaccs[((ke + 1) % 2) * 2]
        p.op("ve", lambda e: e.tensor_tensor(gfin[:, :], geven[:, :], godd[:, :], OP.add),
             reads=(geven_b, godd_b), writes=(gfin_b,))
        p.act(AF.Identity, gbf[:, :], gfin[:, :], bias=b2cc[0:MF, 0:1],
              reads=(gfin_b, b2cc_b), writes=(gbf_b,))
        if STAGE < 4.4:
            continue
        for dc in range(FC):
            pb = PBt[dc % 2]
            p.op("pe", lambda e, dc=dc, pb=pb: e.matmul(pb[0][:, :], woutb[:, dc * 128:(dc + 1) * 128],
                                                        gbf[:, :], start=True, stop=True),
                 reads=(woutb_b, gbf_b), writes=(pb[1],))
            sg = PBt[2 + dc % 2]
            p.act(AF.Silu, sg[0][:, :], pb[0][:, :], bias=boutT[:, dc:dc + 1],
                  reads=(pb[1], boutT_b), writes=(sg[1],))
            tm = TP()
            p.op("ve", lambda e, dc=dc, sg=sg, tm=tm: e.tensor_tensor(tm[0][:, :], s0[:, dc * TC:(dc + 1) * TC],
                                                                      sg[0][:, :], OP.mult),
                 reads=(s0_b, sg[1]), writes=(tm[1],))
            p.op("ve", lambda e, dc=dc, tm=tm: e.tensor_tensor(x1buf[:, dc * TC:(dc + 1) * TC],
                                                               xa[:, dc * TC:(dc + 1) * TC], tm[0][:, :], OP.add),
                 reads=(tm[1], xa_b), writes=(x1buf_b,))
        if STAGE < 4.5:
            continue
        p.dma("se", x1ap.rearrange("(fc p) t -> p fc t", p=128)[:, :, c * TC:(c + 1) * TC],
              x1buf[:, :].rearrange("p (fc t) -> p fc t", fc=FC), reads=(x1buf_b,), writes=(x1scr_b,))
        if c + 1 < NCH:
            load_x(c + 1, 0)
        if DBG and c == 0:
            p.dump("sF0", sF[:, :], (sF_b,))
            p.dump("gacc0", gfin[:, :], (gfin_b,))
            p.dump("x1c0", x1buf[:, :], (x1buf_b,))

        if STAGE < 6:
            continue
        def logits_cb(ic, tf):
            p.op("pe", lambda e, ic=ic, tf=tf: e.matmul(PSm[0:NEXP, :], wgt[:, ic * NEXP:(ic + 1) * NEXP],
                                                        tf[0][:, :], start=(ic == 0), stop=(ic == FC - 1)),
                 reads=(wgt_b, tf[1]), writes=(PSm_b,))

        ln_apply(x1buf, x1buf_b, xn, xn_b, also_f32_cb=logits_cb)
        p.op("ve", lambda e: e.tensor_scalar(lg[:, :], PSm[0:NEXP, :], bgc[0:NEXP, 0:1], None, OP.add),
             reads=(PSm_b, bgc_b), writes=(lg_b,))
        p.dma("se", xn1ap.rearrange("(fc p) t -> p fc t", p=128)[:, :, c * TC:(c + 1) * TC],
              xn[:, :].rearrange("p (fc t) -> p fc t", fc=FC), reads=(xn_b,), writes=(xn1scr_b,))

        if STAGE < 7:
            continue
        for s in range(TC // SUB):
            p.op("pe", lambda e, s=s: e.transpose(PBt[4][0][0:128, 0:NEXP],
                                                  lg[:, s * SUB:(s + 1) * SUB], idf[0:NEXP, 0:NEXP]),
                 reads=(lg_b, idf_b), writes=(PBt[4][1],))
            p.op("ve", lambda e, s=s: e.tensor_copy(ltok[:, s * NEXP:(s + 1) * NEXP], PBt[4][0][0:128, 0:NEXP]),
                 reads=(PBt[4][1],), writes=(ltok_b,))
            p.op("ve", lambda e, s=s: e.reduce_max(mxc[:, s:s + 1], ltok[:, s * NEXP:(s + 1) * NEXP], axis=AX.X),
                 reads=(ltok_b,), writes=(mxc_b,))
            p.op("pe", lambda e, s=s: e.transpose(PBt[4][0][0:1, 0:128], mxc[:, s:s + 1], idf[:, :]),
                 reads=(mxc_b, idf_b), writes=(PBt[4][1],))
            p.op("ve", lambda e, s=s: e.tensor_copy(mxr[0:1, s * SUB:(s + 1) * SUB], PBt[4][0][0:1, 0:128]),
                 reads=(PBt[4][1],), writes=(mxr_b,))
        p.op("pe", lambda e: e.matmul(PBt[4][0][0:NEXP, :], onr[0:1, 0:NEXP], mxr[0:1, :],
                                      start=True, stop=True), reads=(onr_b, mxr_b), writes=(PBt[4][1],))
        p.op("ve", lambda e: e.tensor_tensor(ohb[:, :], lg[:, :], PBt[4][0][0:NEXP, :], OP.is_ge),
             reads=(lg_b, PBt[4][1]), writes=(ohb_b,))
        p.dma("se", ohap[:, c * TC:(c + 1) * TC], ohb[:, :], reads=(ohb_b,), writes=(ohscr_b,))

        if STAGE < 8:
            continue
        xi_f32, xi_f32_b = pool16a
        xg_all, xg_all_b = pool16b
        wt, wb = load_w("W_in", 0)
        for oc2 in range(FC):
            pb = PBt[oc2 % 2]
            for ic in range(FC):
                p.op("pe", lambda e, oc2=oc2, ic=ic, pb=pb, wt=wt: e.matmul(
                    pb[0][:, :], wt[:, ic * D + oc2 * 128: ic * D + (oc2 + 1) * 128],
                    xn[:, ic * TC:(ic + 1) * TC], start=(ic == 0), stop=(ic == FC - 1)),
                    reads=(wb, xn_b), writes=(pb[1],))
            p.op("ve", lambda e, oc2=oc2, pb=pb: e.tensor_scalar(xi_f32[:, oc2 * TC:(oc2 + 1) * TC],
                                                                 pb[0][:, :], biasT["b_in"][0][:, oc2:oc2 + 1],
                                                                 None, OP.add),
                 reads=(pb[1], biasT["b_in"][1]), writes=(xi_f32_b,))
            p.op("ve", lambda e, oc2=oc2: e.tensor_copy(xi_bf[:, oc2 * TC:(oc2 + 1) * TC],
                                                        xi_f32[:, oc2 * TC:(oc2 + 1) * TC]),
                 reads=(xi_f32_b,), writes=(xi_bf_b,))
        if STAGE < 9:
            continue
        wt2, wb2 = load_w("W_gx", 1)
        for oc2 in range(FC):
            pb = PBt[oc2 % 2]
            for ic in range(FC):
                p.op("pe", lambda e, oc2=oc2, ic=ic, pb=pb, wt2=wt2: e.matmul(
                    pb[0][:, :], wt2[:, ic * D + oc2 * 128: ic * D + (oc2 + 1) * 128],
                    xi_bf[:, ic * TC:(ic + 1) * TC], start=(ic == 0), stop=(ic == FC - 1)),
                    reads=(wb2, xi_bf_b), writes=(pb[1],))
            sg = PBt[2 + oc2 % 2]
            p.act(AF.Sigmoid, sg[0][:, :], pb[0][:, :], bias=biasT["b_gx"][0][:, oc2:oc2 + 1],
                  reads=(pb[1], biasT["b_gx"][1]), writes=(sg[1],))
            p.op("ve", lambda e, oc2=oc2, sg=sg: e.tensor_tensor(xg_all[:, oc2 * TC:(oc2 + 1) * TC],
                                                                 sg[0][:, :],
                                                                 xi_f32[:, oc2 * TC:(oc2 + 1) * TC], OP.mult),
                 reads=(sg[1], xi_f32_b), writes=(xg_all_b,))
        if STAGE < 10:
            continue
        a_all, a_all_b = pool16a
        ga_all, ga_all_b = a_all, a_all_b
        wt3, wb3 = load_w("W_ga", 0)
        for oc2 in range(FC):
            pb = PBt[oc2 % 2]
            for ic in range(FC):
                p.op("pe", lambda e, oc2=oc2, ic=ic, pb=pb, wt3=wt3: e.matmul(
                    pb[0][:, :], wt3[:, ic * D + oc2 * 128: ic * D + (oc2 + 1) * 128],
                    xi_bf[:, ic * TC:(ic + 1) * TC], start=(ic == 0), stop=(ic == FC - 1)),
                    reads=(wb3, xi_bf_b), writes=(pb[1],))
            sg = PBt[2 + oc2 % 2]
            p.act(AF.Sigmoid, sg[0][:, :], pb[0][:, :], bias=biasT["b_ga"][0][:, oc2:oc2 + 1],
                  reads=(pb[1], biasT["b_ga"][1]), writes=(sg[1],))
            p.op("ve", lambda e, oc2=oc2, sg=sg: e.tensor_scalar(ga_all[:, oc2 * TC:(oc2 + 1) * TC],
                                                                 sg[0][:, :], sp8[:, oc2:oc2 + 1],
                                                                 None, OP.mult),
                 reads=(sg[1], sp8_b), writes=(ga_all_b,))
        if STAGE < 11:
            continue
        if c + 1 < NCH and STAGE >= 99:
            emit_ln0()
        for oc2 in range(FC):
            osl = slice(oc2 * TC, (oc2 + 1) * TC)
            ap_ = PBt[2 + oc2 % 2]
            p.act(AF.Exp, ap_[0][:, :], ga_all[:, osl], reads=(ga_all_b,), writes=(ap_[1],))
            e2a = TP()
            e2b = TP()
            p.act(AF.Exp, PSm[:, :], ga_all[:, osl], scale=2.0, reads=(ga_all_b,), writes=(PSm_b,))
            p.op("ve", lambda e, e2a=e2a: e.tensor_scalar(e2a[0][:, :], PSm[:, :], -1.0, 1.0, OP.mult, OP.add),
                 reads=(PSm_b,), writes=(e2a[1],))
            p.op("ve", lambda e, e2a=e2a, e2b=e2b: e.tensor_scalar_max(e2b[0][:, :], e2a[0][:, :], 1e-38),
                 reads=(e2a[1],), writes=(e2b[1],))
            lnt = TP()
            p.act(AF.Ln, lnt[0][:, :], e2b[0][:, :], reads=(e2b[1],), writes=(lnt[1],))
            p.act(AF.Exp, PSm[:, :], lnt[0][:, :], scale=0.5, reads=(lnt[1],), writes=(PSm_b,))
            p.op("ve", lambda e, osl=osl: e.tensor_tensor(ga_all[:, osl], PSm[:, :],
                                                          xg_all[:, osl], OP.mult),
                 reads=(PSm_b, xg_all_b), writes=(ga_all_b,))
            hf = TP()
            p.op("ve", lambda e, osl=osl, hf=hf, ap_=ap_, oc2=oc2: e.tensor_tensor_scan(
                hf[0][:, :], ap_[0][:, :], ga_all[:, osl],
                hcar[:, oc2:oc2 + 1], OP.mult, OP.add),
                reads=(ap_[1], ga_all_b, hcar_b), writes=(hf[1],))
            p.op("ve", lambda e, oc2=oc2, hf=hf: e.tensor_copy(hcar[:, oc2:oc2 + 1], hf[0][:, TC - 1:TC]),
                 reads=(hf[1],), writes=(hcar_b,))
            p.op("se", lambda e, oc2=oc2, hf=hf: e.copy(h_bf[:, oc2 * TC:(oc2 + 1) * TC], hf[0][:, :]),
                 reads=(hf[1],), writes=(h_bf_b,))
        if STAGE < 12:
            continue
        wt4, wb4 = load_w("W_out", 1)
        if c + 1 < NCH and STAGE >= 99:
            emit_qkv()
        for oc2 in range(FC):
            pb = PBt[4 + oc2 % 2]
            for ic in range(FC):
                p.op("pe", lambda e, oc2=oc2, ic=ic, pb=pb, wt4=wt4: e.matmul(
                    pb[0][:, :], wt4[:, ic * D + oc2 * 128: ic * D + (oc2 + 1) * 128],
                    h_bf[:, ic * TC:(ic + 1) * TC], start=(ic == 0), stop=(ic == FC - 1)),
                    reads=(wb4, h_bf_b), writes=(pb[1],))
            s2t = TP()
            p.op("ve", lambda e, oc2=oc2, pb=pb, s2t=s2t: e.tensor_scalar(s2t[0][:, :], pb[0][:, :],
                                                                          biasT["b_out"][0][:, oc2:oc2 + 1],
                                                                          None, OP.add),
                 reads=(pb[1], biasT["b_out"][1]), writes=(s2t[1],))
            p.dma("se", s2ap[oc2 * 128:(oc2 + 1) * 128, c * TC:(c + 1) * TC], s2t[0][:, :],
                  reads=(s2t[1],), writes=(s2scr_b,))


def build_p2_cast(nc, p, w1, w2):
    w1bf_t, _ = p.DR("w1bfT", [NEXP, 128, (HDIM // 512) * FC * 512], BF16)
    w2bf_t, _ = p.DR("w2bfT", [NEXP, 128, 32 * D], BF16)
    w1bf, w2bf = w1bf_t.ap(), w2bf_t.ap()
    w1bufs = [Buf() for _ in range(NEXP)]
    w2bufs = [Buf() for _ in range(NEXP)]
    cast_tile = p.cast_tile

    for e_ in range(NEXP):
        for r in range(D // 128):
            for cc in range(HDIM // 512):
                dst = w1bf[e_, :, (cc * FC + r) * 512:(cc * FC + r + 1) * 512]
                cast_tile(w1[e_, r * 128:(r + 1) * 128, cc * 512:(cc + 1) * 512],
                          dst, w1bufs[e_])
        for r in range(HDIM // 128):
            for cc in range(D // 512):
                dst = w2bf[e_, :, r * D + cc * 512:r * D + (cc + 1) * 512]
                cast_tile(w2[e_, r * 128:(r + 1) * 128, cc * 512:(cc + 1) * 512],
                          dst, w2bufs[e_])
    p.cast_flush()
    return w1bf, w1bufs, w2bf, w2bufs


def build_p2(nc, p, dram, moebf=None, ext_w=None):

    def ext_in(nm, shape, dt=F32):
        return nc.dram_tensor(nm, shape, dt, kind="ExternalInput").ap(), Buf()

    x1ap, x1b = dram["x1scr"][0].ap(), dram["x1scr"][1]
    s2ap, s2b = dram["s2scr"][0].ap(), dram["s2scr"][1]
    xn1ap, xn1b = dram["xn1scr"][0].ap(), dram["xn1scr"][1]
    ohap, oh_b = dram["ohscr"][0].ap(), dram["ohscr"][1]

    if ext_w is None:
        w1, w1_b = ext_in("L1_ffn_W1", [NEXP, D, HDIM])
        w2, w2_b = ext_in("L1_ffn_W2", [NEXP, HDIM, D])
    else:
        w1, w2 = ext_w
        w1_b, w2_b = Buf(), Buf()
    out, out_b = nc.dram_tensor("xT2", [D, T], F32, kind="ExternalOutput").ap(), Buf()

    if moebf is None:
        w1bf, w1bufs, w2bf, w2bufs = build_p2_cast(nc, p, w1, w2)
    else:
        w1bf, w1bufs, w2bf, w2bufs = moebf

    onrb2, onrb2_b = p.S("onrb2", [128, 128], BF16)
    orow = nc.dram_tensor("ones_row_bf2", [1, 128], BF16, kind="ExternalInput").ap()
    p.dma("sy", onrb2[0:1, :], orow[:, :], writes=(onrb2_b,))
    p.dma("sy", onrb2[64:65, :], orow[:, :], writes=(onrb2_b,))

    PB = [p.PS(f"p2pb{i}", [128, 512]) for i in range(6)]

    w2sb, w2sb_b = p.S("w2sb", [128, 32 * D], BF16)
    ybufs = [p.S("ybufA", [128, FC * (T // 2)], BF16), p.S("ybufB", [128, FC * (T // 2)], BF16)]
    h_fm, h_fm_b = p.S("hfm", [128, 32 * 512], BF16)
    w1t = [p.S("w1tA", [128, FC * HC], BF16), p.S("w1tB", [128, FC * HC], BF16)]
    xn1s, xn1s_b = p.S("xn1s", [128, FC * 512], BF16)
    xnm, xnm_b = p.S("xnm", [128, FC * 512], BF16)
    mrowt, _mb = p.S("mrow", [128, 512], BF16)
    mrow = [(mrowt[0:1, :], Buf()), (mrowt[64:65, :], Buf())]
    x1t = [p.S("x1tA", [128, 512], F32)]
    s2t = [p.S("s2tA", [128, 512], F32)]
    ctp = [p.S(f"ctp{i}", [128, 512], F32) for i in range(2)]

    NTS = T // 512
    units = [(th, e_, ts) for th in range(2) for e_ in range(NEXP)
             for ts in range(th * (NTS // 2), (th + 1) * (NTS // 2))]

    def emit_w2load(e_):
        for qq in range(4):
            p.dma("se", w2sb[:, qq * 8 * D:(qq + 1) * 8 * D],
                  w2bf[e_, :, qq * 8 * D:(qq + 1) * 8 * D],
                  reads=(w2bufs[e_],), writes=(w2sb_b,))

    def emit_front(u):
        th, e_, ts = units[u]
        tsl = slice(ts * 512, (ts + 1) * 512)
        p.dma("sy", xn1s[:, :].rearrange("p (fc t) -> p fc t", fc=FC),
              xn1ap.rearrange("(fc p) t -> p fc t", p=128)[:, :, tsl],
              reads=(xn1b,), writes=(xn1s_b,))
        mr, mr_b = mrow[u % 2]
        p.dma("sy", mr[:, :], ohap[e_:e_ + 1, tsl], reads=(oh_b,), writes=(mr_b,))
        obase = 64 * (u % 2)
        p.op("pe", lambda e, mr=mr, obase=obase: e.matmul(
            PB[4][0][:, :], onrb2[obase:obase + 1, :], mr[0:1, :],
            start=True, stop=True),
             reads=(onrb2_b, mr_b), writes=(PB[4][1],))
        for ic in range(FC):
            p.op("ve", lambda e, ic=ic: e.tensor_tensor(xnm[:, ic * 512:(ic + 1) * 512],
                                                        xn1s[:, ic * 512:(ic + 1) * 512],
                                                        PB[4][0][:, :], OP.mult),
                 reads=(xn1s_b, PB[4][1]), writes=(xnm_b,))
        wt, wb = w1t[0]
        p.dma("se", wt[:, :], w1bf[e_, :, 0:FC * 512], reads=(w1bufs[e_],), writes=(wb,))

    for u, (th, e_, ts) in enumerate(units):
        tsl = slice(ts * 512, (ts + 1) * 512)
        if u == 0:
            emit_front(0)
        if ts == th * (NTS // 2):
            emit_w2load(e_)
        for hc in range(HDIM // HC):
            wt, wb = w1t[hc % 2]
            if hc + 1 < HDIM // HC:
                nwt, nwb = w1t[(hc + 1) % 2]
                p.dma("se", nwt[:, :], w1bf[e_, :, (hc + 1) * FC * 512:(hc + 2) * FC * 512],
                      reads=(w1bufs[e_],), writes=(nwb,))
            for hs in range(HC // 128):
                pb = PB[(hc * 4 + hs) % 2]
                for ic in range(FC):
                    p.op("pe", lambda e, hc=hc, hs=hs, ic=ic, pb=pb, wt=wt: e.matmul(
                        pb[0][:, :], wt[:, ic * HC + hs * 128: ic * HC + (hs + 1) * 128],
                        xnm[:, ic * 512:(ic + 1) * 512],
                        start=(ic == 0), stop=(ic == FC - 1)),
                        reads=(wb, xnm_b), writes=(pb[1],))
                p.act(AF.Gelu, h_fm[:, (hc * 4 + hs) * 512:(hc * 4 + hs + 1) * 512], pb[0][:, :],
                      reads=(pb[1],), writes=(h_fm_b,))
        if u + 1 < len(units):
            emit_front(u + 1)
        for dc in range(FC):
            pb = PB[2 + dc % 2]
            for ht in range(32):
                p.op("pe", lambda e, dc=dc, ht=ht, pb=pb: e.matmul(
                    pb[0][:, :], w2sb[:, ht * D + dc * 128: ht * D + (dc + 1) * 128],
                    h_fm[:, ht * 512:(ht + 1) * 512],
                    start=(ht == 0), stop=(ht == 31)),
                    reads=(w2sb_b, h_fm_b), writes=(pb[1],))
            yofs = dc * (T // 2) + (ts % (NTS // 2)) * 512
            ydst, ydst_b = ybufs[e_ % 2]
            ysrc, ysrc_b = ybufs[(e_ + 1) % 2]
            ysl = ydst[:, yofs: yofs + 512]
            if e_ == 0:
                p.op("ve", lambda e, ysl=ysl, pb=pb: e.tensor_copy(ysl, pb[0][:, :]),
                     reads=(pb[1],), writes=(ydst_b,))
            else:
                p.op("ve", lambda e, ysl=ysl, pb=pb, ysrc=ysrc, yofs=yofs: e.tensor_tensor(
                    ysl, pb[0][:, :], ysrc[:, yofs: yofs + 512], OP.add),
                     reads=(pb[1], ysrc_b), writes=(ydst_b,))
            if e_ == NEXP - 1:
                xt, xtb = x1t[0]
                st, stb = s2t[0]
                p.dma("sy", xt[:, :], x1ap[dc * 128:(dc + 1) * 128, tsl], reads=(x1b,), writes=(xtb,))
                p.dma("sy", st[:, :], s2ap[dc * 128:(dc + 1) * 128, tsl], reads=(s2b,), writes=(stb,))
                c0 = ctp[0]
                c1 = ctp[1]
                p.act(AF.Tanh, PB[5][0][:, :], ysl, scale=0.5, reads=(ydst_b,), writes=(PB[5][1],))
                p.op("ve", lambda e, ysl=ysl, c0=c0, st=st: e.tensor_tensor(c0[0][:, :], st[:, :],
                                                                            ysl, OP.mult),
                     reads=(ydst_b, stb), writes=(c0[1],))
                p.op("ve", lambda e, c0=c0, c1=c1: e.tensor_tensor(c1[0][:, :], c0[0][:, :],
                                                                   PB[5][0][:, :], OP.mult),
                     reads=(c0[1], PB[5][1]), writes=(c1[1],))
                p.op("ve", lambda e, c0=c0, st=st, xt=xt: e.scalar_tensor_tensor(st[:, :], c0[0][:, :],
                                                                                 0.5, xt[:, :],
                                                                                 OP.mult, OP.add),
                     reads=(c0[1], xtb), writes=(stb,))
                p.op("ve", lambda e, c1=c1, st=st, c0=c0: e.scalar_tensor_tensor(c0[0][:, :], c1[0][:, :],
                                                                                 0.5, st[:, :],
                                                                                 OP.mult, OP.add),
                     reads=(c1[1], stb), writes=(c0[1],))
                p.dma("se", out[dc * 128:(dc + 1) * 128, tsl], c0[0][:, :],
                      reads=(c0[1],), writes=(out_b,))


def _mk_dram(p, external):
    kind = "ExternalOutput" if external == "out" else ("ExternalInput" if external == "in" else "Internal")
    dram = {}
    specs = (("x1scr", [D, T], F32), ("s2scr", [D, T], F32),
             ("xn1scr", [D, T], BF16), ("ohscr", [NEXP, T], BF16))
    for nm, shape, dt_ in specs:
        if external == "internal":
            t, b = p.DR(nm, shape, dt_, kind="Internal")
        else:
            t = p.nc.dram_tensor(nm, shape, dt_, kind=kind)
            b = Buf()
        dram[nm] = (t, b)
    return dram


def _build(mode):
    nc = bass.Bass("TRN2", target_bir_lowering=False)
    p = Prog(nc)
    p.setup_act_guard()
    if mode == "p1":
        dram = _mk_dram(p, "out")
        build_p1(nc, p, dram)
    elif mode == "p2":
        dram = _mk_dram(p, "in")
        build_p2(nc, p, dram)
    else:
        dram = _mk_dram(p, "internal")
        scope1 = ExitStack()
        p.default_scope = scope1
        build_p1(nc, p, dram)
        p.default_scope = None
        p.fence("po")
        snap = {s: v for s, v in p.cnt.items() if v > 0}
        for e in Prog.ENGS:
            if e == "po":
                continue
            for s, v in snap.items():
                p._wait(e, s, v)
        w1e = nc.dram_tensor("L1_ffn_W1", [NEXP, D, HDIM], F32, kind="ExternalInput").ap()
        w2e = nc.dram_tensor("L1_ffn_W2", [NEXP, HDIM, D], F32, kind="ExternalInput").ap()
        moebf = build_p2_cast(nc, p, w1e, w2e)
        scope1.close()
        p.extra_dep = dict(snap)
        build_p2(nc, p, dram, moebf=moebf, ext_w=(w1e, w2e))
    p.final_wait()
    p.es.close()
    return nc


P1_KEYS = ("xT", "L0_attn_basis", "L0_attn_q_coeffs", "L0_attn_k_coeffs",
           "L0_attn_v_coeffs", "L0_attn_o_coeffs", "L0_attn_decay",
           "L0_attn_out_scale", "L0_ffn_Win", "L0_ffn_bin_", "L0_ffn_W1",
           "L0_ffn_b1", "L0_ffn_W2", "L0_ffn_b2", "L0_ffn_Wout", "L0_ffn_bout",
           "L1_attn_W_in", "L1_attn_b_in", "L1_attn_W_gx", "L1_attn_b_gx",
           "L1_attn_W_ga", "L1_attn_b_ga", "L1_attn_a_param", "L1_attn_W_out",
           "L1_attn_b_out", "L1_ffn_Wg", "L1_ffn_bg",
           "ident_f32", "ident_bf", "ones_col", "ones_row", "ones_row_bfs",
           "iotaE", "trimask", "iotarow", "revcol")
P2_KEYS = ("L1_ffn_W1", "L1_ffn_W2", "ones_row_bf2")


def _input_maps(inputs):
    consts = _consts_np()
    x = np.asarray(inputs["L0_attn_x"])
    base = {}
    for k, v in inputs.items():
        if k == "L0_attn_x":
            continue
        a = np.asarray(v, dtype=np.float32)
        if k in ("L0_attn_decay", "L0_attn_out_scale"):
            a = a.reshape(1, 1)
        if a.ndim == 1:
            a = a.reshape(1, -1)
        base[k] = np.ascontiguousarray(a)
    for k, v in consts.items():
        base[k] = v
    base["ones_row_bf2"] = consts["ones_row_bfs"]
    maps = []
    for b in range(B):
        m = dict(base)
        m["xT"] = np.ascontiguousarray(x[b].T.astype(np.float32))
        maps.append(m)
    return maps


FUSED = True


def kernel(**inputs):
    maps = _input_maps(inputs)
    if FUSED:
        nc = _build("full")
        keys = tuple(set(P1_KEYS) | set(P2_KEYS))
        inm = [{k: m[k] for k in keys} for m in maps]
        rr = run_bass_kernel_spmd(nc, inm, core_ids=list(range(B)))
        outs = []
        for b in range(B):
            xt2 = np.asarray(rr.results[b]["xT2"], dtype=np.float32)
            outs.append(np.ascontiguousarray(xt2.T))
        return np.stack(outs, axis=0)
    nc1 = _build("p1")
    in1 = [{k: m[k] for k in P1_KEYS} for m in maps]
    r1 = run_bass_kernel_spmd(nc1, in1, core_ids=list(range(B)))
    r1m = [{k: np.array(v) for k, v in r.items()} for r in r1.results]
    nc2 = _build("p2")
    in2 = []
    for b in range(B):
        m = {k: maps[b][k] for k in P2_KEYS}
        for nm in ("x1scr", "s2scr", "xn1scr", "ohscr"):
            m[nm] = r1m[b][nm]
        in2.append(m)
    r2 = run_bass_kernel_spmd(nc2, in2, core_ids=list(range(B)))
    outs = []
    for b in range(B):
        xt2 = np.asarray(r2.results[b]["xT2"], dtype=np.float32)
        outs.append(np.ascontiguousarray(xt2.T))
    return np.stack(outs, axis=0)


# version 16
# speedup vs baseline: 1.0253x; 1.0130x over previous
import numpy as np
import ml_dtypes
from contextlib import ExitStack

import concourse.bass as bass
import concourse.mybir as mybir
from concourse.bass_utils import run_bass_kernel_spmd
from concourse.hw_specs import get_activation_tables

F32 = mybir.dt.float32
BF16 = mybir.dt.bfloat16
AF = mybir.ActivationFunctionType
OP = mybir.AluOpType
AX = mybir.AxisListType

B = 8
D = 1024
T = 4096
FC = 8
TC = 512
NCH = T // TC
SUB = 128
NB2 = 64
NEXP = 8
HDIM = 4096
HC = 512
MF = 64
INNER = 16

LND_CLAMP = -85.0

_AF = mybir.ActivationFunctionType
ACT_NEUTRAL = {_AF.Identity, _AF.Copy, _AF.Square, _AF.Relu, _AF.Abs, _AF.Sign, _AF.MemsetZero}
ACT_FUNC_SET = {_AF.Exp: "nlexp", _AF.Ln: "nlexp", _AF.Sigmoid: "sigmoid",
                _AF.Gelu: "gelu", _AF.Tanh: "gelu", _AF.Silu: "silu"}
ACT_SET_NAMES = {"nlexp": "natural_log_exp_and_others", "sigmoid": "sigmoid_and_others",
                 "gelu": "gelu_and_others", "silu": "silu_and_others"}
ACT_SET_REP = {"nlexp": _AF.Exp, "sigmoid": _AF.Sigmoid, "gelu": _AF.Gelu,
               "silu": _AF.Silu}


class Buf:
    __slots__ = ("w", "r", "dsem")

    def __init__(self):
        self.w = {}
        self.r = {}
        self.dsem = None


class Prog:
    ENGS = ("pe", "ve", "se", "po", "sy")

    def __init__(self, nc):
        self.nc = nc
        self.es = ExitStack()
        self.E = {"pe": nc.tensor, "ve": nc.vector, "se": nc.scalar,
                  "po": nc.gpsimd, "sy": nc.sync}
        self.sem = {}
        self.cnt = {}
        for s in ("pe", "ve", "se", "po", "qsy", "qse", "qpo"):
            self.sem[s] = self.es.enter_context(nc.semaphore("sem_" + s))
            self.cnt[s] = 0
        self.waited = {}
        self.engsem = {"pe": "pe", "ve": "ve", "se": "se", "po": "po"}
        self.drained = {"ve": 0, "se": 0, "po": 0}
        self.drain_marks = {"ve": [], "se": [], "po": []}
        self.default_scope = None
        self.extra_dep = None
        self.uid = 0

    def name(self, p):
        self.uid += 1
        return f"{p}_{self.uid}"

    def S(self, p, shape, dt, scope=None):
        side = "left" if scope is None else "right"
        t = (scope or self.default_scope or self.es).enter_context(
            self.nc.sbuf_tensor(self.name(p), shape, dt, side=side))
        return t, Buf()

    def cast_staging(self):
        if not hasattr(self, "_cstg"):
            sc = ExitStack()
            self.es.callback(sc.close)
            stgf, _ = self.S("cstgf", [128, 2 * 512], F32, scope=sc)
            stgb, _ = self.S("cstgb", [128, 2 * 512], BF16, scope=sc)
            self._cstg = (stgf, stgb, [(Buf(), Buf()), (Buf(), Buf())], [0])
            self._cast_pend = None
        return self._cstg

    def cast_tile(self, src_ap, dst_ap, dst_buf, src_buf=None):
        stgf, stgb, slotbufs, slot = self.cast_staging()
        sl = slot[0]
        slot[0] = 1 - slot[0]
        fb, bb = slotbufs[sl]
        fap = stgf[:, sl * 512:(sl + 1) * 512]
        bap = stgb[:, sl * 512:(sl + 1) * 512]
        self.dma("po", fap, src_ap, reads=(src_buf,) if src_buf is not None else (),
                 writes=(fb,))
        pend = self._cast_pend
        self._cast_pend = (fap, bap, fb, bb, dst_ap, dst_buf)
        if pend is not None:
            self._cast_emit(pend)

    def _cast_emit(self, pend):
        fap, bap, fb, bb, dst_ap, dst_buf = pend
        self.op("po", lambda e, bap=bap, fap=fap: e.tensor_copy(bap, fap),
                reads=(fb,), writes=(bb,))
        self.dma("po", dst_ap, bap, reads=(bb,), writes=(dst_buf,))

    def cast_flush(self):
        if getattr(self, "_cast_pend", None) is not None:
            self._cast_emit(self._cast_pend)
            self._cast_pend = None

    def barrier(self):
        snap = dict(self.cnt)
        for e in self.ENGS:
            for s, v in snap.items():
                if v > 0:
                    self._wait(e, s, v)

    def PS(self, p, shape, dt=F32, scope=None):
        t = (scope or self.default_scope or self.es).enter_context(
            self.nc.psum_tensor(self.name(p), shape, dt))
        return t, Buf()

    def DR(self, p, shape, dt, kind="Internal"):
        t = self.nc.dram_tensor(self.name(p) if kind == "Internal" else p, shape, dt, kind=kind)
        return t, Buf()

    CLEAN_OP_EDGES = {("se", "pe"), ("ve", "pe"), ("po", "pe"), ("po", "ve"),
                      ("ve", "se")}

    def _wait(self, eng, semname, val, skip_same=True, need_drain=True, dma=False):
        if skip_same and self.engsem.get(eng) == semname:
            return
        if need_drain and not dma and (semname, eng) in self.CLEAN_OP_EDGES:
            need_drain = False
        if need_drain and semname in self.drain_marks and eng != semname:
            cover = None
            for covered, dcnt in self.drain_marks[semname]:
                if covered >= val:
                    cover = dcnt
                    break
            if cover is None:
                covered = self.cnt[semname]
                dinst = self.E[semname].drain()
                self.cnt[semname] += 1
                dinst.then_inc(self.sem[semname], 1)
                cover = self.cnt[semname]
                self.drain_marks[semname].append((covered, cover))
                self.drained[semname] = cover
            val = cover
        key = (eng, semname)
        if self.waited.get(key, 0) >= val:
            return
        self.E[eng].wait_ge(self.sem[semname], val)
        self.waited[key] = val

    def _deps(self, eng, reads, writes, skip_same=True, need_drain=True, dma=False):
        need = {}
        if self.extra_dep is not None:
            for s, v in self.extra_dep.items():
                if need.get(s, 0) < v:
                    need[s] = v
        for b in reads:
            for s, v in b.w.items():
                if need.get(s, 0) < v:
                    need[s] = v
        for b in writes:
            for s, v in b.w.items():
                if need.get(s, 0) < v:
                    need[s] = v
            for s, v in b.r.items():
                if need.get(s, 0) < v:
                    need[s] = v
        for s, v in need.items():
            self._wait(eng, s, v, skip_same=skip_same, need_drain=need_drain, dma=dma)

    def _finish(self, sname, val, reads, writes):
        for b in reads:
            if b.r.get(sname, 0) < val:
                b.r[sname] = val
        for b in writes:
            if b.w.get(sname, 0) < val:
                b.w[sname] = val
            b.r = {}

    def fence(self, eng):
        covered = self.cnt[eng]
        inst = self.E[eng].drain()
        self.cnt[eng] += 1
        inst.then_inc(self.sem[eng], 1)
        if eng in self.drained:
            self.drained[eng] = self.cnt[eng]
            self.drain_marks[eng].append((covered, self.cnt[eng]))
        return (eng, self.cnt[eng])

    def op(self, eng, fn, reads=(), writes=()):
        self._deps(eng, reads, writes, skip_same=(eng == "pe"),
                   need_drain=(eng != "pe"))
        inst = fn(self.E[eng])
        self.cnt[eng] += 1
        inst.then_inc(self.sem[eng], 1)
        self._finish(eng, self.cnt[eng], reads, writes)
        return (eng, self.cnt[eng])

    def dma(self, q, out, in_, reads=(), writes=(), **kw):
        self._deps(q, reads, writes, skip_same=False, dma=True)
        inst = self.E[q].dma_start(out=out, in_=in_, **kw)
        db = writes[0] if writes else None
        if db is not None:
            if db.dsem is None:
                nm = self.name("dsem")
                self.sem[nm] = self.es.enter_context(self.nc.semaphore(nm))
                self.cnt[nm] = 0
                db.dsem = nm
            qs = db.dsem
        else:
            qs = "q" + q
        self.cnt[qs] += 16
        inst.then_inc(self.sem[qs], 16)
        self._finish(qs, self.cnt[qs], reads, writes)
        return (qs, self.cnt[qs])

    def setup_act_guard(self):
        tabs = list(get_activation_tables(self.nc.m.arch).keys())
        self.set_ids = {k: tabs.index(v) for k, v in ACT_SET_NAMES.items()}
        self.cur_set = None

    def ensure_set(self, s):
        if self.cur_set == s:
            return
        self.cur_set = s
        self.E["se"].add_instruction(mybir.InstLoadActFuncSet(
            name=self.nc.get_next_instruction_name(),
            act_func_set_id=self.set_ids[s], ins=[], outs=[]))

    def act(self, fn_func, out, in_, bias=0.0, scale=1.0, reads=(), writes=()):
        s = ACT_FUNC_SET.get(fn_func)
        if s is not None:
            self.ensure_set(s)
        return self.op("se", lambda e: e.activation(out, in_, fn_func, bias=bias, scale=scale),
                       reads=reads, writes=writes)

    def dump(self, name, ap, reads, shape=None, dt=None):
        t = self.nc.dram_tensor("dbg_" + name, shape or list(ap.shape), dt or ap.dtype, kind="ExternalOutput")
        self.dma("po", t.ap()[:, :], ap, reads=reads, writes=(Buf(),))

    def final_wait(self):
        snap = dict(self.cnt)
        for s, v in snap.items():
            if v > 0:
                self._wait("sy", s, v)


def _consts_np():
    i = np.arange(128)
    bf = ml_dtypes.bfloat16
    c = {}
    c["ident_f32"] = np.eye(128, dtype=np.float32)
    c["ident_bf"] = np.eye(128, dtype=np.float32).astype(bf)
    c["ones_col"] = np.ones((128, 1), dtype=np.float32)
    c["ones_row"] = np.ones((1, 128), dtype=np.float32)
    c["ones_row_bfs"] = np.ones((1, 128), dtype=np.float32).astype(bf)
    c["iotaE"] = (i[None, :] - i[:, None] - 1).astype(np.float32)
    c["trimask"] = (i[None, :] > i[:, None]).astype(np.float32)
    c["iotarow"] = i[None, :].astype(np.float32)
    c["revcol"] = (127 - i)[:, None].astype(np.float32)
    return c


CONST_SPECS = (("ident_f32", [128, 128], F32), ("ident_bf", [128, 128], BF16),
               ("ones_col", [128, 1], F32), ("ones_row", [1, 128], F32),
               ("ones_row_bfs", [1, 128], BF16), ("iotaE", [128, 128], F32),
               ("trimask", [128, 128], F32), ("iotarow", [1, 128], F32),
               ("revcol", [128, 1], F32))


DBG = False
STAGE = 99


def build_p1(nc, p, dram):

    def ext_in(nm, shape, dt=F32):
        return nc.dram_tensor(nm, shape, dt, kind="ExternalInput").ap(), Buf()

    xT, xT_b = ext_in("xT", [D, T])
    basis, basis_b = ext_in("L0_attn_basis", [D, NB2])
    coef_in = {}
    for nm in ("q", "k", "v", "o"):
        coef_in[nm] = ext_in(f"L0_attn_{nm}_coeffs", [128, NB2])
    decay, decay_b = ext_in("L0_attn_decay", [1, 1])
    oscale, oscale_b = ext_in("L0_attn_out_scale", [1, 1])
    Win, Win_b = ext_in("L0_ffn_Win", [D, MF])
    binv, binv_b = ext_in("L0_ffn_bin_", [1, MF])
    W1f, W1f_b = ext_in("L0_ffn_W1", [MF, INNER])
    b1f, b1f_b = ext_in("L0_ffn_b1", [MF, INNER])
    W2f, W2f_b = ext_in("L0_ffn_W2", [MF, INNER])
    b2f, b2f_b = ext_in("L0_ffn_b2", [1, MF])
    Wout, Wout_b = ext_in("L0_ffn_Wout", [MF, D])
    boutv, boutv_b = ext_in("L0_ffn_bout", [1, D])
    l1w = {}
    for nm in ("W_in", "W_gx", "W_ga", "W_out"):
        l1w[nm] = ext_in("L1_attn_" + nm, [D, D])
    l1b = {}
    for nm in ("b_in", "b_gx", "b_ga", "b_out", "a_param"):
        l1b[nm] = ext_in("L1_attn_" + nm, [1, D])
    Wg, Wg_b = ext_in("L1_ffn_Wg", [D, NEXP])
    bg, bg_b = ext_in("L1_ffn_bg", [1, NEXP])
    cst = {}
    for nm, shape, dt_ in CONST_SPECS:
        cst[nm] = ext_in(nm, shape, dt_)

    x1scr, x1scr_b = dram["x1scr"]
    s2scr, s2scr_b = dram["s2scr"]
    xn1scr, xn1scr_b = dram["xn1scr"]
    ohscr, ohscr_b = dram["ohscr"]
    x1ap, s2ap, xn1ap, ohap = x1scr.ap(), s2scr.ap(), xn1scr.ap(), ohscr.ap()

    l1wbf = {}
    l1wbf_bufs = {}
    for nm in ("W_in", "W_gx", "W_ga", "W_out"):
        t_, b_ = p.DR("l1wbfT_" + nm, [128, FC * D], BF16)
        l1wbf[nm] = t_
        l1wbf_bufs[nm] = b_
        sap, sbuf = l1w[nm]
        for ic in range(FC):
            for cc in range(2):
                p.cast_tile(sap[ic * 128:(ic + 1) * 128, cc * 512:(cc + 1) * 512],
                            t_.ap()[:, ic * D + cc * 512: ic * D + (cc + 1) * 512],
                            b_, src_buf=sbuf)
    p.cast_flush()

    PBt = [p.PS(f"pb{i}", [128, 512]) for i in range(6)]
    PTb, PTb_b = p.PS("ptb", [128, 1024], BF16)
    PSm, PSm_b = p.PS("psm", [128, 512])

    pre_es = ExitStack()

    def Sld(nm, shape, dt, src, src_b, q="sy", scope=None):
        t, b = p.S(nm, shape, dt, scope=scope)
        p.dma(q, t[:, :], src, reads=(src_b,), writes=(b,))
        return t, b

    idf, idf_b = Sld("idf", [128, 128], F32, cst["ident_f32"][0][:, :], cst["ident_f32"][1])
    idb, idb_b = Sld("idb", [128, 128], BF16, cst["ident_bf"][0][:, :], cst["ident_bf"][1], scope=pre_es)
    onc, onc_b = Sld("onc", [128, 1], F32, cst["ones_col"][0][:, :], cst["ones_col"][1])
    onr, onr_b = Sld("onr", [1, 128], F32, cst["ones_row"][0][:, :], cst["ones_row"][1])
    iotaE, iotaE_b = Sld("iotaE", [128, 128], F32, cst["iotaE"][0][:, :], cst["iotaE"][1], scope=pre_es)
    trim, trim_b = Sld("trim", [128, 128], F32, cst["trimask"][0][:, :], cst["trimask"][1], scope=pre_es)
    iotar, iotar_b = Sld("iotar", [1, 128], F32, cst["iotarow"][0][:, :], cst["iotarow"][1], scope=pre_es)
    revc, revc_b = Sld("revc", [128, 1], F32, cst["revcol"][0][:, :], cst["revcol"][1], scope=pre_es)

    bas, bas_b = p.S("bas", [128, FC * NB2], F32, scope=pre_es)
    for ic in range(FC):
        p.dma("sy", bas[:, ic * NB2:(ic + 1) * NB2], basis[ic * 128:(ic + 1) * 128, :],
              reads=(basis_b,), writes=(bas_b,))
    coef = {}
    for nm in ("q", "k", "v", "o"):
        src, srcb = coef_in[nm]
        coef[nm] = Sld("coef" + nm, [128, NB2], F32, src[:, :], srcb, scope=pre_es)
    dscal, dscal_b = Sld("dscal", [1, 1], F32, decay[:, :], decay_b, scope=pre_es)
    osc, osc_b = Sld("osc", [1, 1], F32, oscale[:, :], oscale_b, scope=pre_es)
    winf, winf_b = p.S("winf", [128, FC * MF], F32, scope=pre_es)
    for ic in range(FC):
        p.dma("sy", winf[:, ic * MF:(ic + 1) * MF], Win[ic * 128:(ic + 1) * 128, :],
              reads=(Win_b,), writes=(winf_b,))
    woutf, woutf_b = Sld("woutf", [MF, D], F32, Wout[:, :], Wout_b, scope=pre_es)
    w1c, w1c_b = Sld("w1c", [MF, INNER], F32, W1f[:, :], W1f_b)
    b1c, b1c_b = Sld("b1c", [MF, INNER], F32, b1f[:, :], b1f_b)
    w2c, w2c_b = Sld("w2c", [MF, INNER], F32, W2f[:, :], W2f_b)
    binr, binr_b = Sld("binr", [1, MF], F32, binv[:, :], binv_b, scope=pre_es)
    b2r, b2r_b = Sld("b2r", [1, MF], F32, b2f[:, :], b2f_b, scope=pre_es)
    boutr, boutr_b = Sld("boutr", [1, D], F32, boutv[:, :], boutv_b, scope=pre_es)
    l1br = {}
    for nm in ("b_in", "b_gx", "b_ga", "b_out", "a_param"):
        l1br[nm] = Sld("r" + nm, [1, D], F32, l1b[nm][0][:, :], l1b[nm][1], scope=pre_es)
    wgt, wgt_b = p.S("wgt", [128, FC * NEXP], F32)
    p.dma("sy", wgt[:, :].rearrange("p (ic e) -> p ic e", ic=FC),
          Wg.rearrange("(ic p) e -> p ic e", p=128), reads=(Wg_b,), writes=(wgt_b,))
    bgr, bgr_b = Sld("bgr", [1, NEXP], F32, bg[:, :], bg_b, scope=pre_es)

    def row_to_cols(row_t, row_b, ncols, name, scope=None):
        colt, colb = p.S(name, [128, max(1, ncols // 128)], F32, scope=scope)
        for j in range(max(1, ncols // 128)):
            w = min(128, ncols - j * 128)
            p.op("pe", lambda e, j=j, w=w: e.transpose(PBt[4][0][0:w, 0:1],
                                                       row_t[0:1, j * 128:j * 128 + w], idf[0:1, 0:1]),
                 reads=(row_b, idf_b), writes=(PBt[4][1],))
            p.op("ve", lambda e, j=j, w=w: e.tensor_copy(colt[0:w, j:j + 1], PBt[4][0][0:w, 0:1]),
                 reads=(PBt[4][1],), writes=(colb,))
        return colt, colb

    binc, binc_b = row_to_cols(binr, binr_b, MF, "binc")
    b2cc, b2cc_b = row_to_cols(b2r, b2r_b, MF, "b2cc")
    boutT, boutT_b = row_to_cols(boutr, boutr_b, D, "boutT")
    biasT = {}
    for nm in ("b_in", "b_gx", "b_ga", "b_out"):
        rt_, rb_ = l1br[nm]
        biasT[nm] = row_to_cols(rt_, rb_, D, "bT" + nm)
    apT, apT_b = row_to_cols(l1br["a_param"][0], l1br["a_param"][1], D, "apT", scope=pre_es)
    bgc, bgc_b = row_to_cols(bgr, bgr_b, NEXP, "bgc")

    sp8, sp8_b = p.S("sp8", [128, FC], F32)
    tpb, tpb_b = p.S("pre_t1", [128, FC], F32, scope=pre_es)
    tpc, tpc_b = p.S("pre_t2", [128, FC], F32, scope=pre_es)
    p.act(AF.Abs, PBt[4][0][0:128, 0:FC], apT[:, :], reads=(apT_b,), writes=(PBt[4][1],))
    p.act(AF.Exp, PBt[5][0][0:128, 0:FC], PBt[4][0][0:128, 0:FC], scale=-1.0,
          reads=(PBt[4][1],), writes=(PBt[5][1],))
    p.act(AF.Ln, PBt[4][0][0:128, 0:FC], PBt[5][0][0:128, 0:FC], bias=1.0,
          reads=(PBt[5][1],), writes=(PBt[4][1],))
    p.op("ve", lambda e: e.tensor_scalar_max(tpb[:, :], apT[:, :], 0.0), reads=(apT_b,), writes=(tpb_b,))
    p.fence("ve")
    p.op("ve", lambda e: e.tensor_tensor(tpc[:, :], tpb[:, :], PBt[4][0][0:128, 0:FC], OP.add),
         reads=(tpb_b, PBt[4][1]), writes=(tpc_b,))
    p.fence("ve")
    p.op("ve", lambda e: e.tensor_scalar_mul(sp8[:, :], tpc[:, :], -8.0), reads=(tpc_b,), writes=(sp8_b,))
    p.fence("ve")

    dsig, dsig_b = p.S("dsig", [1, 2], F32, scope=pre_es)
    p.act(AF.Exp, PBt[4][0][0:1, 0:1], dscal[:, :], scale=-1.0, reads=(dscal_b,), writes=(PBt[4][1],))
    p.op("ve", lambda e: e.tensor_scalar_add(dsig[:, 0:1], PBt[4][0][0:1, 0:1], 1.0),
         reads=(PBt[4][1],), writes=(dsig_b,))
    p.fence("ve")
    p.op("ve", lambda e: e.reciprocal(dsig[:, 1:2], dsig[:, 0:1]), reads=(dsig_b,), writes=(dsig_b,))
    p.fence("ve")
    lnd, lnd_b = p.S("lnd", [1, 1], F32, scope=pre_es)
    p.act(AF.Ln, PBt[4][0][0:1, 0:1], dsig[:, 1:2], reads=(dsig_b,), writes=(PBt[4][1],))
    p.op("ve", lambda e: e.tensor_scalar_max(lnd[:, :], PBt[4][0][0:1, 0:1], LND_CLAMP),
         reads=(PBt[4][1],), writes=(lnd_b,))
    p.fence("ve")
    lndc, lndc_b = p.S("lndc", [128, 1], F32, scope=pre_es)
    p.op("pe", lambda e: e.matmul(PBt[4][0][0:128, 0:1], onr[0:1, :], lnd[0:1, 0:1],
                                  start=True, stop=True), reads=(onr_b, lnd_b), writes=(PBt[4][1],))
    p.op("ve", lambda e: e.tensor_copy(lndc[:, 0:1], PBt[4][0][0:128, 0:1]),
         reads=(PBt[4][1],), writes=(lndc_b,))
    p.fence("ve")
    Dmask, Dmask_b = p.S("Dmask", [128, 128], F32)
    p.op("ve", lambda e: e.tensor_scalar_mul(Dmask[:, :], iotaE[:, :], lndc[:, 0:1]),
         reads=(iotaE_b, lndc_b), writes=(Dmask_b,))
    p.act(AF.Exp, PBt[4][0][:, 0:128], Dmask[:, :], reads=(Dmask_b,), writes=(PBt[4][1],))
    p.op("ve", lambda e: e.tensor_tensor(Dmask[:, :], PBt[4][0][:, 0:128], trim[:, :], OP.mult),
         reads=(PBt[4][1], trim_b), writes=(Dmask_b,))
    drow, drow_b = p.S("drow", [1, 128], F32, scope=pre_es)
    p.op("ve", lambda e: e.tensor_scalar_mul(drow[:, :], iotar[:, :], lnd[0:1, 0:1]),
         reads=(iotar_b, lnd_b), writes=(drow_b,))
    p.act(AF.Exp, PBt[5][0][0:1, 0:128], drow[:, :], reads=(drow_b,), writes=(PBt[5][1],))
    p.op("ve", lambda e: e.tensor_copy(drow[:, :], PBt[5][0][0:1, 0:128]),
         reads=(PBt[5][1],), writes=(drow_b,))
    dprow, dprow_b = p.S("dprow", [128, 128], F32)
    p.op("pe", lambda e: e.matmul(PBt[4][0][0:128, 0:128], onr[0:1, :], drow[0:1, :],
                                  start=True, stop=True), reads=(onr_b, drow_b), writes=(PBt[4][1],))
    p.op("ve", lambda e: e.tensor_copy(dprow[:, :], PBt[4][0][0:128, 0:128]),
         reads=(PBt[4][1],), writes=(dprow_b,))
    dpcol, dpcol_b = p.S("dpcol", [128, 1], F32)
    p.op("ve", lambda e: e.tensor_scalar_mul(dpcol[:, :], revc[:, :], lndc[:, 0:1]),
         reads=(revc_b, lndc_b), writes=(dpcol_b,))
    p.act(AF.Exp, PBt[5][0][0:128, 0:1], dpcol[:, :], reads=(dpcol_b,), writes=(PBt[5][1],))
    p.op("ve", lambda e: e.tensor_copy(dpcol[:, :], PBt[5][0][0:128, 0:1]),
         reads=(PBt[5][1],), writes=(dpcol_b,))
    dTc, dTc_b = p.S("dTc", [128, 1], F32)
    p.op("ve", lambda e: e.tensor_scalar_mul(dTc[:, :], lndc[:, 0:1], 128.0),
         reads=(lndc_b,), writes=(dTc_b,))
    p.act(AF.Exp, PBt[5][0][0:128, 0:1], dTc[:, :], reads=(dTc_b,), writes=(PBt[5][1],))
    p.op("ve", lambda e: e.tensor_copy(dTc[:, :], PBt[5][0][0:128, 0:1]),
         reads=(PBt[5][1],), writes=(dTc_b,))

    basT, basT_b = p.S("basT", [NB2, D], F32, scope=pre_es)
    for ic in range(FC):
        p.op("pe", lambda e, ic=ic: e.transpose(PBt[5][0][0:NB2, 0:128],
                                                bas[:, ic * NB2:(ic + 1) * NB2], idf[:, :]),
             reads=(bas_b, idf_b), writes=(PBt[5][1],))
        p.op("ve", lambda e, ic=ic: e.tensor_copy(basT[:, ic * 128:(ic + 1) * 128], PBt[5][0][0:NB2, 0:128]),
             reads=(PBt[5][1],), writes=(basT_b,))

    wnat = {}
    owT = owT_b = None
    for nm in ("q", "k", "v", "o"):
        ct, cb = coef[nm]
        cT, cTb = p.S("cT" + nm, [NB2, 128], F32, scope=pre_es)
        p.op("pe", lambda e, ct=ct: e.transpose(PBt[5][0][0:NB2, 0:128], ct[:, :], idf[:, :]),
             reads=(cb, idf_b), writes=(PBt[5][1],))
        p.op("ve", lambda e, cT=cT: e.tensor_copy(cT[:, :], PBt[5][0][0:NB2, 0:128]),
             reads=(PBt[5][1],), writes=(cTb,))
        wT, wTb = p.S("wT" + nm, [128, D], F32, scope=pre_es)
        for hh in range(2):
            p.op("pe", lambda e, hh=hh, cT=cT: e.matmul(PBt[hh][0][:, :], cT[:, :],
                                                        basT[0:NB2, hh * 512:(hh + 1) * 512],
                                                        start=True, stop=True),
                 reads=(cTb, basT_b), writes=(PBt[hh][1],))
            if nm == "o":
                p.op("ve", lambda e, hh=hh, wT=wT: e.tensor_copy(wT[:, hh * 512:(hh + 1) * 512],
                                                                 PBt[hh][0][:, :]),
                     reads=(PBt[hh][1],), writes=(wTb,))
        if nm == "o":
            oscb, oscb_b = p.S("oscb", [128, 1], F32, scope=pre_es)
            p.op("pe", lambda e: e.matmul(PBt[4][0][0:128, 0:1], onr[0:1, :], osc[0:1, 0:1],
                                          start=True, stop=True), reads=(onr_b, osc_b), writes=(PBt[4][1],))
            p.op("ve", lambda e: e.tensor_copy(oscb[:, 0:1], PBt[4][0][0:128, 0:1]),
                 reads=(PBt[4][1],), writes=(oscb_b,))
            owT, owT_b = p.S("owT", [128, D], BF16)
            p.op("se", lambda e: e.activation(owT[:, :], wT[:, :], AF.Identity, scale=oscb[:, 0:1]),
                 reads=(wTb, oscb_b), writes=(owT_b,))
        else:
            mx, mx_b = p.S("mxw", [128, 4], F32, scope=pre_es)
            p.op("ve", lambda e: e.reduce_max(mx[:, 0:1], PBt[0][0][:, :], axis=AX.X),
                 reads=(PBt[0][1],), writes=(mx_b,))
            p.op("ve", lambda e: e.reduce_max(mx[:, 1:2], PBt[1][0][:, :], axis=AX.X),
                 reads=(PBt[1][1],), writes=(mx_b,))
            p.fence("ve")
            p.op("ve", lambda e: e.tensor_max(mx[:, 2:3], mx[:, 0:1], mx[:, 1:2]), reads=(mx_b,), writes=(mx_b,))
            p.op("se", lambda e: e.mul(mx[:, 3:4], mx[:, 2:3], -1.0), reads=(mx_b,), writes=(mx_b,))
            if DBG and nm == "q":
                p.dump("nmx", mx[:, 3:4], (mx_b,))
            for hh in range(2):
                p.act(AF.Exp, PBt[2 + hh][0][:, :], PBt[hh][0][:, :], bias=mx[:, 3:4],
                      reads=(PBt[hh][1], mx_b), writes=(PBt[2 + hh][1],))
            sm, sm_b = p.S("smw", [128, 4], F32, scope=pre_es)
            p.op("ve", lambda e: e.reduce_sum(sm[:, 0:1], PBt[2][0][:, :], axis=AX.X),
                 reads=(PBt[2][1],), writes=(sm_b,))
            p.op("ve", lambda e: e.reduce_sum(sm[:, 1:2], PBt[3][0][:, :], axis=AX.X),
                 reads=(PBt[3][1],), writes=(sm_b,))
            p.fence("ve")
            p.op("ve", lambda e: e.tensor_add(sm[:, 2:3], sm[:, 0:1], sm[:, 1:2]), reads=(sm_b,), writes=(sm_b,))
            p.fence("ve")
            p.op("ve", lambda e: e.reciprocal(sm[:, 3:4], sm[:, 2:3]), reads=(sm_b,), writes=(sm_b,))
            p.fence("ve")
            if DBG and nm == "q":
                p.dump("smq", sm[:, :], (sm_b,))
                p.dump("mxq", mx[:, :], (mx_b,))
            wTb16, wTb16_b = p.S("wTb" + nm, [128, D], BF16, scope=pre_es)
            for hh in range(2):
                p.op("ve", lambda e, hh=hh: e.tensor_scalar_mul(wTb16[:, hh * 512:(hh + 1) * 512],
                                                                PBt[2 + hh][0][:, :], sm[:, 3:4]),
                     reads=(PBt[2 + hh][1], sm_b), writes=(wTb16_b,))
            nat, nat_b = p.S("wnat" + nm, [128, FC * 128], BF16)
            for ic in range(FC):
                p.op("pe", lambda e, ic=ic: e.transpose(PTb[0:128, 0:128],
                                                        wTb16[:, ic * 128:(ic + 1) * 128], idb[:, :]),
                     reads=(wTb16_b, idb_b), writes=(PTb_b,))
                p.op("ve", lambda e, ic=ic: e.tensor_copy(nat[:, ic * 128:(ic + 1) * 128], PTb[0:128, 0:128]),
                     reads=(PTb_b,), writes=(nat_b,))
            wnat[nm] = (nat, nat_b)
            if DBG and nm == "q":
                p.dump("basT", basT[:, :], (basT_b,))
                p.dump("cTq", cT[:, :], (cTb,))
                p.dump("wTq", wT[:, :], (wTb,))
                p.dump("wTb16q", wTb16[:, :], (wTb16_b,))
                p.dump("wnatq", nat[:, :], (nat_b,))

    winb, winb_b = p.S("winb", [128, FC * MF], BF16)
    p.op("ve", lambda e: e.tensor_copy(winb[:, :], winf[:, :]), reads=(winf_b,), writes=(winb_b,))
    woutb, woutb_b = p.S("woutb", [MF, D], BF16)
    p.op("ve", lambda e: e.tensor_copy(woutb[:, :], woutf[:, :]), reads=(woutf_b,), writes=(woutb_b,))

    pre_es.close()
    p.barrier()

    Mst, Mst_b = p.S("Mst", [128, 128], F32)
    Mbf, Mbf_b = p.S("Mbf", [128, 128], BF16)
    p.op("ve", lambda e: e.memset(Mst[:, :], 0.0), writes=(Mst_b,))
    p.op("ve", lambda e: e.memset(Mbf[:, :], 0.0), writes=(Mbf_b,))
    hcar, hcar_b = p.S("hcar", [128, FC], F32)
    p.op("ve", lambda e: e.memset(hcar[:, :], 0.0), writes=(hcar_b,))

    xab = [p.S("xa", [128, FC * TC], F32)]
    x1buf, x1buf_b = p.S("x1buf", [128, FC * TC], F32)
    xn, xn_b = p.S("xn", [128, FC * TC], BF16)
    s0, s0_b = p.S("s0", [128, FC * TC], F32)
    qT, qT_b = p.S("qT", [128, TC], BF16)
    kT, kT_b = p.S("kT", [128, TC], BF16)
    ktok, ktok_b = p.S("ktok", [128, TC], BF16)
    vtok, vtok_b = p.S("vtok", [128, TC], BF16)
    qtil, qtil_b = p.S("qtil", [128, 128], BF16)
    ktil, ktil_b = p.S("ktil", [128, 128], BF16)
    SpT, SpT_b = p.S("SpT", [128, 128], BF16)
    ret, ret_b = p.S("ret", [128, TC], BF16)
    Mtmp, Mtmp_b = p.S("Mtmp", [128, 128], F32)
    sF, sF_b = p.S("sF", [MF, TC], F32)
    gaccs = [p.S(f"gacc{i}", [MF, TC], F32) for i in range(4)]
    gbf, gbf_b = p.S("gbf", [MF, TC], BF16)
    xi_bf, xi_bf_b = p.S("xibf", [128, FC * TC], BF16)
    pool16a = p.S("p16a", [128, FC * TC], F32)
    pool16b = p.S("p16b", [128, FC * TC], F32)
    h_bf, h_bf_b = p.S("hbf", [128, FC * TC], BF16)
    tp = [p.S(f"tp{i}", [128, TC], F32) for i in range(6)]
    tpi = [0]

    def TP():
        t = tp[tpi[0] % len(tp)]
        tpi[0] += 1
        return t

    rows_stats = p.S("rstat", [1, 4 * TC], F32)
    lg, lg_b = p.S("lg", [NEXP, TC], F32)
    ohb, ohb_b = p.S("ohb", [NEXP, TC], BF16)
    ltok, ltok_b = p.S("ltok", [128, 4 * NEXP], F32)
    mxc, mxc_b = p.S("mxc", [128, 4], F32)
    mxr, mxr_b = p.S("mxr", [1, TC], F32)
    wstr = [p.S("wstrA", [128, FC * D], BF16), p.S("wstrB", [128, FC * D], BF16)]

    def ln_apply(xa, xa_b, out_t, out_b, also_f32_cb=None):
        rs, rs_b = rows_stats
        for ic in range(FC):
            sq = TP()
            p.op("se", lambda e, ic=ic, sq=sq: e.square(sq[0][:, :], xa[:, ic * TC:(ic + 1) * TC]),
                 reads=(xa_b,), writes=(sq[1],))
            p.op("pe", lambda e, ic=ic: e.matmul(PBt[2][0][0:1, :], onc[:, 0:1], xa[:, ic * TC:(ic + 1) * TC],
                                                 start=(ic == 0), stop=(ic == FC - 1)),
                 reads=(xa_b, onc_b), writes=(PBt[2][1],))
            p.op("pe", lambda e, ic=ic, sq=sq: e.matmul(PBt[3][0][0:1, :], onc[:, 0:1], sq[0][:, :],
                                                        start=(ic == 0), stop=(ic == FC - 1)),
                 reads=(sq[1], onc_b), writes=(PBt[3][1],))
        p.op("ve", lambda e: e.tensor_scalar_mul(rs[0:1, 0:TC], PBt[2][0][0:1, :], 1.0 / D),
             reads=(PBt[2][1],), writes=(rs_b,))
        p.op("ve", lambda e: e.tensor_scalar_mul(rs[0:1, TC:2 * TC], PBt[3][0][0:1, :], 1.0 / D),
             reads=(PBt[3][1],), writes=(rs_b,))
        p.op("ve", lambda e: e.tensor_tensor(rs[0:1, 2 * TC:3 * TC], rs[0:1, 0:TC], rs[0:1, 0:TC], OP.mult),
             reads=(rs_b,), writes=(rs_b,))
        p.op("ve", lambda e: e.tensor_sub(rs[0:1, 3 * TC:4 * TC], rs[0:1, TC:2 * TC], rs[0:1, 2 * TC:3 * TC]),
             reads=(rs_b,), writes=(rs_b,))
        p.op("ve", lambda e: e.tensor_scalar_add(rs[0:1, 2 * TC:3 * TC], rs[0:1, 3 * TC:4 * TC], 1e-5),
             reads=(rs_b,), writes=(rs_b,))
        p.act(AF.Ln, PBt[2][0][0:1, :], rs[0:1, 2 * TC:3 * TC], reads=(rs_b,), writes=(PBt[2][1],))
        p.act(AF.Exp, PBt[3][0][0:1, :], PBt[2][0][0:1, :], scale=-0.5,
              reads=(PBt[2][1],), writes=(PBt[3][1],))
        p.op("ve", lambda e: e.tensor_copy(rs[0:1, 3 * TC:4 * TC], PBt[3][0][0:1, :]),
             reads=(PBt[3][1],), writes=(rs_b,))
        p.op("pe", lambda e: e.matmul(PBt[2][0][:, :], onr[0:1, :], rs[0:1, 0:TC],
                                      start=True, stop=True), reads=(rs_b, onr_b), writes=(PBt[2][1],))
        p.op("pe", lambda e: e.matmul(PBt[3][0][:, :], onr[0:1, :], rs[0:1, 3 * TC:4 * TC],
                                      start=True, stop=True), reads=(rs_b, onr_b), writes=(PBt[3][1],))
        for ic in range(FC):
            t1 = TP()
            p.op("ve", lambda e, ic=ic, t1=t1: e.tensor_tensor(t1[0][:, :], xa[:, ic * TC:(ic + 1) * TC],
                                                               PBt[2][0][:, :], OP.subtract),
                 reads=(xa_b, PBt[2][1]), writes=(t1[1],))
            if also_f32_cb is None:
                p.op("ve", lambda e, ic=ic, t1=t1: e.tensor_tensor(out_t[:, ic * TC:(ic + 1) * TC],
                                                                   t1[0][:, :], PBt[3][0][:, :], OP.mult),
                     reads=(t1[1], PBt[3][1]), writes=(out_b,))
            else:
                tf = TP()
                p.op("ve", lambda e, t1=t1, tf=tf: e.tensor_tensor(tf[0][:, :], t1[0][:, :],
                                                                   PBt[3][0][:, :], OP.mult),
                     reads=(t1[1], PBt[3][1]), writes=(tf[1],))
                p.op("ve", lambda e, ic=ic, tf=tf: e.tensor_copy(out_t[:, ic * TC:(ic + 1) * TC], tf[0][:, :]),
                     reads=(tf[1],), writes=(out_b,))
                also_f32_cb(ic, tf)

    def load_x(c, slot):
        xt, xb = xab[slot]
        p.dma("sy", xt[:, :].rearrange("p (fc t) -> p fc t", fc=FC),
              xT.rearrange("(fc p) t -> p fc t", p=128)[:, :, c * TC:(c + 1) * TC],
              reads=(xT_b,), writes=(xb,))

    def load_w(nm, slot):
        wt, wb = wstr[slot]
        p.dma("sy", wt[:, :], l1wbf[nm].ap()[:, :],
              reads=(l1wbf_bufs[nm],), writes=(wb,))
        return wt, wb

    xa, xa_b = xab[0]

    def emit_ln0():
        ln_apply(xa, xa_b, xn, xn_b)

    def emit_qkv():
        for dst, dstb, nm, bk in ((qT, qT_b, "q", 0), (kT, kT_b, "k", 1)):
            nat, nat_b = wnat[nm]
            for ic in range(FC):
                p.op("pe", lambda e, ic=ic, nat=nat, bk=bk: e.matmul(
                    PBt[bk][0][:, :], nat[:, ic * 128:(ic + 1) * 128],
                    xn[:, ic * TC:(ic + 1) * TC],
                    start=(ic == 0), stop=(ic == FC - 1)),
                     reads=(nat_b, xn_b), writes=(PBt[bk][1],))
            p.op("se", lambda e, dst=dst, bk=bk: e.copy(dst[:, :], PBt[bk][0][:, :]),
                 reads=(PBt[bk][1],), writes=(dstb,))
        for dst, dstb, nm in ((ktok, ktok_b, "k"), (vtok, vtok_b, "v")):
            nat, nat_b = wnat[nm]
            for s in range(TC // SUB):
                pb = PBt[s % 2]
                for ic in range(FC):
                    p.op("pe", lambda e, s=s, ic=ic, nat=nat, pb=pb: e.matmul(
                        pb[0][:, 0:128], xn[:, ic * TC + s * SUB: ic * TC + (s + 1) * SUB],
                        nat[:, ic * 128:(ic + 1) * 128],
                        start=(ic == 0), stop=(ic == FC - 1)),
                        reads=(nat_b, xn_b), writes=(pb[1],))
                p.op("se", lambda e, s=s, dst=dst, pb=pb: e.copy(dst[:, s * SUB:(s + 1) * SUB],
                                                                 pb[0][:, 0:128]),
                     reads=(pb[1],), writes=(dstb,))

    load_x(0, 0)
    emit_ln0()
    emit_qkv()
    for c in range(NCH):
        if DBG and c == 0:
            p.dump("xa0", xa[:, :], (xa_b,))
            p.dump("xn0", xn[:, :], (xn_b,))
        if STAGE < 2:
            continue
        if DBG and c == 0:
            p.dump("qT0", qT[:, :], (qT_b,))
            p.dump("kT0", kT[:, :], (kT_b,))
            p.dump("ktok0", ktok[:, :], (ktok_b,))
            p.dump("vtok0", vtok[:, :], (vtok_b,))
        if STAGE < 5:
            continue
        for ic in range(FC):
            p.op("pe", lambda e, ic=ic: e.matmul(PBt[0][0][0:MF, :], winb[:, ic * MF:(ic + 1) * MF],
                                                 xn[:, ic * TC:(ic + 1) * TC],
                                                 start=(ic == 0), stop=(ic == FC - 1)),
                 reads=(winb_b, xn_b), writes=(PBt[0][1],))
        p.op("se", lambda e: e.activation(sF[:, :], PBt[0][0][0:MF, :], AF.Identity, bias=binc[0:MF, 0:1]),
             reads=(PBt[0][1], binc_b), writes=(sF_b,))
        if STAGE < 4.2:
            continue
        for h in range(INNER):
            rp = PBt[2 + h % 2]
            par = h % 2
            k = h // 2
            ga_o, ga_ob = gaccs[(k % 2) * 2 + par]
            ga_i, ga_ib = gaccs[((k + 1) % 2) * 2 + par]
            p.act(AF.Relu, rp[0][0:MF, :], sF[:, :], bias=b1c[:, h:h + 1], scale=w1c[:, h:h + 1],
                  reads=(sF_b, b1c_b, w1c_b), writes=(rp[1],))
            if h < 2:
                p.op("ve", lambda e, h=h, rp=rp, ga_o=ga_o: e.tensor_scalar_mul(ga_o[:, :], rp[0][0:MF, :],
                                                                                w2c[:, h:h + 1]),
                     reads=(rp[1], w2c_b), writes=(ga_ob,))
            else:
                p.op("ve", lambda e, h=h, rp=rp, ga_o=ga_o, ga_i=ga_i: e.scalar_tensor_tensor(
                    ga_o[:, :], rp[0][0:MF, :], w2c[:, h:h + 1], ga_i[:, :], OP.mult, OP.add),
                     reads=(rp[1], w2c_b, ga_ib), writes=(ga_ob,))
        if STAGE < 4.3:
            continue
        ke = (INNER - 2) // 2
        ko = (INNER - 1) // 2
        geven, geven_b = gaccs[(ke % 2) * 2]
        godd, godd_b = gaccs[(ko % 2) * 2 + 1]
        gfin, gfin_b = gaccs[((ke + 1) % 2) * 2]
        p.op("ve", lambda e: e.tensor_tensor(gfin[:, :], geven[:, :], godd[:, :], OP.add),
             reads=(geven_b, godd_b), writes=(gfin_b,))
        p.act(AF.Identity, gbf[:, :], gfin[:, :], bias=b2cc[0:MF, 0:1],
              reads=(gfin_b, b2cc_b), writes=(gbf_b,))
        if STAGE < 3:
            continue
        for s in range(TC // SUB):
            sl = slice(s * SUB, (s + 1) * SUB)
            p.op("pe", lambda e, sl=sl: e.matmul(PBt[4][0][:, 0:128], kT[:, sl], qT[:, sl],
                                                 start=True, stop=True),
                 reads=(kT_b, qT_b), writes=(PBt[4][1],))
            p.op("ve", lambda e: e.tensor_tensor(SpT[:, :], PBt[4][0][:, 0:128], Dmask[:, :], OP.mult),
                 reads=(PBt[4][1], Dmask_b), writes=(SpT_b,))
            p.op("ve", lambda e, sl=sl: e.tensor_tensor(qtil[:, :], qT[:, sl], dprow[:, :], OP.mult),
                 reads=(qT_b, dprow_b), writes=(qtil_b,))
            p.op("pe", lambda e, sl=sl: e.matmul(PBt[5][0][:, 0:128], vtok[:, sl], SpT[:, :],
                                                 start=True, stop=False),
                 reads=(vtok_b, SpT_b), writes=(PBt[5][1],))
            p.op("pe", lambda e: e.matmul(PBt[5][0][:, 0:128], Mbf[:, :], qtil[:, :],
                                          start=False, stop=True),
                 reads=(Mbf_b, qtil_b), writes=(PBt[5][1],))
            p.op("se", lambda e, sl=sl: e.copy(ret[:, sl], PBt[5][0][:, 0:128]),
                 reads=(PBt[5][1],), writes=(ret_b,))
            p.op("ve", lambda e, sl=sl: e.tensor_scalar_mul(ktil[:, :], ktok[:, sl], dpcol[:, 0:1]),
                 reads=(ktok_b, dpcol_b), writes=(ktil_b,))
            p.op("pe", lambda e, sl=sl: e.matmul(PBt[4][0][:, 0:128], ktil[:, :], vtok[:, sl],
                                                 start=True, stop=True),
                 reads=(ktil_b, vtok_b), writes=(PBt[4][1],))
            p.op("ve", lambda e: e.tensor_scalar_mul(Mtmp[:, :], Mst[:, :], dTc[:, 0:1]),
                 reads=(Mst_b, dTc_b), writes=(Mtmp_b,))
            p.op("ve", lambda e: e.tensor_tensor(Mst[:, :], Mtmp[:, :], PBt[4][0][:, 0:128], OP.add),
                 reads=(Mtmp_b, PBt[4][1]), writes=(Mst_b,))
            p.op("se", lambda e: e.copy(Mbf[:, :], Mst[:, :]), reads=(Mst_b,), writes=(Mbf_b,))
        if STAGE < 4:
            continue
        for vcB in range(FC):
            pb = PBt[vcB % 2]
            p.op("pe", lambda e, vcB=vcB, pb=pb: e.matmul(pb[0][:, :], owT[:, vcB * 128:(vcB + 1) * 128],
                                                          ret[:, :], start=True, stop=True),
                 reads=(owT_b, ret_b), writes=(pb[1],))
            p.op("se", lambda e, vcB=vcB, pb=pb: e.copy(s0[:, vcB * TC:(vcB + 1) * TC], pb[0][:, :]),
                 reads=(pb[1],), writes=(s0_b,))

        if DBG and c == 0:
            p.dump("ret0", ret[:, :], (ret_b,))
            p.dump("s00", s0[:, :], (s0_b,))
            p.dump("Dmask", Dmask[:, :], (Dmask_b,))
            p.dump("dprow", dprow[:, :], (dprow_b,))
        if STAGE < 4.4:
            continue
        for dc in range(FC):
            pb = PBt[dc % 2]
            p.op("pe", lambda e, dc=dc, pb=pb: e.matmul(pb[0][:, :], woutb[:, dc * 128:(dc + 1) * 128],
                                                        gbf[:, :], start=True, stop=True),
                 reads=(woutb_b, gbf_b), writes=(pb[1],))
            sg = PBt[2 + dc % 2]
            p.act(AF.Silu, sg[0][:, :], pb[0][:, :], bias=boutT[:, dc:dc + 1],
                  reads=(pb[1], boutT_b), writes=(sg[1],))
            tm = TP()
            p.op("ve", lambda e, dc=dc, sg=sg, tm=tm: e.tensor_tensor(tm[0][:, :], s0[:, dc * TC:(dc + 1) * TC],
                                                                      sg[0][:, :], OP.mult),
                 reads=(s0_b, sg[1]), writes=(tm[1],))
            p.op("ve", lambda e, dc=dc, tm=tm: e.tensor_tensor(x1buf[:, dc * TC:(dc + 1) * TC],
                                                               xa[:, dc * TC:(dc + 1) * TC], tm[0][:, :], OP.add),
                 reads=(tm[1], xa_b), writes=(x1buf_b,))
        if STAGE < 4.5:
            continue
        p.dma("se", x1ap.rearrange("(fc p) t -> p fc t", p=128)[:, :, c * TC:(c + 1) * TC],
              x1buf[:, :].rearrange("p (fc t) -> p fc t", fc=FC), reads=(x1buf_b,), writes=(x1scr_b,))
        if c + 1 < NCH:
            load_x(c + 1, 0)
        if DBG and c == 0:
            p.dump("sF0", sF[:, :], (sF_b,))
            p.dump("gacc0", gfin[:, :], (gfin_b,))
            p.dump("x1c0", x1buf[:, :], (x1buf_b,))

        if STAGE < 6:
            continue
        def logits_cb(ic, tf):
            p.op("pe", lambda e, ic=ic, tf=tf: e.matmul(PSm[0:NEXP, :], wgt[:, ic * NEXP:(ic + 1) * NEXP],
                                                        tf[0][:, :], start=(ic == 0), stop=(ic == FC - 1)),
                 reads=(wgt_b, tf[1]), writes=(PSm_b,))

        ln_apply(x1buf, x1buf_b, xn, xn_b, also_f32_cb=logits_cb)
        p.op("ve", lambda e: e.tensor_scalar(lg[:, :], PSm[0:NEXP, :], bgc[0:NEXP, 0:1], None, OP.add),
             reads=(PSm_b, bgc_b), writes=(lg_b,))
        p.dma("se", xn1ap.rearrange("(fc p) t -> p fc t", p=128)[:, :, c * TC:(c + 1) * TC],
              xn[:, :].rearrange("p (fc t) -> p fc t", fc=FC), reads=(xn_b,), writes=(xn1scr_b,))

        if STAGE < 7:
            continue
        for s in range(TC // SUB):
            p.op("pe", lambda e, s=s: e.transpose(PBt[4][0][0:128, 0:NEXP],
                                                  lg[:, s * SUB:(s + 1) * SUB], idf[0:NEXP, 0:NEXP]),
                 reads=(lg_b, idf_b), writes=(PBt[4][1],))
            p.op("ve", lambda e, s=s: e.tensor_copy(ltok[:, s * NEXP:(s + 1) * NEXP], PBt[4][0][0:128, 0:NEXP]),
                 reads=(PBt[4][1],), writes=(ltok_b,))
            p.op("ve", lambda e, s=s: e.reduce_max(mxc[:, s:s + 1], ltok[:, s * NEXP:(s + 1) * NEXP], axis=AX.X),
                 reads=(ltok_b,), writes=(mxc_b,))
            p.op("pe", lambda e, s=s: e.transpose(PBt[4][0][0:1, 0:128], mxc[:, s:s + 1], idf[:, :]),
                 reads=(mxc_b, idf_b), writes=(PBt[4][1],))
            p.op("ve", lambda e, s=s: e.tensor_copy(mxr[0:1, s * SUB:(s + 1) * SUB], PBt[4][0][0:1, 0:128]),
                 reads=(PBt[4][1],), writes=(mxr_b,))
        p.op("pe", lambda e: e.matmul(PBt[4][0][0:NEXP, :], onr[0:1, 0:NEXP], mxr[0:1, :],
                                      start=True, stop=True), reads=(onr_b, mxr_b), writes=(PBt[4][1],))
        p.op("ve", lambda e: e.tensor_tensor(ohb[:, :], lg[:, :], PBt[4][0][0:NEXP, :], OP.is_ge),
             reads=(lg_b, PBt[4][1]), writes=(ohb_b,))
        p.dma("se", ohap[:, c * TC:(c + 1) * TC], ohb[:, :], reads=(ohb_b,), writes=(ohscr_b,))

        if STAGE < 8:
            continue
        xi_f32, xi_f32_b = pool16a
        xg_all, xg_all_b = pool16b
        wt, wb = load_w("W_in", 0)
        for oc2 in range(FC):
            pb = PBt[oc2 % 2]
            for ic in range(FC):
                p.op("pe", lambda e, oc2=oc2, ic=ic, pb=pb, wt=wt: e.matmul(
                    pb[0][:, :], wt[:, ic * D + oc2 * 128: ic * D + (oc2 + 1) * 128],
                    xn[:, ic * TC:(ic + 1) * TC], start=(ic == 0), stop=(ic == FC - 1)),
                    reads=(wb, xn_b), writes=(pb[1],))
            p.act(AF.Identity, xi_f32[:, oc2 * TC:(oc2 + 1) * TC], pb[0][:, :],
                  bias=biasT["b_in"][0][:, oc2:oc2 + 1],
                  reads=(pb[1], biasT["b_in"][1]), writes=(xi_f32_b,))
            p.act(AF.Identity, xi_bf[:, oc2 * TC:(oc2 + 1) * TC], pb[0][:, :],
                  bias=biasT["b_in"][0][:, oc2:oc2 + 1],
                  reads=(pb[1], biasT["b_in"][1]), writes=(xi_bf_b,))
        if STAGE < 9:
            continue
        wt2, wb2 = load_w("W_gx", 1)
        for oc2 in range(FC):
            pb = PBt[oc2 % 2]
            for ic in range(FC):
                p.op("pe", lambda e, oc2=oc2, ic=ic, pb=pb, wt2=wt2: e.matmul(
                    pb[0][:, :], wt2[:, ic * D + oc2 * 128: ic * D + (oc2 + 1) * 128],
                    xi_bf[:, ic * TC:(ic + 1) * TC], start=(ic == 0), stop=(ic == FC - 1)),
                    reads=(wb2, xi_bf_b), writes=(pb[1],))
            sg = PBt[2 + oc2 % 2]
            p.act(AF.Sigmoid, sg[0][:, :], pb[0][:, :], bias=biasT["b_gx"][0][:, oc2:oc2 + 1],
                  reads=(pb[1], biasT["b_gx"][1]), writes=(sg[1],))
            p.op("ve", lambda e, oc2=oc2, sg=sg: e.tensor_tensor(xg_all[:, oc2 * TC:(oc2 + 1) * TC],
                                                                 sg[0][:, :],
                                                                 xi_f32[:, oc2 * TC:(oc2 + 1) * TC], OP.mult),
                 reads=(sg[1], xi_f32_b), writes=(xg_all_b,))
        if STAGE < 10:
            continue
        a_all, a_all_b = pool16a
        ga_all, ga_all_b = a_all, a_all_b
        wt3, wb3 = load_w("W_ga", 0)
        for oc2 in range(FC):
            pb = PBt[oc2 % 2]
            for ic in range(FC):
                p.op("pe", lambda e, oc2=oc2, ic=ic, pb=pb, wt3=wt3: e.matmul(
                    pb[0][:, :], wt3[:, ic * D + oc2 * 128: ic * D + (oc2 + 1) * 128],
                    xi_bf[:, ic * TC:(ic + 1) * TC], start=(ic == 0), stop=(ic == FC - 1)),
                    reads=(wb3, xi_bf_b), writes=(pb[1],))
            sg = PBt[2 + oc2 % 2]
            p.act(AF.Sigmoid, sg[0][:, :], pb[0][:, :], bias=biasT["b_ga"][0][:, oc2:oc2 + 1],
                  reads=(pb[1], biasT["b_ga"][1]), writes=(sg[1],))
            p.op("ve", lambda e, oc2=oc2, sg=sg: e.tensor_scalar(ga_all[:, oc2 * TC:(oc2 + 1) * TC],
                                                                 sg[0][:, :], sp8[:, oc2:oc2 + 1],
                                                                 None, OP.mult),
                 reads=(sg[1], sp8_b), writes=(ga_all_b,))
        if STAGE < 11:
            continue
        if c + 1 < NCH and STAGE >= 99:
            emit_ln0()
        for oc2 in range(FC):
            osl = slice(oc2 * TC, (oc2 + 1) * TC)
            ap_ = PBt[2 + oc2 % 2]
            p.act(AF.Exp, ap_[0][:, :], ga_all[:, osl], reads=(ga_all_b,), writes=(ap_[1],))
            e2a = TP()
            e2b = TP()
            p.act(AF.Exp, PSm[:, :], ga_all[:, osl], scale=2.0, reads=(ga_all_b,), writes=(PSm_b,))
            p.op("ve", lambda e, e2a=e2a: e.tensor_scalar(e2a[0][:, :], PSm[:, :], -1.0, 1.0, OP.mult, OP.add),
                 reads=(PSm_b,), writes=(e2a[1],))
            p.op("ve", lambda e, e2a=e2a, e2b=e2b: e.tensor_scalar_max(e2b[0][:, :], e2a[0][:, :], 1e-38),
                 reads=(e2a[1],), writes=(e2b[1],))
            lnt = TP()
            p.act(AF.Ln, lnt[0][:, :], e2b[0][:, :], reads=(e2b[1],), writes=(lnt[1],))
            p.act(AF.Exp, PSm[:, :], lnt[0][:, :], scale=0.5, reads=(lnt[1],), writes=(PSm_b,))
            p.op("ve", lambda e, osl=osl: e.tensor_tensor(ga_all[:, osl], PSm[:, :],
                                                          xg_all[:, osl], OP.mult),
                 reads=(PSm_b, xg_all_b), writes=(ga_all_b,))
            hf = TP()
            p.op("ve", lambda e, osl=osl, hf=hf, ap_=ap_, oc2=oc2: e.tensor_tensor_scan(
                hf[0][:, :], ap_[0][:, :], ga_all[:, osl],
                hcar[:, oc2:oc2 + 1], OP.mult, OP.add),
                reads=(ap_[1], ga_all_b, hcar_b), writes=(hf[1],))
            p.op("ve", lambda e, oc2=oc2, hf=hf: e.tensor_copy(hcar[:, oc2:oc2 + 1], hf[0][:, TC - 1:TC]),
                 reads=(hf[1],), writes=(hcar_b,))
            p.op("se", lambda e, oc2=oc2, hf=hf: e.copy(h_bf[:, oc2 * TC:(oc2 + 1) * TC], hf[0][:, :]),
                 reads=(hf[1],), writes=(h_bf_b,))
        if STAGE < 12:
            continue
        wt4, wb4 = load_w("W_out", 1)
        if c + 1 < NCH and STAGE >= 99:
            emit_qkv()
        for oc2 in range(FC):
            pb = PBt[4 + oc2 % 2]
            for ic in range(FC):
                p.op("pe", lambda e, oc2=oc2, ic=ic, pb=pb, wt4=wt4: e.matmul(
                    pb[0][:, :], wt4[:, ic * D + oc2 * 128: ic * D + (oc2 + 1) * 128],
                    h_bf[:, ic * TC:(ic + 1) * TC], start=(ic == 0), stop=(ic == FC - 1)),
                    reads=(wb4, h_bf_b), writes=(pb[1],))
            s2t = TP()
            p.act(AF.Identity, s2t[0][:, :], pb[0][:, :], bias=biasT["b_out"][0][:, oc2:oc2 + 1],
                  reads=(pb[1], biasT["b_out"][1]), writes=(s2t[1],))
            p.dma("se", s2ap[oc2 * 128:(oc2 + 1) * 128, c * TC:(c + 1) * TC], s2t[0][:, :],
                  reads=(s2t[1],), writes=(s2scr_b,))


def build_p2_cast(nc, p, w1, w2):
    w1bf_t, _ = p.DR("w1bfT", [NEXP, 128, (HDIM // 512) * FC * 512], BF16)
    w2bf_t, _ = p.DR("w2bfT", [NEXP, 128, 32 * D], BF16)
    w1bf, w2bf = w1bf_t.ap(), w2bf_t.ap()
    w1bufs = [Buf() for _ in range(NEXP)]
    w2bufs = [Buf() for _ in range(NEXP)]
    cast_tile = p.cast_tile

    for e_ in range(NEXP):
        for r in range(D // 128):
            for cc in range(HDIM // 512):
                dst = w1bf[e_, :, (cc * FC + r) * 512:(cc * FC + r + 1) * 512]
                cast_tile(w1[e_, r * 128:(r + 1) * 128, cc * 512:(cc + 1) * 512],
                          dst, w1bufs[e_])
        for r in range(HDIM // 128):
            for cc in range(D // 512):
                dst = w2bf[e_, :, r * D + cc * 512:r * D + (cc + 1) * 512]
                cast_tile(w2[e_, r * 128:(r + 1) * 128, cc * 512:(cc + 1) * 512],
                          dst, w2bufs[e_])
    p.cast_flush()
    return w1bf, w1bufs, w2bf, w2bufs


def build_p2(nc, p, dram, moebf=None, ext_w=None):

    def ext_in(nm, shape, dt=F32):
        return nc.dram_tensor(nm, shape, dt, kind="ExternalInput").ap(), Buf()

    x1ap, x1b = dram["x1scr"][0].ap(), dram["x1scr"][1]
    s2ap, s2b = dram["s2scr"][0].ap(), dram["s2scr"][1]
    xn1ap, xn1b = dram["xn1scr"][0].ap(), dram["xn1scr"][1]
    ohap, oh_b = dram["ohscr"][0].ap(), dram["ohscr"][1]

    if ext_w is None:
        w1, w1_b = ext_in("L1_ffn_W1", [NEXP, D, HDIM])
        w2, w2_b = ext_in("L1_ffn_W2", [NEXP, HDIM, D])
    else:
        w1, w2 = ext_w
        w1_b, w2_b = Buf(), Buf()
    out, out_b = nc.dram_tensor("xT2", [D, T], F32, kind="ExternalOutput").ap(), Buf()

    if moebf is None:
        w1bf, w1bufs, w2bf, w2bufs = build_p2_cast(nc, p, w1, w2)
    else:
        w1bf, w1bufs, w2bf, w2bufs = moebf

    onrb2, onrb2_b = p.S("onrb2", [128, 128], BF16)
    orow = nc.dram_tensor("ones_row_bf2", [1, 128], BF16, kind="ExternalInput").ap()
    p.dma("sy", onrb2[0:1, :], orow[:, :], writes=(onrb2_b,))
    p.dma("sy", onrb2[64:65, :], orow[:, :], writes=(onrb2_b,))

    PB = [p.PS(f"p2pb{i}", [128, 512]) for i in range(6)]

    w2sb, w2sb_b = p.S("w2sb", [128, 32 * D], BF16)
    ybufs = [p.S("ybufA", [128, FC * (T // 2)], BF16), p.S("ybufB", [128, FC * (T // 2)], BF16)]
    h_fm, h_fm_b = p.S("hfm", [128, 32 * 512], BF16)
    w1t = [p.S("w1tA", [128, FC * HC], BF16), p.S("w1tB", [128, FC * HC], BF16)]
    xn1s, xn1s_b = p.S("xn1s", [128, FC * 512], BF16)
    xnm, xnm_b = p.S("xnm", [128, FC * 512], BF16)
    mrowt, _mb = p.S("mrow", [128, 512], BF16)
    mrow = [(mrowt[0:1, :], Buf()), (mrowt[64:65, :], Buf())]
    x1t = [p.S("x1tA", [128, 512], F32)]
    s2t = [p.S("s2tA", [128, 512], F32)]
    ctp = [p.S(f"ctp{i}", [128, 512], F32) for i in range(2)]

    NTS = T // 512
    units = [(th, e_, ts) for th in range(2) for e_ in range(NEXP)
             for ts in range(th * (NTS // 2), (th + 1) * (NTS // 2))]

    def emit_w2load(e_):
        for qq in range(4):
            p.dma("se", w2sb[:, qq * 8 * D:(qq + 1) * 8 * D],
                  w2bf[e_, :, qq * 8 * D:(qq + 1) * 8 * D],
                  reads=(w2bufs[e_],), writes=(w2sb_b,))

    def emit_front(u):
        th, e_, ts = units[u]
        tsl = slice(ts * 512, (ts + 1) * 512)
        p.dma("sy", xn1s[:, :].rearrange("p (fc t) -> p fc t", fc=FC),
              xn1ap.rearrange("(fc p) t -> p fc t", p=128)[:, :, tsl],
              reads=(xn1b,), writes=(xn1s_b,))
        mr, mr_b = mrow[u % 2]
        p.dma("sy", mr[:, :], ohap[e_:e_ + 1, tsl], reads=(oh_b,), writes=(mr_b,))
        obase = 64 * (u % 2)
        p.op("pe", lambda e, mr=mr, obase=obase: e.matmul(
            PB[4][0][:, :], onrb2[obase:obase + 1, :], mr[0:1, :],
            start=True, stop=True),
             reads=(onrb2_b, mr_b), writes=(PB[4][1],))
        for ic in range(FC):
            p.op("ve", lambda e, ic=ic: e.tensor_tensor(xnm[:, ic * 512:(ic + 1) * 512],
                                                        xn1s[:, ic * 512:(ic + 1) * 512],
                                                        PB[4][0][:, :], OP.mult),
                 reads=(xn1s_b, PB[4][1]), writes=(xnm_b,))
        wt, wb = w1t[0]
        p.dma("se", wt[:, :], w1bf[e_, :, 0:FC * 512], reads=(w1bufs[e_],), writes=(wb,))

    for u, (th, e_, ts) in enumerate(units):
        tsl = slice(ts * 512, (ts + 1) * 512)
        if u == 0:
            emit_front(0)
        if ts == th * (NTS // 2):
            emit_w2load(e_)
        for hc in range(HDIM // HC):
            wt, wb = w1t[hc % 2]
            if hc + 1 < HDIM // HC:
                nwt, nwb = w1t[(hc + 1) % 2]
                p.dma("se", nwt[:, :], w1bf[e_, :, (hc + 1) * FC * 512:(hc + 2) * FC * 512],
                      reads=(w1bufs[e_],), writes=(nwb,))
            for hs in range(HC // 128):
                pb = PB[(hc * 4 + hs) % 2]
                for ic in range(FC):
                    p.op("pe", lambda e, hc=hc, hs=hs, ic=ic, pb=pb, wt=wt: e.matmul(
                        pb[0][:, :], wt[:, ic * HC + hs * 128: ic * HC + (hs + 1) * 128],
                        xnm[:, ic * 512:(ic + 1) * 512],
                        start=(ic == 0), stop=(ic == FC - 1)),
                        reads=(wb, xnm_b), writes=(pb[1],))
                p.act(AF.Gelu, h_fm[:, (hc * 4 + hs) * 512:(hc * 4 + hs + 1) * 512], pb[0][:, :],
                      reads=(pb[1],), writes=(h_fm_b,))
        if u + 1 < len(units):
            emit_front(u + 1)
        for dc in range(FC):
            pb = PB[2 + dc % 2]
            for ht in range(32):
                p.op("pe", lambda e, dc=dc, ht=ht, pb=pb: e.matmul(
                    pb[0][:, :], w2sb[:, ht * D + dc * 128: ht * D + (dc + 1) * 128],
                    h_fm[:, ht * 512:(ht + 1) * 512],
                    start=(ht == 0), stop=(ht == 31)),
                    reads=(w2sb_b, h_fm_b), writes=(pb[1],))
            yofs = dc * (T // 2) + (ts % (NTS // 2)) * 512
            ydst, ydst_b = ybufs[e_ % 2]
            ysrc, ysrc_b = ybufs[(e_ + 1) % 2]
            ysl = ydst[:, yofs: yofs + 512]
            if e_ == 0:
                p.op("ve", lambda e, ysl=ysl, pb=pb: e.tensor_copy(ysl, pb[0][:, :]),
                     reads=(pb[1],), writes=(ydst_b,))
            else:
                p.op("ve", lambda e, ysl=ysl, pb=pb, ysrc=ysrc, yofs=yofs: e.tensor_tensor(
                    ysl, pb[0][:, :], ysrc[:, yofs: yofs + 512], OP.add),
                     reads=(pb[1], ysrc_b), writes=(ydst_b,))
            if e_ == NEXP - 1:
                xt, xtb = x1t[0]
                st, stb = s2t[0]
                p.dma("sy", xt[:, :], x1ap[dc * 128:(dc + 1) * 128, tsl], reads=(x1b,), writes=(xtb,))
                p.dma("sy", st[:, :], s2ap[dc * 128:(dc + 1) * 128, tsl], reads=(s2b,), writes=(stb,))
                c0 = ctp[0]
                c1 = ctp[1]
                p.act(AF.Tanh, PB[5][0][:, :], ysl, scale=0.5, reads=(ydst_b,), writes=(PB[5][1],))
                p.op("ve", lambda e, ysl=ysl, c0=c0, st=st: e.tensor_tensor(c0[0][:, :], st[:, :],
                                                                            ysl, OP.mult),
                     reads=(ydst_b, stb), writes=(c0[1],))
                p.op("ve", lambda e, c0=c0, c1=c1: e.tensor_tensor(c1[0][:, :], c0[0][:, :],
                                                                   PB[5][0][:, :], OP.mult),
                     reads=(c0[1], PB[5][1]), writes=(c1[1],))
                p.op("ve", lambda e, c0=c0, st=st, xt=xt: e.scalar_tensor_tensor(st[:, :], c0[0][:, :],
                                                                                 0.5, xt[:, :],
                                                                                 OP.mult, OP.add),
                     reads=(c0[1], xtb), writes=(stb,))
                p.op("ve", lambda e, c1=c1, st=st, c0=c0: e.scalar_tensor_tensor(c0[0][:, :], c1[0][:, :],
                                                                                 0.5, st[:, :],
                                                                                 OP.mult, OP.add),
                     reads=(c1[1], stb), writes=(c0[1],))
                p.dma("se", out[dc * 128:(dc + 1) * 128, tsl], c0[0][:, :],
                      reads=(c0[1],), writes=(out_b,))


def _mk_dram(p, external):
    kind = "ExternalOutput" if external == "out" else ("ExternalInput" if external == "in" else "Internal")
    dram = {}
    specs = (("x1scr", [D, T], F32), ("s2scr", [D, T], F32),
             ("xn1scr", [D, T], BF16), ("ohscr", [NEXP, T], BF16))
    for nm, shape, dt_ in specs:
        if external == "internal":
            t, b = p.DR(nm, shape, dt_, kind="Internal")
        else:
            t = p.nc.dram_tensor(nm, shape, dt_, kind=kind)
            b = Buf()
        dram[nm] = (t, b)
    return dram


def _build(mode):
    nc = bass.Bass("TRN2", target_bir_lowering=False)
    p = Prog(nc)
    p.setup_act_guard()
    if mode == "p1":
        dram = _mk_dram(p, "out")
        build_p1(nc, p, dram)
    elif mode == "p2":
        dram = _mk_dram(p, "in")
        build_p2(nc, p, dram)
    else:
        dram = _mk_dram(p, "internal")
        scope1 = ExitStack()
        p.default_scope = scope1
        build_p1(nc, p, dram)
        p.default_scope = None
        p.fence("po")
        snap = {s: v for s, v in p.cnt.items() if v > 0}
        for e in Prog.ENGS:
            if e == "po":
                continue
            for s, v in snap.items():
                p._wait(e, s, v)
        w1e = nc.dram_tensor("L1_ffn_W1", [NEXP, D, HDIM], F32, kind="ExternalInput").ap()
        w2e = nc.dram_tensor("L1_ffn_W2", [NEXP, HDIM, D], F32, kind="ExternalInput").ap()
        moebf = build_p2_cast(nc, p, w1e, w2e)
        scope1.close()
        p.extra_dep = dict(snap)
        build_p2(nc, p, dram, moebf=moebf, ext_w=(w1e, w2e))
    p.final_wait()
    p.es.close()
    return nc


P1_KEYS = ("xT", "L0_attn_basis", "L0_attn_q_coeffs", "L0_attn_k_coeffs",
           "L0_attn_v_coeffs", "L0_attn_o_coeffs", "L0_attn_decay",
           "L0_attn_out_scale", "L0_ffn_Win", "L0_ffn_bin_", "L0_ffn_W1",
           "L0_ffn_b1", "L0_ffn_W2", "L0_ffn_b2", "L0_ffn_Wout", "L0_ffn_bout",
           "L1_attn_W_in", "L1_attn_b_in", "L1_attn_W_gx", "L1_attn_b_gx",
           "L1_attn_W_ga", "L1_attn_b_ga", "L1_attn_a_param", "L1_attn_W_out",
           "L1_attn_b_out", "L1_ffn_Wg", "L1_ffn_bg",
           "ident_f32", "ident_bf", "ones_col", "ones_row", "ones_row_bfs",
           "iotaE", "trimask", "iotarow", "revcol")
P2_KEYS = ("L1_ffn_W1", "L1_ffn_W2", "ones_row_bf2")


def _input_maps(inputs):
    consts = _consts_np()
    x = np.asarray(inputs["L0_attn_x"])
    base = {}
    for k, v in inputs.items():
        if k == "L0_attn_x":
            continue
        a = np.asarray(v, dtype=np.float32)
        if k in ("L0_attn_decay", "L0_attn_out_scale"):
            a = a.reshape(1, 1)
        if a.ndim == 1:
            a = a.reshape(1, -1)
        base[k] = np.ascontiguousarray(a)
    for k, v in consts.items():
        base[k] = v
    base["ones_row_bf2"] = consts["ones_row_bfs"]
    maps = []
    for b in range(B):
        m = dict(base)
        m["xT"] = np.ascontiguousarray(x[b].T.astype(np.float32))
        maps.append(m)
    return maps


FUSED = True


def kernel(**inputs):
    maps = _input_maps(inputs)
    if FUSED:
        nc = _build("full")
        keys = tuple(set(P1_KEYS) | set(P2_KEYS))
        inm = [{k: m[k] for k in keys} for m in maps]
        rr = run_bass_kernel_spmd(nc, inm, core_ids=list(range(B)))
        outs = []
        for b in range(B):
            xt2 = np.asarray(rr.results[b]["xT2"], dtype=np.float32)
            outs.append(np.ascontiguousarray(xt2.T))
        return np.stack(outs, axis=0)
    nc1 = _build("p1")
    in1 = [{k: m[k] for k in P1_KEYS} for m in maps]
    r1 = run_bass_kernel_spmd(nc1, in1, core_ids=list(range(B)))
    r1m = [{k: np.array(v) for k, v in r.items()} for r in r1.results]
    nc2 = _build("p2")
    in2 = []
    for b in range(B):
        m = {k: maps[b][k] for k in P2_KEYS}
        for nm in ("x1scr", "s2scr", "xn1scr", "ohscr"):
            m[nm] = r1m[b][nm]
        in2.append(m)
    r2 = run_bass_kernel_spmd(nc2, in2, core_ids=list(range(B)))
    outs = []
    for b in range(B):
        xt2 = np.asarray(r2.results[b]["xT2"], dtype=np.float32)
        outs.append(np.ascontiguousarray(xt2.T))
    return np.stack(outs, axis=0)


# version 18
# speedup vs baseline: 1.0452x; 1.0194x over previous
import numpy as np
import ml_dtypes
from contextlib import ExitStack

import concourse.bass as bass
import concourse.mybir as mybir
from concourse.bass_utils import run_bass_kernel_spmd
from concourse.hw_specs import get_activation_tables

F32 = mybir.dt.float32
BF16 = mybir.dt.bfloat16
AF = mybir.ActivationFunctionType
OP = mybir.AluOpType
AX = mybir.AxisListType

B = 8
D = 1024
T = 4096
FC = 8
TC = 512
NCH = T // TC
SUB = 128
NB2 = 64
NEXP = 8
HDIM = 4096
HC = 512
MF = 64
INNER = 16

LND_CLAMP = -85.0

_AF = mybir.ActivationFunctionType
ACT_NEUTRAL = {_AF.Identity, _AF.Copy, _AF.Square, _AF.Relu, _AF.Abs, _AF.Sign, _AF.MemsetZero}
ACT_FUNC_SET = {_AF.Exp: "nlexp", _AF.Ln: "nlexp", _AF.Sigmoid: "sigmoid",
                _AF.Gelu: "gelu", _AF.Tanh: "gelu", _AF.Silu: "silu",
                _AF.Sqrt: "sqrt"}
ACT_SET_NAMES = {"nlexp": "natural_log_exp_and_others", "sigmoid": "sigmoid_and_others",
                 "gelu": "gelu_and_others", "silu": "silu_and_others",
                 "sqrt": "sqrt_and_others"}
ACT_SET_REP = {"nlexp": _AF.Exp, "sigmoid": _AF.Sigmoid, "gelu": _AF.Gelu,
               "silu": _AF.Silu, "sqrt": _AF.Sqrt}


class Buf:
    __slots__ = ("w", "r", "dsem")

    def __init__(self):
        self.w = {}
        self.r = {}
        self.dsem = None


class Prog:
    ENGS = ("pe", "ve", "se", "po", "sy")

    def __init__(self, nc):
        self.nc = nc
        self.es = ExitStack()
        self.E = {"pe": nc.tensor, "ve": nc.vector, "se": nc.scalar,
                  "po": nc.gpsimd, "sy": nc.sync}
        self.sem = {}
        self.cnt = {}
        for s in ("pe", "ve", "se", "po", "qsy", "qse", "qpo"):
            self.sem[s] = self.es.enter_context(nc.semaphore("sem_" + s))
            self.cnt[s] = 0
        self.waited = {}
        self.engsem = {"pe": "pe", "ve": "ve", "se": "se", "po": "po"}
        self.drained = {"ve": 0, "se": 0, "po": 0}
        self.drain_marks = {"ve": [], "se": [], "po": []}
        self.default_scope = None
        self.extra_dep = None
        self.uid = 0

    def name(self, p):
        self.uid += 1
        return f"{p}_{self.uid}"

    def S(self, p, shape, dt, scope=None):
        side = "left" if scope is None else "right"
        t = (scope or self.default_scope or self.es).enter_context(
            self.nc.sbuf_tensor(self.name(p), shape, dt, side=side))
        return t, Buf()

    def cast_staging(self):
        if not hasattr(self, "_cstg"):
            sc = ExitStack()
            self.es.callback(sc.close)
            stgf, _ = self.S("cstgf", [128, 2 * 512], F32, scope=sc)
            stgb, _ = self.S("cstgb", [128, 2 * 512], BF16, scope=sc)
            self._cstg = (stgf, stgb, [(Buf(), Buf()), (Buf(), Buf())], [0])
            self._cast_pend = None
        return self._cstg

    def cast_tile(self, src_ap, dst_ap, dst_buf, src_buf=None):
        stgf, stgb, slotbufs, slot = self.cast_staging()
        sl = slot[0]
        slot[0] = 1 - slot[0]
        fb, bb = slotbufs[sl]
        fap = stgf[:, sl * 512:(sl + 1) * 512]
        bap = stgb[:, sl * 512:(sl + 1) * 512]
        self.dma("po", fap, src_ap, reads=(src_buf,) if src_buf is not None else (),
                 writes=(fb,))
        pend = self._cast_pend
        self._cast_pend = (fap, bap, fb, bb, dst_ap, dst_buf)
        if pend is not None:
            self._cast_emit(pend)

    def _cast_emit(self, pend):
        fap, bap, fb, bb, dst_ap, dst_buf = pend
        self.op("po", lambda e, bap=bap, fap=fap: e.tensor_copy(bap, fap),
                reads=(fb,), writes=(bb,))
        self.dma("po", dst_ap, bap, reads=(bb,), writes=(dst_buf,))

    def cast_flush(self):
        if getattr(self, "_cast_pend", None) is not None:
            self._cast_emit(self._cast_pend)
            self._cast_pend = None

    def barrier(self):
        snap = dict(self.cnt)
        for e in self.ENGS:
            for s, v in snap.items():
                if v > 0:
                    self._wait(e, s, v)

    def PS(self, p, shape, dt=F32, scope=None):
        t = (scope or self.default_scope or self.es).enter_context(
            self.nc.psum_tensor(self.name(p), shape, dt))
        return t, Buf()

    def DR(self, p, shape, dt, kind="Internal"):
        t = self.nc.dram_tensor(self.name(p) if kind == "Internal" else p, shape, dt, kind=kind)
        return t, Buf()

    CLEAN_OP_EDGES = {("se", "pe"), ("ve", "pe"), ("po", "pe"), ("po", "ve"),
                      ("ve", "se")}

    def _wait(self, eng, semname, val, skip_same=True, need_drain=True, dma=False):
        if skip_same and self.engsem.get(eng) == semname:
            return
        if need_drain and not dma and (semname, eng) in self.CLEAN_OP_EDGES:
            need_drain = False
        if need_drain and semname in self.drain_marks and eng != semname:
            cover = None
            for covered, dcnt in self.drain_marks[semname]:
                if covered >= val:
                    cover = dcnt
                    break
            if cover is None:
                covered = self.cnt[semname]
                dinst = self.E[semname].drain()
                self.cnt[semname] += 1
                dinst.then_inc(self.sem[semname], 1)
                cover = self.cnt[semname]
                self.drain_marks[semname].append((covered, cover))
                self.drained[semname] = cover
            val = cover
        key = (eng, semname)
        if self.waited.get(key, 0) >= val:
            return
        self.E[eng].wait_ge(self.sem[semname], val)
        self.waited[key] = val

    def _deps(self, eng, reads, writes, skip_same=True, need_drain=True, dma=False):
        need = {}
        if self.extra_dep is not None:
            for s, v in self.extra_dep.items():
                if need.get(s, 0) < v:
                    need[s] = v
        for b in reads:
            for s, v in b.w.items():
                if need.get(s, 0) < v:
                    need[s] = v
        for b in writes:
            for s, v in b.w.items():
                if need.get(s, 0) < v:
                    need[s] = v
            for s, v in b.r.items():
                if need.get(s, 0) < v:
                    need[s] = v
        for s, v in need.items():
            self._wait(eng, s, v, skip_same=skip_same, need_drain=need_drain, dma=dma)

    def _finish(self, sname, val, reads, writes):
        for b in reads:
            if b.r.get(sname, 0) < val:
                b.r[sname] = val
        for b in writes:
            if b.w.get(sname, 0) < val:
                b.w[sname] = val
            b.r = {}

    def fence(self, eng):
        covered = self.cnt[eng]
        inst = self.E[eng].drain()
        self.cnt[eng] += 1
        inst.then_inc(self.sem[eng], 1)
        if eng in self.drained:
            self.drained[eng] = self.cnt[eng]
            self.drain_marks[eng].append((covered, self.cnt[eng]))
        return (eng, self.cnt[eng])

    def op(self, eng, fn, reads=(), writes=()):
        self._deps(eng, reads, writes, skip_same=(eng == "pe"),
                   need_drain=(eng != "pe"))
        inst = fn(self.E[eng])
        self.cnt[eng] += 1
        inst.then_inc(self.sem[eng], 1)
        self._finish(eng, self.cnt[eng], reads, writes)
        return (eng, self.cnt[eng])

    def dma(self, q, out, in_, reads=(), writes=(), **kw):
        self._deps(q, reads, writes, skip_same=False, dma=True)
        inst = self.E[q].dma_start(out=out, in_=in_, **kw)
        db = writes[0] if writes else None
        if db is not None:
            if db.dsem is None:
                nm = self.name("dsem")
                self.sem[nm] = self.es.enter_context(self.nc.semaphore(nm))
                self.cnt[nm] = 0
                db.dsem = nm
            qs = db.dsem
        else:
            qs = "q" + q
        self.cnt[qs] += 16
        inst.then_inc(self.sem[qs], 16)
        self._finish(qs, self.cnt[qs], reads, writes)
        return (qs, self.cnt[qs])

    def setup_act_guard(self):
        tabs = list(get_activation_tables(self.nc.m.arch).keys())
        self.set_ids = {k: tabs.index(v) for k, v in ACT_SET_NAMES.items()}
        self.cur_set = None

    def ensure_set(self, s):
        if self.cur_set == s:
            return
        self.cur_set = s
        self.E["se"].add_instruction(mybir.InstLoadActFuncSet(
            name=self.nc.get_next_instruction_name(),
            act_func_set_id=self.set_ids[s], ins=[], outs=[]))

    def act(self, fn_func, out, in_, bias=0.0, scale=1.0, reads=(), writes=()):
        s = ACT_FUNC_SET.get(fn_func)
        if s is not None:
            self.ensure_set(s)
        return self.op("se", lambda e: e.activation(out, in_, fn_func, bias=bias, scale=scale),
                       reads=reads, writes=writes)

    def dump(self, name, ap, reads, shape=None, dt=None):
        t = self.nc.dram_tensor("dbg_" + name, shape or list(ap.shape), dt or ap.dtype, kind="ExternalOutput")
        self.dma("po", t.ap()[:, :], ap, reads=reads, writes=(Buf(),))

    def final_wait(self):
        snap = dict(self.cnt)
        for s, v in snap.items():
            if v > 0:
                self._wait("sy", s, v)


def _consts_np():
    i = np.arange(128)
    bf = ml_dtypes.bfloat16
    c = {}
    c["ident_f32"] = np.eye(128, dtype=np.float32)
    c["ident_bf"] = np.eye(128, dtype=np.float32).astype(bf)
    c["ones_col"] = np.ones((128, 1), dtype=np.float32)
    c["ones_row"] = np.ones((1, 128), dtype=np.float32)
    c["ones_row_bfs"] = np.ones((1, 128), dtype=np.float32).astype(bf)
    c["iotaE"] = (i[None, :] - i[:, None] - 1).astype(np.float32)
    c["trimask"] = (i[None, :] > i[:, None]).astype(np.float32)
    c["iotarow"] = i[None, :].astype(np.float32)
    c["revcol"] = (127 - i)[:, None].astype(np.float32)
    return c


CONST_SPECS = (("ident_f32", [128, 128], F32), ("ident_bf", [128, 128], BF16),
               ("ones_col", [128, 1], F32), ("ones_row", [1, 128], F32),
               ("ones_row_bfs", [1, 128], BF16), ("iotaE", [128, 128], F32),
               ("trimask", [128, 128], F32), ("iotarow", [1, 128], F32),
               ("revcol", [128, 1], F32))


DBG = False
STAGE = 99


def build_p1(nc, p, dram):

    def ext_in(nm, shape, dt=F32):
        return nc.dram_tensor(nm, shape, dt, kind="ExternalInput").ap(), Buf()

    xT, xT_b = ext_in("xT", [D, T])
    basis, basis_b = ext_in("L0_attn_basis", [D, NB2])
    coef_in = {}
    for nm in ("q", "k", "v", "o"):
        coef_in[nm] = ext_in(f"L0_attn_{nm}_coeffs", [128, NB2])
    decay, decay_b = ext_in("L0_attn_decay", [1, 1])
    oscale, oscale_b = ext_in("L0_attn_out_scale", [1, 1])
    Win, Win_b = ext_in("L0_ffn_Win", [D, MF])
    binv, binv_b = ext_in("L0_ffn_bin_", [1, MF])
    W1f, W1f_b = ext_in("L0_ffn_W1", [MF, INNER])
    b1f, b1f_b = ext_in("L0_ffn_b1", [MF, INNER])
    W2f, W2f_b = ext_in("L0_ffn_W2", [MF, INNER])
    b2f, b2f_b = ext_in("L0_ffn_b2", [1, MF])
    Wout, Wout_b = ext_in("L0_ffn_Wout", [MF, D])
    boutv, boutv_b = ext_in("L0_ffn_bout", [1, D])
    l1w = {}
    for nm in ("W_in", "W_gx", "W_ga", "W_out"):
        l1w[nm] = ext_in("L1_attn_" + nm, [D, D])
    l1b = {}
    for nm in ("b_in", "b_gx", "b_ga", "b_out", "a_param"):
        l1b[nm] = ext_in("L1_attn_" + nm, [1, D])
    Wg, Wg_b = ext_in("L1_ffn_Wg", [D, NEXP])
    bg, bg_b = ext_in("L1_ffn_bg", [1, NEXP])
    cst = {}
    for nm, shape, dt_ in CONST_SPECS:
        cst[nm] = ext_in(nm, shape, dt_)

    x1scr, x1scr_b = dram["x1scr"]
    s2scr, s2scr_b = dram["s2scr"]
    xn1scr, xn1scr_b = dram["xn1scr"]
    ohscr, ohscr_b = dram["ohscr"]
    x1ap, s2ap, xn1ap, ohap = x1scr.ap(), s2scr.ap(), xn1scr.ap(), ohscr.ap()
    s2w = [Buf() for _ in range(4)]

    l1wbf = {}
    l1wbf_bufs = {}
    for nm in ("W_in", "W_gx", "W_ga", "W_out"):
        t_, b_ = p.DR("l1wbfT_" + nm, [128, FC * D], BF16)
        l1wbf[nm] = t_
        l1wbf_bufs[nm] = b_
        sap, sbuf = l1w[nm]
        for ic in range(FC):
            for cc in range(2):
                p.cast_tile(sap[ic * 128:(ic + 1) * 128, cc * 512:(cc + 1) * 512],
                            t_.ap()[:, ic * D + cc * 512: ic * D + (cc + 1) * 512],
                            b_, src_buf=sbuf)
    p.cast_flush()

    PBt = [p.PS(f"pb{i}", [128, 512]) for i in range(6)]
    PTb, PTb_b = p.PS("ptb", [128, 1024], BF16)
    PSm, PSm_b = p.PS("psm", [128, 512])

    pre_es = ExitStack()

    def Sld(nm, shape, dt, src, src_b, q="sy", scope=None):
        t, b = p.S(nm, shape, dt, scope=scope)
        p.dma(q, t[:, :], src, reads=(src_b,), writes=(b,))
        return t, b

    idf, idf_b = Sld("idf", [128, 128], F32, cst["ident_f32"][0][:, :], cst["ident_f32"][1])
    idb, idb_b = Sld("idb", [128, 128], BF16, cst["ident_bf"][0][:, :], cst["ident_bf"][1], scope=pre_es)
    onc, onc_b = Sld("onc", [128, 1], F32, cst["ones_col"][0][:, :], cst["ones_col"][1])
    onr, onr_b = Sld("onr", [1, 128], F32, cst["ones_row"][0][:, :], cst["ones_row"][1])
    iotaE, iotaE_b = Sld("iotaE", [128, 128], F32, cst["iotaE"][0][:, :], cst["iotaE"][1], scope=pre_es)
    trim, trim_b = Sld("trim", [128, 128], F32, cst["trimask"][0][:, :], cst["trimask"][1], scope=pre_es)
    iotar, iotar_b = Sld("iotar", [1, 128], F32, cst["iotarow"][0][:, :], cst["iotarow"][1], scope=pre_es)
    revc, revc_b = Sld("revc", [128, 1], F32, cst["revcol"][0][:, :], cst["revcol"][1], scope=pre_es)

    bas, bas_b = p.S("bas", [128, FC * NB2], F32, scope=pre_es)
    for ic in range(FC):
        p.dma("sy", bas[:, ic * NB2:(ic + 1) * NB2], basis[ic * 128:(ic + 1) * 128, :],
              reads=(basis_b,), writes=(bas_b,))
    coef = {}
    for nm in ("q", "k", "v", "o"):
        src, srcb = coef_in[nm]
        coef[nm] = Sld("coef" + nm, [128, NB2], F32, src[:, :], srcb, scope=pre_es)
    dscal, dscal_b = Sld("dscal", [1, 1], F32, decay[:, :], decay_b, scope=pre_es)
    osc, osc_b = Sld("osc", [1, 1], F32, oscale[:, :], oscale_b, scope=pre_es)
    winf, winf_b = p.S("winf", [128, FC * MF], F32, scope=pre_es)
    for ic in range(FC):
        p.dma("sy", winf[:, ic * MF:(ic + 1) * MF], Win[ic * 128:(ic + 1) * 128, :],
              reads=(Win_b,), writes=(winf_b,))
    woutf, woutf_b = Sld("woutf", [MF, D], F32, Wout[:, :], Wout_b, scope=pre_es)
    w1c, w1c_b = Sld("w1c", [MF, INNER], F32, W1f[:, :], W1f_b)
    b1c, b1c_b = Sld("b1c", [MF, INNER], F32, b1f[:, :], b1f_b)
    w2c, w2c_b = Sld("w2c", [MF, INNER], F32, W2f[:, :], W2f_b)
    binr, binr_b = Sld("binr", [1, MF], F32, binv[:, :], binv_b, scope=pre_es)
    b2r, b2r_b = Sld("b2r", [1, MF], F32, b2f[:, :], b2f_b, scope=pre_es)
    boutr, boutr_b = Sld("boutr", [1, D], F32, boutv[:, :], boutv_b, scope=pre_es)
    l1br = {}
    for nm in ("b_in", "b_gx", "b_ga", "b_out", "a_param"):
        l1br[nm] = Sld("r" + nm, [1, D], F32, l1b[nm][0][:, :], l1b[nm][1], scope=pre_es)
    wgt, wgt_b = p.S("wgt", [128, FC * NEXP], F32)
    p.dma("sy", wgt[:, :].rearrange("p (ic e) -> p ic e", ic=FC),
          Wg.rearrange("(ic p) e -> p ic e", p=128), reads=(Wg_b,), writes=(wgt_b,))
    bgr, bgr_b = Sld("bgr", [1, NEXP], F32, bg[:, :], bg_b, scope=pre_es)

    def row_to_cols(row_t, row_b, ncols, name, scope=None):
        colt, colb = p.S(name, [128, max(1, ncols // 128)], F32, scope=scope)
        for j in range(max(1, ncols // 128)):
            w = min(128, ncols - j * 128)
            p.op("pe", lambda e, j=j, w=w: e.transpose(PBt[4][0][0:w, 0:1],
                                                       row_t[0:1, j * 128:j * 128 + w], idf[0:1, 0:1]),
                 reads=(row_b, idf_b), writes=(PBt[4][1],))
            p.op("ve", lambda e, j=j, w=w: e.tensor_copy(colt[0:w, j:j + 1], PBt[4][0][0:w, 0:1]),
                 reads=(PBt[4][1],), writes=(colb,))
        return colt, colb

    binc, binc_b = row_to_cols(binr, binr_b, MF, "binc")
    b2cc, b2cc_b = row_to_cols(b2r, b2r_b, MF, "b2cc")
    boutT, boutT_b = row_to_cols(boutr, boutr_b, D, "boutT")
    biasT = {}
    for nm in ("b_in", "b_gx", "b_ga", "b_out"):
        rt_, rb_ = l1br[nm]
        biasT[nm] = row_to_cols(rt_, rb_, D, "bT" + nm)
    apT, apT_b = row_to_cols(l1br["a_param"][0], l1br["a_param"][1], D, "apT", scope=pre_es)
    bgc, bgc_b = row_to_cols(bgr, bgr_b, NEXP, "bgc")

    sp8, sp8_b = p.S("sp8", [128, FC], F32)
    tpb, tpb_b = p.S("pre_t1", [128, FC], F32, scope=pre_es)
    tpc, tpc_b = p.S("pre_t2", [128, FC], F32, scope=pre_es)
    p.act(AF.Abs, PBt[4][0][0:128, 0:FC], apT[:, :], reads=(apT_b,), writes=(PBt[4][1],))
    p.act(AF.Exp, PBt[5][0][0:128, 0:FC], PBt[4][0][0:128, 0:FC], scale=-1.0,
          reads=(PBt[4][1],), writes=(PBt[5][1],))
    p.act(AF.Ln, PBt[4][0][0:128, 0:FC], PBt[5][0][0:128, 0:FC], bias=1.0,
          reads=(PBt[5][1],), writes=(PBt[4][1],))
    p.op("ve", lambda e: e.tensor_scalar_max(tpb[:, :], apT[:, :], 0.0), reads=(apT_b,), writes=(tpb_b,))
    p.fence("ve")
    p.op("ve", lambda e: e.tensor_tensor(tpc[:, :], tpb[:, :], PBt[4][0][0:128, 0:FC], OP.add),
         reads=(tpb_b, PBt[4][1]), writes=(tpc_b,))
    p.fence("ve")
    p.op("ve", lambda e: e.tensor_scalar_mul(sp8[:, :], tpc[:, :], -8.0), reads=(tpc_b,), writes=(sp8_b,))
    p.fence("ve")

    dsig, dsig_b = p.S("dsig", [1, 2], F32, scope=pre_es)
    p.act(AF.Exp, PBt[4][0][0:1, 0:1], dscal[:, :], scale=-1.0, reads=(dscal_b,), writes=(PBt[4][1],))
    p.op("ve", lambda e: e.tensor_scalar_add(dsig[:, 0:1], PBt[4][0][0:1, 0:1], 1.0),
         reads=(PBt[4][1],), writes=(dsig_b,))
    p.fence("ve")
    p.op("ve", lambda e: e.reciprocal(dsig[:, 1:2], dsig[:, 0:1]), reads=(dsig_b,), writes=(dsig_b,))
    p.fence("ve")
    lnd, lnd_b = p.S("lnd", [1, 1], F32, scope=pre_es)
    p.act(AF.Ln, PBt[4][0][0:1, 0:1], dsig[:, 1:2], reads=(dsig_b,), writes=(PBt[4][1],))
    p.op("ve", lambda e: e.tensor_scalar_max(lnd[:, :], PBt[4][0][0:1, 0:1], LND_CLAMP),
         reads=(PBt[4][1],), writes=(lnd_b,))
    p.fence("ve")
    lndc, lndc_b = p.S("lndc", [128, 1], F32, scope=pre_es)
    p.op("pe", lambda e: e.matmul(PBt[4][0][0:128, 0:1], onr[0:1, :], lnd[0:1, 0:1],
                                  start=True, stop=True), reads=(onr_b, lnd_b), writes=(PBt[4][1],))
    p.op("ve", lambda e: e.tensor_copy(lndc[:, 0:1], PBt[4][0][0:128, 0:1]),
         reads=(PBt[4][1],), writes=(lndc_b,))
    p.fence("ve")
    Dmask, Dmask_b = p.S("Dmask", [128, 128], F32)
    p.op("ve", lambda e: e.tensor_scalar_mul(Dmask[:, :], iotaE[:, :], lndc[:, 0:1]),
         reads=(iotaE_b, lndc_b), writes=(Dmask_b,))
    p.act(AF.Exp, PBt[4][0][:, 0:128], Dmask[:, :], reads=(Dmask_b,), writes=(PBt[4][1],))
    p.op("ve", lambda e: e.tensor_tensor(Dmask[:, :], PBt[4][0][:, 0:128], trim[:, :], OP.mult),
         reads=(PBt[4][1], trim_b), writes=(Dmask_b,))
    drow, drow_b = p.S("drow", [1, 128], F32, scope=pre_es)
    p.op("ve", lambda e: e.tensor_scalar_mul(drow[:, :], iotar[:, :], lnd[0:1, 0:1]),
         reads=(iotar_b, lnd_b), writes=(drow_b,))
    p.act(AF.Exp, PBt[5][0][0:1, 0:128], drow[:, :], reads=(drow_b,), writes=(PBt[5][1],))
    p.op("ve", lambda e: e.tensor_copy(drow[:, :], PBt[5][0][0:1, 0:128]),
         reads=(PBt[5][1],), writes=(drow_b,))
    dprow, dprow_b = p.S("dprow", [128, 128], F32)
    p.op("pe", lambda e: e.matmul(PBt[4][0][0:128, 0:128], onr[0:1, :], drow[0:1, :],
                                  start=True, stop=True), reads=(onr_b, drow_b), writes=(PBt[4][1],))
    p.op("ve", lambda e: e.tensor_copy(dprow[:, :], PBt[4][0][0:128, 0:128]),
         reads=(PBt[4][1],), writes=(dprow_b,))
    dpcol, dpcol_b = p.S("dpcol", [128, 1], F32)
    p.op("ve", lambda e: e.tensor_scalar_mul(dpcol[:, :], revc[:, :], lndc[:, 0:1]),
         reads=(revc_b, lndc_b), writes=(dpcol_b,))
    p.act(AF.Exp, PBt[5][0][0:128, 0:1], dpcol[:, :], reads=(dpcol_b,), writes=(PBt[5][1],))
    p.op("ve", lambda e: e.tensor_copy(dpcol[:, :], PBt[5][0][0:128, 0:1]),
         reads=(PBt[5][1],), writes=(dpcol_b,))
    dTc, dTc_b = p.S("dTc", [128, 1], F32)
    p.op("ve", lambda e: e.tensor_scalar_mul(dTc[:, :], lndc[:, 0:1], 128.0),
         reads=(lndc_b,), writes=(dTc_b,))
    p.act(AF.Exp, PBt[5][0][0:128, 0:1], dTc[:, :], reads=(dTc_b,), writes=(PBt[5][1],))
    p.op("ve", lambda e: e.tensor_copy(dTc[:, :], PBt[5][0][0:128, 0:1]),
         reads=(PBt[5][1],), writes=(dTc_b,))

    basT, basT_b = p.S("basT", [NB2, D], F32, scope=pre_es)
    for ic in range(FC):
        p.op("pe", lambda e, ic=ic: e.transpose(PBt[5][0][0:NB2, 0:128],
                                                bas[:, ic * NB2:(ic + 1) * NB2], idf[:, :]),
             reads=(bas_b, idf_b), writes=(PBt[5][1],))
        p.op("ve", lambda e, ic=ic: e.tensor_copy(basT[:, ic * 128:(ic + 1) * 128], PBt[5][0][0:NB2, 0:128]),
             reads=(PBt[5][1],), writes=(basT_b,))

    wnat = {}
    owT = owT_b = None
    for nm in ("q", "k", "v", "o"):
        ct, cb = coef[nm]
        cT, cTb = p.S("cT" + nm, [NB2, 128], F32, scope=pre_es)
        p.op("pe", lambda e, ct=ct: e.transpose(PBt[5][0][0:NB2, 0:128], ct[:, :], idf[:, :]),
             reads=(cb, idf_b), writes=(PBt[5][1],))
        p.op("ve", lambda e, cT=cT: e.tensor_copy(cT[:, :], PBt[5][0][0:NB2, 0:128]),
             reads=(PBt[5][1],), writes=(cTb,))
        wT, wTb = p.S("wT" + nm, [128, D], F32, scope=pre_es)
        for hh in range(2):
            p.op("pe", lambda e, hh=hh, cT=cT: e.matmul(PBt[hh][0][:, :], cT[:, :],
                                                        basT[0:NB2, hh * 512:(hh + 1) * 512],
                                                        start=True, stop=True),
                 reads=(cTb, basT_b), writes=(PBt[hh][1],))
            if nm == "o":
                p.op("ve", lambda e, hh=hh, wT=wT: e.tensor_copy(wT[:, hh * 512:(hh + 1) * 512],
                                                                 PBt[hh][0][:, :]),
                     reads=(PBt[hh][1],), writes=(wTb,))
        if nm == "o":
            oscb, oscb_b = p.S("oscb", [128, 1], F32, scope=pre_es)
            p.op("pe", lambda e: e.matmul(PBt[4][0][0:128, 0:1], onr[0:1, :], osc[0:1, 0:1],
                                          start=True, stop=True), reads=(onr_b, osc_b), writes=(PBt[4][1],))
            p.op("ve", lambda e: e.tensor_copy(oscb[:, 0:1], PBt[4][0][0:128, 0:1]),
                 reads=(PBt[4][1],), writes=(oscb_b,))
            owT, owT_b = p.S("owT", [128, D], BF16)
            p.op("se", lambda e: e.activation(owT[:, :], wT[:, :], AF.Identity, scale=oscb[:, 0:1]),
                 reads=(wTb, oscb_b), writes=(owT_b,))
        else:
            mx, mx_b = p.S("mxw", [128, 4], F32, scope=pre_es)
            p.op("ve", lambda e: e.reduce_max(mx[:, 0:1], PBt[0][0][:, :], axis=AX.X),
                 reads=(PBt[0][1],), writes=(mx_b,))
            p.op("ve", lambda e: e.reduce_max(mx[:, 1:2], PBt[1][0][:, :], axis=AX.X),
                 reads=(PBt[1][1],), writes=(mx_b,))
            p.fence("ve")
            p.op("ve", lambda e: e.tensor_max(mx[:, 2:3], mx[:, 0:1], mx[:, 1:2]), reads=(mx_b,), writes=(mx_b,))
            p.op("se", lambda e: e.mul(mx[:, 3:4], mx[:, 2:3], -1.0), reads=(mx_b,), writes=(mx_b,))
            if DBG and nm == "q":
                p.dump("nmx", mx[:, 3:4], (mx_b,))
            for hh in range(2):
                p.act(AF.Exp, PBt[2 + hh][0][:, :], PBt[hh][0][:, :], bias=mx[:, 3:4],
                      reads=(PBt[hh][1], mx_b), writes=(PBt[2 + hh][1],))
            sm, sm_b = p.S("smw", [128, 4], F32, scope=pre_es)
            p.op("ve", lambda e: e.reduce_sum(sm[:, 0:1], PBt[2][0][:, :], axis=AX.X),
                 reads=(PBt[2][1],), writes=(sm_b,))
            p.op("ve", lambda e: e.reduce_sum(sm[:, 1:2], PBt[3][0][:, :], axis=AX.X),
                 reads=(PBt[3][1],), writes=(sm_b,))
            p.fence("ve")
            p.op("ve", lambda e: e.tensor_add(sm[:, 2:3], sm[:, 0:1], sm[:, 1:2]), reads=(sm_b,), writes=(sm_b,))
            p.fence("ve")
            p.op("ve", lambda e: e.reciprocal(sm[:, 3:4], sm[:, 2:3]), reads=(sm_b,), writes=(sm_b,))
            p.fence("ve")
            if DBG and nm == "q":
                p.dump("smq", sm[:, :], (sm_b,))
                p.dump("mxq", mx[:, :], (mx_b,))
            wTb16, wTb16_b = p.S("wTb" + nm, [128, D], BF16, scope=pre_es)
            for hh in range(2):
                p.op("ve", lambda e, hh=hh: e.tensor_scalar_mul(wTb16[:, hh * 512:(hh + 1) * 512],
                                                                PBt[2 + hh][0][:, :], sm[:, 3:4]),
                     reads=(PBt[2 + hh][1], sm_b), writes=(wTb16_b,))
            nat, nat_b = p.S("wnat" + nm, [128, FC * 128], BF16)
            for ic in range(FC):
                p.op("pe", lambda e, ic=ic: e.transpose(PTb[0:128, 0:128],
                                                        wTb16[:, ic * 128:(ic + 1) * 128], idb[:, :]),
                     reads=(wTb16_b, idb_b), writes=(PTb_b,))
                p.op("ve", lambda e, ic=ic: e.tensor_copy(nat[:, ic * 128:(ic + 1) * 128], PTb[0:128, 0:128]),
                     reads=(PTb_b,), writes=(nat_b,))
            wnat[nm] = (nat, nat_b)
            if DBG and nm == "q":
                p.dump("basT", basT[:, :], (basT_b,))
                p.dump("cTq", cT[:, :], (cTb,))
                p.dump("wTq", wT[:, :], (wTb,))
                p.dump("wTb16q", wTb16[:, :], (wTb16_b,))
                p.dump("wnatq", nat[:, :], (nat_b,))

    winb, winb_b = p.S("winb", [128, FC * MF], BF16)
    p.op("ve", lambda e: e.tensor_copy(winb[:, :], winf[:, :]), reads=(winf_b,), writes=(winb_b,))
    woutb, woutb_b = p.S("woutb", [MF, D], BF16)
    p.op("ve", lambda e: e.tensor_copy(woutb[:, :], woutf[:, :]), reads=(woutf_b,), writes=(woutb_b,))

    pre_es.close()
    p.barrier()

    Mst, Mst_b = p.S("Mst", [128, 128], F32)
    Mbf, Mbf_b = p.S("Mbf", [128, 128], BF16)
    p.op("ve", lambda e: e.memset(Mst[:, :], 0.0), writes=(Mst_b,))
    p.op("ve", lambda e: e.memset(Mbf[:, :], 0.0), writes=(Mbf_b,))
    hcar, hcar_b = p.S("hcar", [128, FC], F32)
    p.op("ve", lambda e: e.memset(hcar[:, :], 0.0), writes=(hcar_b,))

    xab = [p.S("xa", [128, FC * TC], F32)]
    x1buf, x1buf_b = p.S("x1buf", [128, FC * TC], F32)
    xn, xn_b = p.S("xn", [128, FC * TC], BF16)
    s0, s0_b = p.S("s0", [128, FC * TC], F32)
    qT, qT_b = p.S("qT", [128, TC], BF16)
    kT, kT_b = p.S("kT", [128, TC], BF16)
    ktok, ktok_b = p.S("ktok", [128, TC], BF16)
    vtok, vtok_b = p.S("vtok", [128, TC], BF16)
    qtil, qtil_b = p.S("qtil", [128, 128], BF16)
    ktil, ktil_b = p.S("ktil", [128, 128], BF16)
    SpT, SpT_b = p.S("SpT", [128, 128], BF16)
    ret, ret_b = p.S("ret", [128, TC], BF16)
    Mtmp, Mtmp_b = p.S("Mtmp", [128, 128], F32)
    sF, sF_b = p.S("sF", [MF, TC], F32)
    gaccs = [p.S(f"gacc{i}", [MF, TC], F32) for i in range(4)]
    gbf, gbf_b = p.S("gbf", [MF, TC], BF16)
    xi_bf, xi_bf_b = p.S("xibf", [128, FC * TC], BF16)
    pool16a = p.S("p16a", [128, FC * TC], F32)
    pool16b = p.S("p16b", [128, FC * TC], F32)
    h_bf, h_bf_b = p.S("hbf", [128, FC * TC], BF16)
    tp = [p.S(f"tp{i}", [128, TC], F32) for i in range(8)]
    rstage, rstage_b = p.S("rstage", [128, TC], F32)
    tpi = [0]

    def TP():
        t = tp[tpi[0] % len(tp)]
        tpi[0] += 1
        return t

    rows_stats = p.S("rstat", [1, 4 * TC], F32)
    lg, lg_b = p.S("lg", [NEXP, TC], F32)
    ohb, ohb_b = p.S("ohb", [NEXP, TC], BF16)
    ltok, ltok_b = p.S("ltok", [128, 4 * NEXP], F32)
    mxc, mxc_b = p.S("mxc", [128, 4], F32)
    mxr, mxr_b = p.S("mxr", [1, TC], F32)
    wstr = [p.S("wstrA", [128, FC * D], BF16), p.S("wstrB", [128, FC * D], BF16)]

    def ln_apply(xa, xa_b, out_t, out_b, also_f32_cb=None):
        rs, rs_b = rows_stats
        for ic in range(FC):
            sq = TP()
            p.op("se", lambda e, ic=ic, sq=sq: e.square(sq[0][:, :], xa[:, ic * TC:(ic + 1) * TC]),
                 reads=(xa_b,), writes=(sq[1],))
            p.op("pe", lambda e, ic=ic: e.matmul(PBt[2][0][0:1, :], onc[:, 0:1], xa[:, ic * TC:(ic + 1) * TC],
                                                 start=(ic == 0), stop=(ic == FC - 1)),
                 reads=(xa_b, onc_b), writes=(PBt[2][1],))
            p.op("pe", lambda e, ic=ic, sq=sq: e.matmul(PBt[3][0][0:1, :], onc[:, 0:1], sq[0][:, :],
                                                        start=(ic == 0), stop=(ic == FC - 1)),
                 reads=(sq[1], onc_b), writes=(PBt[3][1],))
        p.op("ve", lambda e: e.tensor_scalar_mul(rs[0:1, 0:TC], PBt[2][0][0:1, :], 1.0 / D),
             reads=(PBt[2][1],), writes=(rs_b,))
        p.op("ve", lambda e: e.tensor_scalar_mul(rs[0:1, TC:2 * TC], PBt[3][0][0:1, :], 1.0 / D),
             reads=(PBt[3][1],), writes=(rs_b,))
        p.op("ve", lambda e: e.tensor_tensor(rs[0:1, 2 * TC:3 * TC], rs[0:1, 0:TC], rs[0:1, 0:TC], OP.mult),
             reads=(rs_b,), writes=(rs_b,))
        p.op("ve", lambda e: e.tensor_sub(rs[0:1, 3 * TC:4 * TC], rs[0:1, TC:2 * TC], rs[0:1, 2 * TC:3 * TC]),
             reads=(rs_b,), writes=(rs_b,))
        p.op("ve", lambda e: e.tensor_scalar_add(rs[0:1, 2 * TC:3 * TC], rs[0:1, 3 * TC:4 * TC], 1e-5),
             reads=(rs_b,), writes=(rs_b,))
        p.act(AF.Ln, PBt[2][0][0:1, :], rs[0:1, 2 * TC:3 * TC], reads=(rs_b,), writes=(PBt[2][1],))
        p.act(AF.Exp, PBt[3][0][0:1, :], PBt[2][0][0:1, :], scale=-0.5,
              reads=(PBt[2][1],), writes=(PBt[3][1],))
        p.op("ve", lambda e: e.tensor_copy(rs[0:1, 3 * TC:4 * TC], PBt[3][0][0:1, :]),
             reads=(PBt[3][1],), writes=(rs_b,))
        p.op("pe", lambda e: e.matmul(PBt[2][0][:, :], onr[0:1, :], rs[0:1, 0:TC],
                                      start=True, stop=True), reads=(rs_b, onr_b), writes=(PBt[2][1],))
        p.op("pe", lambda e: e.matmul(PBt[3][0][:, :], onr[0:1, :], rs[0:1, 3 * TC:4 * TC],
                                      start=True, stop=True), reads=(rs_b, onr_b), writes=(PBt[3][1],))
        for ic in range(FC):
            t1 = TP()
            p.op("ve", lambda e, ic=ic, t1=t1: e.tensor_tensor(t1[0][:, :], xa[:, ic * TC:(ic + 1) * TC],
                                                               PBt[2][0][:, :], OP.subtract),
                 reads=(xa_b, PBt[2][1]), writes=(t1[1],))
            if also_f32_cb is None:
                p.op("ve", lambda e, ic=ic, t1=t1: e.tensor_tensor(out_t[:, ic * TC:(ic + 1) * TC],
                                                                   t1[0][:, :], PBt[3][0][:, :], OP.mult),
                     reads=(t1[1], PBt[3][1]), writes=(out_b,))
            else:
                tf = TP()
                p.op("ve", lambda e, t1=t1, tf=tf: e.tensor_tensor(tf[0][:, :], t1[0][:, :],
                                                                   PBt[3][0][:, :], OP.mult),
                     reads=(t1[1], PBt[3][1]), writes=(tf[1],))
                p.op("ve", lambda e, ic=ic, tf=tf: e.tensor_copy(out_t[:, ic * TC:(ic + 1) * TC], tf[0][:, :]),
                     reads=(tf[1],), writes=(out_b,))
                also_f32_cb(ic, tf)

    def load_x(c, slot):
        xt, xb = xab[slot]
        p.dma("sy", xt[:, :].rearrange("p (fc t) -> p fc t", fc=FC),
              xT.rearrange("(fc p) t -> p fc t", p=128)[:, :, c * TC:(c + 1) * TC],
              reads=(xT_b,), writes=(xb,))

    def load_w(nm, slot):
        wt, wb = wstr[slot]
        p.dma("sy", wt[:, :], l1wbf[nm].ap()[:, :],
              reads=(l1wbf_bufs[nm],), writes=(wb,))
        return wt, wb

    xa, xa_b = xab[0]

    def emit_ln0():
        ln_apply(xa, xa_b, xn, xn_b)

    def qkv_units():
        units = []

        def qk_unit(dst, dstb, nm, bk):
            nat, nat_b = wnat[nm]
            for ic in range(FC):
                p.op("pe", lambda e, ic=ic, nat=nat, bk=bk: e.matmul(
                    PBt[bk][0][:, :], nat[:, ic * 128:(ic + 1) * 128],
                    xn[:, ic * TC:(ic + 1) * TC],
                    start=(ic == 0), stop=(ic == FC - 1)),
                     reads=(nat_b, xn_b), writes=(PBt[bk][1],))
            p.op("se", lambda e, dst=dst, bk=bk: e.copy(dst[:, :], PBt[bk][0][:, :]),
                 reads=(PBt[bk][1],), writes=(dstb,))

        def tok_unit(dst, dstb, nm, s):
            nat, nat_b = wnat[nm]
            pb = PBt[s % 2]
            for ic in range(FC):
                p.op("pe", lambda e, s=s, ic=ic, nat=nat, pb=pb: e.matmul(
                    pb[0][:, 0:128], xn[:, ic * TC + s * SUB: ic * TC + (s + 1) * SUB],
                    nat[:, ic * 128:(ic + 1) * 128],
                    start=(ic == 0), stop=(ic == FC - 1)),
                    reads=(nat_b, xn_b), writes=(pb[1],))
            p.op("se", lambda e, s=s, dst=dst, pb=pb: e.copy(dst[:, s * SUB:(s + 1) * SUB],
                                                             pb[0][:, 0:128]),
                 reads=(pb[1],), writes=(dstb,))

        for dst, dstb, nm, bk in ((qT, qT_b, "q", 0), (kT, kT_b, "k", 1)):
            units.append(lambda dst=dst, dstb=dstb, nm=nm, bk=bk: qk_unit(dst, dstb, nm, bk))
        for dst, dstb, nm in ((ktok, ktok_b, "k"), (vtok, vtok_b, "v")):
            for s in range(TC // SUB):
                units.append(lambda dst=dst, dstb=dstb, nm=nm, s=s: tok_unit(dst, dstb, nm, s))
        return units

    def emit_qkv():
        for u in qkv_units():
            u()

    load_x(0, 0)
    emit_ln0()
    emit_qkv()
    for c in range(NCH):
        if DBG and c == 0:
            p.dump("xa0", xa[:, :], (xa_b,))
            p.dump("xn0", xn[:, :], (xn_b,))
        if STAGE < 2:
            continue
        if DBG and c == 0:
            p.dump("qT0", qT[:, :], (qT_b,))
            p.dump("kT0", kT[:, :], (kT_b,))
            p.dump("ktok0", ktok[:, :], (ktok_b,))
            p.dump("vtok0", vtok[:, :], (vtok_b,))
        if STAGE < 5:
            continue
        for ic in range(FC):
            p.op("pe", lambda e, ic=ic: e.matmul(PBt[0][0][0:MF, :], winb[:, ic * MF:(ic + 1) * MF],
                                                 xn[:, ic * TC:(ic + 1) * TC],
                                                 start=(ic == 0), stop=(ic == FC - 1)),
                 reads=(winb_b, xn_b), writes=(PBt[0][1],))
        p.op("se", lambda e: e.activation(sF[:, :], PBt[0][0][0:MF, :], AF.Identity, bias=binc[0:MF, 0:1]),
             reads=(PBt[0][1], binc_b), writes=(sF_b,))
        if STAGE < 4.2:
            continue
        for h in range(INNER):
            t1 = TP()
            p.op("ve", lambda e, h=h, t1=t1: e.tensor_scalar(
                t1[0][0:MF, :], sF[:, :], w1c[:, h:h + 1], b1c[:, h:h + 1], OP.mult, OP.add),
                 reads=(sF_b, w1c_b, b1c_b), writes=(t1[1],))
            t2 = TP()
            p.op("ve", lambda e, t1=t1, t2=t2: e.tensor_scalar_max(t2[0][0:MF, :], t1[0][0:MF, :], 0.0),
                 reads=(t1[1],), writes=(t2[1],))
            ga_o, ga_ob = gaccs[h % 2]
            ga_i, ga_ib = gaccs[(h + 1) % 2]
            if h == 0:
                p.op("ve", lambda e, t2=t2, ga_o=ga_o: e.tensor_scalar_mul(ga_o[:, :], t2[0][0:MF, :],
                                                                           w2c[:, 0:1]),
                     reads=(t2[1], w2c_b), writes=(ga_ob,))
            else:
                p.op("ve", lambda e, h=h, t2=t2, ga_o=ga_o, ga_i=ga_i: e.scalar_tensor_tensor(
                    ga_o[:, :], t2[0][0:MF, :], w2c[:, h:h + 1], ga_i[:, :], OP.mult, OP.add),
                     reads=(t2[1], w2c_b, ga_ib), writes=(ga_ob,))
        if STAGE < 4.3:
            continue
        gfin, gfin_b = gaccs[(INNER - 1) % 2]
        p.act(AF.Identity, gbf[:, :], gfin[:, :], bias=b2cc[0:MF, 0:1],
              reads=(gfin_b, b2cc_b), writes=(gbf_b,))
        if STAGE < 3:
            continue
        for s in range(TC // SUB):
            sl = slice(s * SUB, (s + 1) * SUB)
            p.op("pe", lambda e, sl=sl: e.matmul(PBt[4][0][:, 0:128], kT[:, sl], qT[:, sl],
                                                 start=True, stop=True),
                 reads=(kT_b, qT_b), writes=(PBt[4][1],))
            p.op("ve", lambda e: e.tensor_tensor(SpT[:, :], PBt[4][0][:, 0:128], Dmask[:, :], OP.mult),
                 reads=(PBt[4][1], Dmask_b), writes=(SpT_b,))
            p.op("ve", lambda e, sl=sl: e.tensor_tensor(qtil[:, :], qT[:, sl], dprow[:, :], OP.mult),
                 reads=(qT_b, dprow_b), writes=(qtil_b,))
            p.op("pe", lambda e, sl=sl: e.matmul(PBt[5][0][:, 0:128], vtok[:, sl], SpT[:, :],
                                                 start=True, stop=False),
                 reads=(vtok_b, SpT_b), writes=(PBt[5][1],))
            p.op("pe", lambda e: e.matmul(PBt[5][0][:, 0:128], Mbf[:, :], qtil[:, :],
                                          start=False, stop=True),
                 reads=(Mbf_b, qtil_b), writes=(PBt[5][1],))
            p.op("se", lambda e, sl=sl: e.copy(ret[:, sl], PBt[5][0][:, 0:128]),
                 reads=(PBt[5][1],), writes=(ret_b,))
            p.op("ve", lambda e, sl=sl: e.tensor_scalar_mul(ktil[:, :], ktok[:, sl], dpcol[:, 0:1]),
                 reads=(ktok_b, dpcol_b), writes=(ktil_b,))
            p.op("pe", lambda e, sl=sl: e.matmul(PBt[4][0][:, 0:128], ktil[:, :], vtok[:, sl],
                                                 start=True, stop=True),
                 reads=(ktil_b, vtok_b), writes=(PBt[4][1],))
            p.op("ve", lambda e: e.tensor_scalar_mul(Mtmp[:, :], Mst[:, :], dTc[:, 0:1]),
                 reads=(Mst_b, dTc_b), writes=(Mtmp_b,))
            p.op("ve", lambda e: e.tensor_tensor(Mst[:, :], Mtmp[:, :], PBt[4][0][:, 0:128], OP.add),
                 reads=(Mtmp_b, PBt[4][1]), writes=(Mst_b,))
            p.op("se", lambda e: e.copy(Mbf[:, :], Mst[:, :]), reads=(Mst_b,), writes=(Mbf_b,))
        if STAGE < 4:
            continue
        for vcB in range(FC):
            pb = PBt[vcB % 2]
            p.op("pe", lambda e, vcB=vcB, pb=pb: e.matmul(pb[0][:, :], owT[:, vcB * 128:(vcB + 1) * 128],
                                                          ret[:, :], start=True, stop=True),
                 reads=(owT_b, ret_b), writes=(pb[1],))
            p.op("se", lambda e, vcB=vcB, pb=pb: e.copy(s0[:, vcB * TC:(vcB + 1) * TC], pb[0][:, :]),
                 reads=(pb[1],), writes=(s0_b,))

        if DBG and c == 0:
            p.dump("ret0", ret[:, :], (ret_b,))
            p.dump("s00", s0[:, :], (s0_b,))
            p.dump("Dmask", Dmask[:, :], (Dmask_b,))
            p.dump("dprow", dprow[:, :], (dprow_b,))
        if STAGE < 4.4:
            continue
        for dc in range(FC):
            pb = PBt[dc % 2]
            p.op("pe", lambda e, dc=dc, pb=pb: e.matmul(pb[0][:, :], woutb[:, dc * 128:(dc + 1) * 128],
                                                        gbf[:, :], start=True, stop=True),
                 reads=(woutb_b, gbf_b), writes=(pb[1],))
            sg = PBt[2 + dc % 2]
            p.act(AF.Silu, sg[0][:, :], pb[0][:, :], bias=boutT[:, dc:dc + 1],
                  reads=(pb[1], boutT_b), writes=(sg[1],))
            tm = TP()
            p.op("ve", lambda e, dc=dc, sg=sg, tm=tm: e.tensor_tensor(tm[0][:, :], s0[:, dc * TC:(dc + 1) * TC],
                                                                      sg[0][:, :], OP.mult),
                 reads=(s0_b, sg[1]), writes=(tm[1],))
            p.op("ve", lambda e, dc=dc, tm=tm: e.tensor_tensor(x1buf[:, dc * TC:(dc + 1) * TC],
                                                               xa[:, dc * TC:(dc + 1) * TC], tm[0][:, :], OP.add),
                 reads=(tm[1], xa_b), writes=(x1buf_b,))
        if STAGE < 4.5:
            continue
        p.dma("se", x1ap.rearrange("(fc p) t -> p fc t", p=128)[:, :, c * TC:(c + 1) * TC],
              x1buf[:, :].rearrange("p (fc t) -> p fc t", fc=FC), reads=(x1buf_b,), writes=(x1scr_b,))
        if c + 1 < NCH:
            load_x(c + 1, 0)
        if DBG and c == 0:
            p.dump("sF0", sF[:, :], (sF_b,))
            p.dump("gacc0", gfin[:, :], (gfin_b,))
            p.dump("x1c0", x1buf[:, :], (x1buf_b,))

        if STAGE < 6:
            continue
        def logits_cb(ic, tf):
            p.op("pe", lambda e, ic=ic, tf=tf: e.matmul(PSm[0:NEXP, :], wgt[:, ic * NEXP:(ic + 1) * NEXP],
                                                        tf[0][:, :], start=(ic == 0), stop=(ic == FC - 1)),
                 reads=(wgt_b, tf[1]), writes=(PSm_b,))

        ln_apply(x1buf, x1buf_b, xn, xn_b, also_f32_cb=logits_cb)
        p.op("ve", lambda e: e.tensor_scalar(lg[:, :], PSm[0:NEXP, :], bgc[0:NEXP, 0:1], None, OP.add),
             reads=(PSm_b, bgc_b), writes=(lg_b,))
        p.dma("se", xn1ap.rearrange("(fc p) t -> p fc t", p=128)[:, :, c * TC:(c + 1) * TC],
              xn[:, :].rearrange("p (fc t) -> p fc t", fc=FC), reads=(xn_b,), writes=(xn1scr_b,))

        if STAGE < 7:
            continue
        for s in range(TC // SUB):
            p.op("pe", lambda e, s=s: e.transpose(PBt[4][0][0:128, 0:NEXP],
                                                  lg[:, s * SUB:(s + 1) * SUB], idf[0:NEXP, 0:NEXP]),
                 reads=(lg_b, idf_b), writes=(PBt[4][1],))
            p.op("ve", lambda e, s=s: e.tensor_copy(ltok[:, s * NEXP:(s + 1) * NEXP], PBt[4][0][0:128, 0:NEXP]),
                 reads=(PBt[4][1],), writes=(ltok_b,))
            p.op("ve", lambda e, s=s: e.reduce_max(mxc[:, s:s + 1], ltok[:, s * NEXP:(s + 1) * NEXP], axis=AX.X),
                 reads=(ltok_b,), writes=(mxc_b,))
            p.op("pe", lambda e, s=s: e.transpose(PBt[4][0][0:1, 0:128], mxc[:, s:s + 1], idf[:, :]),
                 reads=(mxc_b, idf_b), writes=(PBt[4][1],))
            p.op("ve", lambda e, s=s: e.tensor_copy(mxr[0:1, s * SUB:(s + 1) * SUB], PBt[4][0][0:1, 0:128]),
                 reads=(PBt[4][1],), writes=(mxr_b,))
        p.op("pe", lambda e: e.matmul(PBt[4][0][0:NEXP, :], onr[0:1, 0:NEXP], mxr[0:1, :],
                                      start=True, stop=True), reads=(onr_b, mxr_b), writes=(PBt[4][1],))
        p.op("ve", lambda e: e.tensor_tensor(ohb[:, :], lg[:, :], PBt[4][0][0:NEXP, :], OP.is_ge),
             reads=(lg_b, PBt[4][1]), writes=(ohb_b,))
        p.dma("se", ohap[:, c * TC:(c + 1) * TC], ohb[:, :], reads=(ohb_b,), writes=(ohscr_b,))

        if STAGE < 8:
            continue
        xi_f32, xi_f32_b = pool16a
        xg_all, xg_all_b = pool16b
        wt, wb = load_w("W_in", 0)
        for oc2 in range(FC):
            pb = PBt[oc2 % 2]
            for ic in range(FC):
                p.op("pe", lambda e, oc2=oc2, ic=ic, pb=pb, wt=wt: e.matmul(
                    pb[0][:, :], wt[:, ic * D + oc2 * 128: ic * D + (oc2 + 1) * 128],
                    xn[:, ic * TC:(ic + 1) * TC], start=(ic == 0), stop=(ic == FC - 1)),
                    reads=(wb, xn_b), writes=(pb[1],))
            p.act(AF.Identity, xi_f32[:, oc2 * TC:(oc2 + 1) * TC], pb[0][:, :],
                  bias=biasT["b_in"][0][:, oc2:oc2 + 1],
                  reads=(pb[1], biasT["b_in"][1]), writes=(xi_f32_b,))
            p.act(AF.Identity, xi_bf[:, oc2 * TC:(oc2 + 1) * TC], pb[0][:, :],
                  bias=biasT["b_in"][0][:, oc2:oc2 + 1],
                  reads=(pb[1], biasT["b_in"][1]), writes=(xi_bf_b,))
        if STAGE < 9:
            continue
        wt2, wb2 = load_w("W_gx", 1)
        for oc2 in range(FC):
            pb = PBt[oc2 % 2]
            for ic in range(FC):
                p.op("pe", lambda e, oc2=oc2, ic=ic, pb=pb, wt2=wt2: e.matmul(
                    pb[0][:, :], wt2[:, ic * D + oc2 * 128: ic * D + (oc2 + 1) * 128],
                    xi_bf[:, ic * TC:(ic + 1) * TC], start=(ic == 0), stop=(ic == FC - 1)),
                    reads=(wb2, xi_bf_b), writes=(pb[1],))
            sg = PBt[2 + oc2 % 2]
            p.act(AF.Sigmoid, sg[0][:, :], pb[0][:, :], bias=biasT["b_gx"][0][:, oc2:oc2 + 1],
                  reads=(pb[1], biasT["b_gx"][1]), writes=(sg[1],))
            p.op("ve", lambda e, oc2=oc2, sg=sg: e.tensor_tensor(xg_all[:, oc2 * TC:(oc2 + 1) * TC],
                                                                 sg[0][:, :],
                                                                 xi_f32[:, oc2 * TC:(oc2 + 1) * TC], OP.mult),
                 reads=(sg[1], xi_f32_b), writes=(xg_all_b,))
        if STAGE < 10:
            continue
        a_all, a_all_b = pool16a
        ga_all, ga_all_b = a_all, a_all_b
        wt3, wb3 = load_w("W_ga", 0)
        for oc2 in range(FC):
            pb = PBt[oc2 % 2]
            for ic in range(FC):
                p.op("pe", lambda e, oc2=oc2, ic=ic, pb=pb, wt3=wt3: e.matmul(
                    pb[0][:, :], wt3[:, ic * D + oc2 * 128: ic * D + (oc2 + 1) * 128],
                    xi_bf[:, ic * TC:(ic + 1) * TC], start=(ic == 0), stop=(ic == FC - 1)),
                    reads=(wb3, xi_bf_b), writes=(pb[1],))
            sg = PBt[2 + oc2 % 2]
            p.act(AF.Sigmoid, sg[0][:, :], pb[0][:, :], bias=biasT["b_ga"][0][:, oc2:oc2 + 1],
                  reads=(pb[1], biasT["b_ga"][1]), writes=(sg[1],))
            p.op("ve", lambda e, oc2=oc2, sg=sg: e.tensor_scalar(ga_all[:, oc2 * TC:(oc2 + 1) * TC],
                                                                 sg[0][:, :], sp8[:, oc2:oc2 + 1],
                                                                 None, OP.mult),
                 reads=(sg[1], sp8_b), writes=(ga_all_b,))
        if STAGE < 11:
            continue
        if c + 1 < NCH and STAGE >= 99:
            emit_ln0()
            qunits = qkv_units()
        else:
            qunits = []
        for oc2 in range(FC):
            osl = slice(oc2 * TC, (oc2 + 1) * TC)
            p.act(AF.Exp, s0[:, osl], ga_all[:, osl], reads=(ga_all_b,), writes=(s0_b,))
            p.act(AF.Exp, tp[oc2][0][:, :], ga_all[:, osl], scale=2.0,
                  reads=(ga_all_b,), writes=(tp[oc2][1],))
        for oc2 in range(FC):
            osl = slice(oc2 * TC, (oc2 + 1) * TC)
            p.act(AF.Relu, rstage[:, :], tp[oc2][0][:, :], bias=1.0, scale=-1.0,
                  reads=(tp[oc2][1],), writes=(rstage_b,))
            p.act(AF.Sqrt, ga_all[:, osl], rstage[:, :], reads=(rstage_b,), writes=(ga_all_b,))
            p.op("ve", lambda e, osl=osl, oc2=oc2: e.tensor_tensor(tp[oc2][0][:, :], ga_all[:, osl],
                                                                   xg_all[:, osl], OP.mult),
                 reads=(ga_all_b, xg_all_b), writes=(tp[oc2][1],))
            p.op("ve", lambda e, osl=osl, oc2=oc2: e.tensor_tensor_scan(
                ga_all[:, osl], s0[:, osl], tp[oc2][0][:, :],
                hcar[:, oc2:oc2 + 1], OP.mult, OP.add),
                reads=(s0_b, tp[oc2][1], hcar_b), writes=(ga_all_b,))
            p.op("ve", lambda e, oc2=oc2: e.tensor_copy(hcar[:, oc2:oc2 + 1],
                                                        ga_all[:, oc2 * TC + TC - 1:oc2 * TC + TC]),
                 reads=(ga_all_b,), writes=(hcar_b,))
            p.op("ve", lambda e, oc2=oc2, osl=osl: e.tensor_copy(h_bf[:, osl], ga_all[:, osl]),
                 reads=(ga_all_b,), writes=(h_bf_b,))
        if STAGE < 12:
            continue
        wt4, wb4 = load_w("W_out", 1)
        for u in qunits:
            u()
        for oc2 in range(FC):
            pb = PBt[4 + oc2 % 2]
            for ic in range(FC):
                p.op("pe", lambda e, oc2=oc2, ic=ic, pb=pb, wt4=wt4: e.matmul(
                    pb[0][:, :], wt4[:, ic * D + oc2 * 128: ic * D + (oc2 + 1) * 128],
                    h_bf[:, ic * TC:(ic + 1) * TC], start=(ic == 0), stop=(ic == FC - 1)),
                    reads=(wb4, h_bf_b), writes=(pb[1],))
            s2t = TP()
            p.act(AF.Identity, s2t[0][:, :], pb[0][:, :], bias=biasT["b_out"][0][:, oc2:oc2 + 1],
                  reads=(pb[1], biasT["b_out"][1]), writes=(s2t[1],))
            p.dma("se", s2ap[oc2 * 128:(oc2 + 1) * 128, c * TC:(c + 1) * TC], s2t[0][:, :],
                  reads=(s2t[1],), writes=(s2w[oc2 % 4],))


def build_p2_cast(nc, p, w1, w2):
    w1bf_t, _ = p.DR("w1bfT", [NEXP, 128, (HDIM // 512) * FC * 512], BF16)
    w2bf_t, _ = p.DR("w2bfT", [NEXP, 128, 32 * D], BF16)
    w1bf, w2bf = w1bf_t.ap(), w2bf_t.ap()
    w1bufs = [Buf() for _ in range(NEXP)]
    w2bufs = [Buf() for _ in range(NEXP)]
    cast_tile = p.cast_tile

    for e_ in range(NEXP):
        for r in range(D // 128):
            for cc in range(HDIM // 512):
                dst = w1bf[e_, :, (cc * FC + r) * 512:(cc * FC + r + 1) * 512]
                cast_tile(w1[e_, r * 128:(r + 1) * 128, cc * 512:(cc + 1) * 512],
                          dst, w1bufs[e_])
        for r in range(HDIM // 128):
            for cc in range(D // 512):
                dst = w2bf[e_, :, r * D + cc * 512:r * D + (cc + 1) * 512]
                cast_tile(w2[e_, r * 128:(r + 1) * 128, cc * 512:(cc + 1) * 512],
                          dst, w2bufs[e_])
    p.cast_flush()
    return w1bf, w1bufs, w2bf, w2bufs


def build_p2(nc, p, dram, moebf=None, ext_w=None):

    def ext_in(nm, shape, dt=F32):
        return nc.dram_tensor(nm, shape, dt, kind="ExternalInput").ap(), Buf()

    x1ap, x1b = dram["x1scr"][0].ap(), dram["x1scr"][1]
    s2ap, s2b = dram["s2scr"][0].ap(), dram["s2scr"][1]
    xn1ap, xn1b = dram["xn1scr"][0].ap(), dram["xn1scr"][1]
    ohap, oh_b = dram["ohscr"][0].ap(), dram["ohscr"][1]

    if ext_w is None:
        w1, w1_b = ext_in("L1_ffn_W1", [NEXP, D, HDIM])
        w2, w2_b = ext_in("L1_ffn_W2", [NEXP, HDIM, D])
    else:
        w1, w2 = ext_w
        w1_b, w2_b = Buf(), Buf()
    out, out_b = nc.dram_tensor("xT2", [D, T], F32, kind="ExternalOutput").ap(), Buf()

    if moebf is None:
        w1bf, w1bufs, w2bf, w2bufs = build_p2_cast(nc, p, w1, w2)
    else:
        w1bf, w1bufs, w2bf, w2bufs = moebf

    onrb2, onrb2_b = p.S("onrb2", [128, 128], BF16)
    orow = nc.dram_tensor("ones_row_bf2", [1, 128], BF16, kind="ExternalInput").ap()
    p.dma("sy", onrb2[0:1, :], orow[:, :], writes=(onrb2_b,))
    p.dma("sy", onrb2[64:65, :], orow[:, :], writes=(onrb2_b,))

    PB = [p.PS(f"p2pb{i}", [128, 512]) for i in range(6)]

    w2sb, w2sb_b = p.S("w2sb", [128, 32 * D], BF16)
    w2qb = [Buf() for _ in range(4)]
    ybufs = [p.S("ybufA", [128, FC * (T // 2)], BF16), p.S("ybufB", [128, FC * (T // 2)], BF16)]
    h_fm, h_fm_b = p.S("hfm", [128, 32 * 512], BF16)
    w1t = [p.S("w1tA", [128, FC * HC], BF16), p.S("w1tB", [128, FC * HC], BF16)]
    xn1s, xn1s_b = p.S("xn1s", [128, FC * 512], BF16)
    xnm, xnm_b = p.S("xnm", [128, FC * 512], BF16)
    mrowt, _mb = p.S("mrow", [128, 512], BF16)
    mrow = [(mrowt[0:1, :], Buf()), (mrowt[64:65, :], Buf())]
    x1t = [p.S("x1tA", [128, 512], F32)]
    s2t = [p.S("s2tA", [128, 512], F32)]
    ctp = [p.S(f"ctp{i}", [128, 512], F32) for i in range(2)]
    outw = [Buf() for _ in range(4)]

    NTS = T // 512
    units = [(th, e_, ts) for th in range(2) for e_ in range(NEXP)
             for ts in range(th * (NTS // 2), (th + 1) * (NTS // 2))]

    def emit_w2load(e_):
        for qq in range(4):
            p.dma("sy", w2sb[:, qq * 8 * D:(qq + 1) * 8 * D],
                  w2bf[e_, :, qq * 8 * D:(qq + 1) * 8 * D],
                  reads=(w2bufs[e_],), writes=(w2qb[qq],))

    def emit_front(u):
        th, e_, ts = units[u]
        tsl = slice(ts * 512, (ts + 1) * 512)
        p.dma("sy", xn1s[:, :].rearrange("p (fc t) -> p fc t", fc=FC),
              xn1ap.rearrange("(fc p) t -> p fc t", p=128)[:, :, tsl],
              reads=(xn1b,), writes=(xn1s_b,))
        mr, mr_b = mrow[u % 2]
        p.dma("sy", mr[:, :], ohap[e_:e_ + 1, tsl], reads=(oh_b,), writes=(mr_b,))
        obase = 64 * (u % 2)
        p.op("pe", lambda e, mr=mr, obase=obase: e.matmul(
            PB[4][0][:, :], onrb2[obase:obase + 1, :], mr[0:1, :],
            start=True, stop=True),
             reads=(onrb2_b, mr_b), writes=(PB[4][1],))
        for ic in range(FC):
            p.op("ve", lambda e, ic=ic: e.tensor_tensor(xnm[:, ic * 512:(ic + 1) * 512],
                                                        xn1s[:, ic * 512:(ic + 1) * 512],
                                                        PB[4][0][:, :], OP.mult),
                 reads=(xn1s_b, PB[4][1]), writes=(xnm_b,))
        wt, wb = w1t[0]
        p.dma("se", wt[:, :], w1bf[e_, :, 0:FC * 512], reads=(w1bufs[e_],), writes=(wb,))

    for u, (th, e_, ts) in enumerate(units):
        tsl = slice(ts * 512, (ts + 1) * 512)
        if u == 0:
            emit_front(0)
        if ts == th * (NTS // 2):
            emit_w2load(e_)
        for hc in range(HDIM // HC):
            wt, wb = w1t[hc % 2]
            if hc + 1 < HDIM // HC:
                nwt, nwb = w1t[(hc + 1) % 2]
                p.dma("se", nwt[:, :], w1bf[e_, :, (hc + 1) * FC * 512:(hc + 2) * FC * 512],
                      reads=(w1bufs[e_],), writes=(nwb,))
            for hs in range(HC // 128):
                pb = PB[(hc * 4 + hs) % 2]
                for ic in range(FC):
                    p.op("pe", lambda e, hc=hc, hs=hs, ic=ic, pb=pb, wt=wt: e.matmul(
                        pb[0][:, :], wt[:, ic * HC + hs * 128: ic * HC + (hs + 1) * 128],
                        xnm[:, ic * 512:(ic + 1) * 512],
                        start=(ic == 0), stop=(ic == FC - 1)),
                        reads=(wb, xnm_b), writes=(pb[1],))
                p.act(AF.Gelu, h_fm[:, (hc * 4 + hs) * 512:(hc * 4 + hs + 1) * 512], pb[0][:, :],
                      reads=(pb[1],), writes=(h_fm_b,))
        if u + 1 < len(units):
            emit_front(u + 1)
        for dc in range(FC):
            pb = PB[2 + dc % 2]
            for ht in range(32):
                p.op("pe", lambda e, dc=dc, ht=ht, pb=pb: e.matmul(
                    pb[0][:, :], w2sb[:, ht * D + dc * 128: ht * D + (dc + 1) * 128],
                    h_fm[:, ht * 512:(ht + 1) * 512],
                    start=(ht == 0), stop=(ht == 31)),
                    reads=(w2qb[ht // 8], h_fm_b), writes=(pb[1],))
            yofs = dc * (T // 2) + (ts % (NTS // 2)) * 512
            ydst, ydst_b = ybufs[e_ % 2]
            ysrc, ysrc_b = ybufs[(e_ + 1) % 2]
            ysl = ydst[:, yofs: yofs + 512]
            if e_ == 0:
                p.op("ve", lambda e, ysl=ysl, pb=pb: e.tensor_copy(ysl, pb[0][:, :]),
                     reads=(pb[1],), writes=(ydst_b,))
            else:
                p.op("ve", lambda e, ysl=ysl, pb=pb, ysrc=ysrc, yofs=yofs: e.tensor_tensor(
                    ysl, pb[0][:, :], ysrc[:, yofs: yofs + 512], OP.add),
                     reads=(pb[1], ysrc_b), writes=(ydst_b,))
            if e_ == NEXP - 1:
                xt, xtb = x1t[0]
                st, stb = s2t[0]
                p.dma("sy", xt[:, :], x1ap[dc * 128:(dc + 1) * 128, tsl], reads=(x1b,), writes=(xtb,))
                p.dma("sy", st[:, :], s2ap[dc * 128:(dc + 1) * 128, tsl], reads=(s2b,), writes=(stb,))
                c0 = ctp[0]
                c1 = ctp[1]
                p.act(AF.Tanh, PB[5][0][:, :], ysl, scale=0.5, reads=(ydst_b,), writes=(PB[5][1],))
                p.op("ve", lambda e, ysl=ysl, c0=c0, st=st: e.tensor_tensor(c0[0][:, :], st[:, :],
                                                                            ysl, OP.mult),
                     reads=(ydst_b, stb), writes=(c0[1],))
                p.op("ve", lambda e, c0=c0, c1=c1: e.tensor_tensor(c1[0][:, :], c0[0][:, :],
                                                                   PB[5][0][:, :], OP.mult),
                     reads=(c0[1], PB[5][1]), writes=(c1[1],))
                p.op("ve", lambda e, c0=c0, st=st, xt=xt: e.scalar_tensor_tensor(st[:, :], c0[0][:, :],
                                                                                 0.5, xt[:, :],
                                                                                 OP.mult, OP.add),
                     reads=(c0[1], xtb), writes=(stb,))
                p.op("ve", lambda e, c1=c1, st=st, c0=c0: e.scalar_tensor_tensor(c0[0][:, :], c1[0][:, :],
                                                                                 0.5, st[:, :],
                                                                                 OP.mult, OP.add),
                     reads=(c1[1], stb), writes=(c0[1],))
                p.dma("sy", out[dc * 128:(dc + 1) * 128, tsl], c0[0][:, :],
                      reads=(c0[1],), writes=(outw[dc % 4],))


def _mk_dram(p, external):
    kind = "ExternalOutput" if external == "out" else ("ExternalInput" if external == "in" else "Internal")
    dram = {}
    specs = (("x1scr", [D, T], F32), ("s2scr", [D, T], F32),
             ("xn1scr", [D, T], BF16), ("ohscr", [NEXP, T], BF16))
    for nm, shape, dt_ in specs:
        if external == "internal":
            t, b = p.DR(nm, shape, dt_, kind="Internal")
        else:
            t = p.nc.dram_tensor(nm, shape, dt_, kind=kind)
            b = Buf()
        dram[nm] = (t, b)
    return dram


def _build(mode):
    nc = bass.Bass("TRN2", target_bir_lowering=False)
    p = Prog(nc)
    p.setup_act_guard()
    if mode == "p1":
        dram = _mk_dram(p, "out")
        build_p1(nc, p, dram)
    elif mode == "p2":
        dram = _mk_dram(p, "in")
        build_p2(nc, p, dram)
    else:
        dram = _mk_dram(p, "internal")
        scope1 = ExitStack()
        p.default_scope = scope1
        build_p1(nc, p, dram)
        p.default_scope = None
        p.fence("po")
        snap = {s: v for s, v in p.cnt.items() if v > 0}
        for e in Prog.ENGS:
            if e == "po":
                continue
            for s, v in snap.items():
                p._wait(e, s, v)
        w1e = nc.dram_tensor("L1_ffn_W1", [NEXP, D, HDIM], F32, kind="ExternalInput").ap()
        w2e = nc.dram_tensor("L1_ffn_W2", [NEXP, HDIM, D], F32, kind="ExternalInput").ap()
        moebf = build_p2_cast(nc, p, w1e, w2e)
        scope1.close()
        p.extra_dep = dict(snap)
        build_p2(nc, p, dram, moebf=moebf, ext_w=(w1e, w2e))
    p.final_wait()
    p.es.close()
    return nc


P1_KEYS = ("xT", "L0_attn_basis", "L0_attn_q_coeffs", "L0_attn_k_coeffs",
           "L0_attn_v_coeffs", "L0_attn_o_coeffs", "L0_attn_decay",
           "L0_attn_out_scale", "L0_ffn_Win", "L0_ffn_bin_", "L0_ffn_W1",
           "L0_ffn_b1", "L0_ffn_W2", "L0_ffn_b2", "L0_ffn_Wout", "L0_ffn_bout",
           "L1_attn_W_in", "L1_attn_b_in", "L1_attn_W_gx", "L1_attn_b_gx",
           "L1_attn_W_ga", "L1_attn_b_ga", "L1_attn_a_param", "L1_attn_W_out",
           "L1_attn_b_out", "L1_ffn_Wg", "L1_ffn_bg",
           "ident_f32", "ident_bf", "ones_col", "ones_row", "ones_row_bfs",
           "iotaE", "trimask", "iotarow", "revcol")
P2_KEYS = ("L1_ffn_W1", "L1_ffn_W2", "ones_row_bf2")


def _input_maps(inputs):
    consts = _consts_np()
    x = np.asarray(inputs["L0_attn_x"])
    base = {}
    for k, v in inputs.items():
        if k == "L0_attn_x":
            continue
        a = np.asarray(v, dtype=np.float32)
        if k in ("L0_attn_decay", "L0_attn_out_scale"):
            a = a.reshape(1, 1)
        if a.ndim == 1:
            a = a.reshape(1, -1)
        base[k] = np.ascontiguousarray(a)
    for k, v in consts.items():
        base[k] = v
    base["ones_row_bf2"] = consts["ones_row_bfs"]
    maps = []
    for b in range(B):
        m = dict(base)
        m["xT"] = np.ascontiguousarray(x[b].T.astype(np.float32))
        maps.append(m)
    return maps


FUSED = True


def kernel(**inputs):
    maps = _input_maps(inputs)
    if FUSED:
        nc = _build("full")
        keys = tuple(set(P1_KEYS) | set(P2_KEYS))
        inm = [{k: m[k] for k in keys} for m in maps]
        rr = run_bass_kernel_spmd(nc, inm, core_ids=list(range(B)))
        outs = []
        for b in range(B):
            xt2 = np.asarray(rr.results[b]["xT2"], dtype=np.float32)
            outs.append(np.ascontiguousarray(xt2.T))
        return np.stack(outs, axis=0)
    nc1 = _build("p1")
    in1 = [{k: m[k] for k in P1_KEYS} for m in maps]
    r1 = run_bass_kernel_spmd(nc1, in1, core_ids=list(range(B)))
    r1m = [{k: np.array(v) for k, v in r.items()} for r in r1.results]
    nc2 = _build("p2")
    in2 = []
    for b in range(B):
        m = {k: maps[b][k] for k in P2_KEYS}
        for nm in ("x1scr", "s2scr", "xn1scr", "ohscr"):
            m[nm] = r1m[b][nm]
        in2.append(m)
    r2 = run_bass_kernel_spmd(nc2, in2, core_ids=list(range(B)))
    outs = []
    for b in range(B):
        xt2 = np.asarray(r2.results[b]["xT2"], dtype=np.float32)
        outs.append(np.ascontiguousarray(xt2.T))
    return np.stack(outs, axis=0)


# version 21
# speedup vs baseline: 1.0843x; 1.0374x over previous
import numpy as np
import ml_dtypes
from contextlib import ExitStack

import concourse.bass as bass
import concourse.mybir as mybir
from concourse.bass_utils import run_bass_kernel_spmd
from concourse.hw_specs import get_activation_tables

F32 = mybir.dt.float32
BF16 = mybir.dt.bfloat16
AF = mybir.ActivationFunctionType
OP = mybir.AluOpType
AX = mybir.AxisListType

B = 8
D = 1024
T = 4096
FC = 8
TC = 512
NCH = T // TC
SUB = 128
NB2 = 64
NEXP = 8
HDIM = 4096
HC = 512
MF = 64
INNER = 16

LND_CLAMP = -85.0

_AF = mybir.ActivationFunctionType
ACT_NEUTRAL = {_AF.Identity, _AF.Copy, _AF.Square, _AF.Relu, _AF.Abs, _AF.Sign, _AF.MemsetZero}
ACT_FUNC_SET = {_AF.Exp: "nlexp", _AF.Ln: "nlexp", _AF.Sigmoid: "sigmoid",
                _AF.Gelu: "gelu", _AF.Tanh: "gelu", _AF.Silu: "silu",
                _AF.Sqrt: "sqrt"}
ACT_SET_NAMES = {"nlexp": "natural_log_exp_and_others", "sigmoid": "sigmoid_and_others",
                 "gelu": "gelu_and_others", "silu": "silu_and_others",
                 "sqrt": "sqrt_and_others"}
ACT_SET_REP = {"nlexp": _AF.Exp, "sigmoid": _AF.Sigmoid, "gelu": _AF.Gelu,
               "silu": _AF.Silu, "sqrt": _AF.Sqrt}


class Buf:
    __slots__ = ("w", "r", "dsem")

    def __init__(self):
        self.w = {}
        self.r = {}
        self.dsem = None


class Prog:
    ENGS = ("pe", "ve", "se", "po", "sy")

    def __init__(self, nc):
        self.nc = nc
        self.es = ExitStack()
        self.E = {"pe": nc.tensor, "ve": nc.vector, "se": nc.scalar,
                  "po": nc.gpsimd, "sy": nc.sync}
        self.sem = {}
        self.cnt = {}
        for s in ("pe", "ve", "se", "po", "qsy", "qse", "qpo"):
            self.sem[s] = self.es.enter_context(nc.semaphore("sem_" + s))
            self.cnt[s] = 0
        self.waited = {}
        self.engsem = {"pe": "pe", "ve": "ve", "se": "se", "po": "po"}
        self.drained = {"ve": 0, "se": 0, "po": 0}
        self.drain_marks = {"ve": [], "se": [], "po": []}
        self.default_scope = None
        self.extra_dep = None
        self.uid = 0

    def name(self, p):
        self.uid += 1
        return f"{p}_{self.uid}"

    def S(self, p, shape, dt, scope=None):
        side = "left" if scope is None else "right"
        t = (scope or self.default_scope or self.es).enter_context(
            self.nc.sbuf_tensor(self.name(p), shape, dt, side=side))
        return t, Buf()

    def cast_staging(self):
        if not hasattr(self, "_cstg"):
            sc = ExitStack()
            self.es.callback(sc.close)
            stgf, _ = self.S("cstgf", [128, 2 * 512], F32, scope=sc)
            stgb, _ = self.S("cstgb", [128, 2 * 512], BF16, scope=sc)
            self._cstg = (stgf, stgb, [(Buf(), Buf()), (Buf(), Buf())], [0])
            self._cast_pend = None
        return self._cstg

    def cast_tile(self, src_ap, dst_ap, dst_buf, src_buf=None):
        stgf, stgb, slotbufs, slot = self.cast_staging()
        sl = slot[0]
        slot[0] = 1 - slot[0]
        fb, bb = slotbufs[sl]
        fap = stgf[:, sl * 512:(sl + 1) * 512]
        bap = stgb[:, sl * 512:(sl + 1) * 512]
        self.dma("po", fap, src_ap, reads=(src_buf,) if src_buf is not None else (),
                 writes=(fb,))
        pend = self._cast_pend
        self._cast_pend = (fap, bap, fb, bb, dst_ap, dst_buf)
        if pend is not None:
            self._cast_emit(pend)

    def _cast_emit(self, pend):
        fap, bap, fb, bb, dst_ap, dst_buf = pend
        self.op("po", lambda e, bap=bap, fap=fap: e.tensor_copy(bap, fap),
                reads=(fb,), writes=(bb,))
        self.dma("po", dst_ap, bap, reads=(bb,), writes=(dst_buf,))

    def cast_flush(self):
        if getattr(self, "_cast_pend", None) is not None:
            self._cast_emit(self._cast_pend)
            self._cast_pend = None

    def barrier(self):
        snap = dict(self.cnt)
        for e in self.ENGS:
            for s, v in snap.items():
                if v > 0:
                    self._wait(e, s, v)

    def PS(self, p, shape, dt=F32, scope=None):
        t = (scope or self.default_scope or self.es).enter_context(
            self.nc.psum_tensor(self.name(p), shape, dt))
        return t, Buf()

    def DR(self, p, shape, dt, kind="Internal"):
        t = self.nc.dram_tensor(self.name(p) if kind == "Internal" else p, shape, dt, kind=kind)
        return t, Buf()

    CLEAN_OP_EDGES = {("se", "pe"), ("ve", "pe"), ("po", "pe"), ("po", "ve"),
                      ("ve", "se")}

    def _wait(self, eng, semname, val, skip_same=True, need_drain=True, dma=False):
        if skip_same and self.engsem.get(eng) == semname:
            return
        if need_drain and not dma and (semname, eng) in self.CLEAN_OP_EDGES:
            need_drain = False
        if need_drain and semname in self.drain_marks and eng != semname:
            cover = None
            for covered, dcnt in self.drain_marks[semname]:
                if covered >= val:
                    cover = dcnt
                    break
            if cover is None:
                covered = self.cnt[semname]
                dinst = self.E[semname].drain()
                self.cnt[semname] += 1
                dinst.then_inc(self.sem[semname], 1)
                cover = self.cnt[semname]
                self.drain_marks[semname].append((covered, cover))
                self.drained[semname] = cover
            val = cover
        key = (eng, semname)
        if self.waited.get(key, 0) >= val:
            return
        self.E[eng].wait_ge(self.sem[semname], val)
        self.waited[key] = val

    def _deps(self, eng, reads, writes, skip_same=True, need_drain=True, dma=False):
        need = {}
        if self.extra_dep is not None:
            for s, v in self.extra_dep.items():
                if need.get(s, 0) < v:
                    need[s] = v
        for b in reads:
            for s, v in b.w.items():
                if need.get(s, 0) < v:
                    need[s] = v
        for b in writes:
            for s, v in b.w.items():
                if need.get(s, 0) < v:
                    need[s] = v
            for s, v in b.r.items():
                if need.get(s, 0) < v:
                    need[s] = v
        for s, v in need.items():
            self._wait(eng, s, v, skip_same=skip_same, need_drain=need_drain, dma=dma)

    def _finish(self, sname, val, reads, writes):
        for b in reads:
            if b.r.get(sname, 0) < val:
                b.r[sname] = val
        for b in writes:
            if b.w.get(sname, 0) < val:
                b.w[sname] = val
            b.r = {}

    def fence(self, eng):
        covered = self.cnt[eng]
        inst = self.E[eng].drain()
        self.cnt[eng] += 1
        inst.then_inc(self.sem[eng], 1)
        if eng in self.drained:
            self.drained[eng] = self.cnt[eng]
            self.drain_marks[eng].append((covered, self.cnt[eng]))
        return (eng, self.cnt[eng])

    def op(self, eng, fn, reads=(), writes=()):
        self._deps(eng, reads, writes, skip_same=(eng == "pe"),
                   need_drain=(eng != "pe"))
        inst = fn(self.E[eng])
        self.cnt[eng] += 1
        inst.then_inc(self.sem[eng], 1)
        self._finish(eng, self.cnt[eng], reads, writes)
        return (eng, self.cnt[eng])

    def dma(self, q, out, in_, reads=(), writes=(), **kw):
        self._deps(q, reads, writes, skip_same=False, dma=True)
        inst = self.E[q].dma_start(out=out, in_=in_, **kw)
        db = writes[0] if writes else None
        if db is not None:
            if db.dsem is None:
                nm = self.name("dsem")
                self.sem[nm] = self.es.enter_context(self.nc.semaphore(nm))
                self.cnt[nm] = 0
                db.dsem = nm
            qs = db.dsem
        else:
            qs = "q" + q
        self.cnt[qs] += 16
        inst.then_inc(self.sem[qs], 16)
        self._finish(qs, self.cnt[qs], reads, writes)
        return (qs, self.cnt[qs])

    def setup_act_guard(self):
        tabs = list(get_activation_tables(self.nc.m.arch).keys())
        self.set_ids = {k: tabs.index(v) for k, v in ACT_SET_NAMES.items()}
        self.cur_set = None

    def ensure_set(self, s):
        if self.cur_set == s:
            return
        self.cur_set = s
        self.E["se"].add_instruction(mybir.InstLoadActFuncSet(
            name=self.nc.get_next_instruction_name(),
            act_func_set_id=self.set_ids[s], ins=[], outs=[]))

    def act(self, fn_func, out, in_, bias=0.0, scale=1.0, reads=(), writes=()):
        s = ACT_FUNC_SET.get(fn_func)
        if s is not None:
            self.ensure_set(s)
        return self.op("se", lambda e: e.activation(out, in_, fn_func, bias=bias, scale=scale),
                       reads=reads, writes=writes)

    def dump(self, name, ap, reads, shape=None, dt=None):
        t = self.nc.dram_tensor("dbg_" + name, shape or list(ap.shape), dt or ap.dtype, kind="ExternalOutput")
        self.dma("po", t.ap()[:, :], ap, reads=reads, writes=(Buf(),))

    def final_wait(self):
        snap = dict(self.cnt)
        for s, v in snap.items():
            if v > 0:
                self._wait("sy", s, v)


def _consts_np():
    i = np.arange(128)
    bf = ml_dtypes.bfloat16
    c = {}
    c["ident_f32"] = np.eye(128, dtype=np.float32)
    c["ident_bf"] = np.eye(128, dtype=np.float32).astype(bf)
    c["ones_col"] = np.ones((128, 1), dtype=np.float32)
    c["ones_row"] = np.ones((1, 128), dtype=np.float32)
    c["ones_row_bfs"] = np.ones((1, 128), dtype=np.float32).astype(bf)
    c["iotaE"] = (i[None, :] - i[:, None] - 1).astype(np.float32)
    c["trimask"] = (i[None, :] > i[:, None]).astype(np.float32)
    c["iotarow"] = i[None, :].astype(np.float32)
    c["revcol"] = (127 - i)[:, None].astype(np.float32)
    return c


CONST_SPECS = (("ident_f32", [128, 128], F32), ("ident_bf", [128, 128], BF16),
               ("ones_col", [128, 1], F32), ("ones_row", [1, 128], F32),
               ("ones_row_bfs", [1, 128], BF16), ("iotaE", [128, 128], F32),
               ("trimask", [128, 128], F32), ("iotarow", [1, 128], F32),
               ("revcol", [128, 1], F32))


DBG = False
STAGE = 99


def build_p1(nc, p, dram):

    def ext_in(nm, shape, dt=F32):
        return nc.dram_tensor(nm, shape, dt, kind="ExternalInput").ap(), Buf()

    xT, xT_b = ext_in("xT", [D, T])
    basis, basis_b = ext_in("L0_attn_basis", [D, NB2])
    coef_in = {}
    for nm in ("q", "k", "v", "o"):
        coef_in[nm] = ext_in(f"L0_attn_{nm}_coeffs", [128, NB2])
    decay, decay_b = ext_in("L0_attn_decay", [1, 1])
    oscale, oscale_b = ext_in("L0_attn_out_scale", [1, 1])
    Win, Win_b = ext_in("L0_ffn_Win", [D, MF])
    binv, binv_b = ext_in("L0_ffn_bin_", [1, MF])
    W1f, W1f_b = ext_in("L0_ffn_W1", [MF, INNER])
    b1f, b1f_b = ext_in("L0_ffn_b1", [MF, INNER])
    W2f, W2f_b = ext_in("L0_ffn_W2", [MF, INNER])
    b2f, b2f_b = ext_in("L0_ffn_b2", [1, MF])
    Wout, Wout_b = ext_in("L0_ffn_Wout", [MF, D])
    boutv, boutv_b = ext_in("L0_ffn_bout", [1, D])
    l1w = {}
    for nm in ("W_in", "W_gx", "W_ga", "W_out"):
        l1w[nm] = ext_in("L1_attn_" + nm, [D, D])
    l1b = {}
    for nm in ("b_in", "b_gx", "b_ga", "b_out", "a_param"):
        l1b[nm] = ext_in("L1_attn_" + nm, [1, D])
    Wg, Wg_b = ext_in("L1_ffn_Wg", [D, NEXP])
    bg, bg_b = ext_in("L1_ffn_bg", [1, NEXP])
    cst = {}
    for nm, shape, dt_ in CONST_SPECS:
        cst[nm] = ext_in(nm, shape, dt_)

    x1scr, x1scr_b = dram["x1scr"]
    s2scr, s2scr_b = dram["s2scr"]
    xn1scr, xn1scr_b = dram["xn1scr"]
    ohscr, ohscr_b = dram["ohscr"]
    x1ap, s2ap, xn1ap, ohap = x1scr.ap(), s2scr.ap(), xn1scr.ap(), ohscr.ap()
    s2w = [Buf() for _ in range(4)]

    l1wbf = {}
    l1wbf_bufs = {}
    for nm in ("W_in", "W_gx", "W_ga", "W_out"):
        t_, b_ = p.DR("l1wbfT_" + nm, [128, FC * D], BF16)
        l1wbf[nm] = t_
        l1wbf_bufs[nm] = b_
        sap, sbuf = l1w[nm]
        for ic in range(FC):
            for cc in range(2):
                p.cast_tile(sap[ic * 128:(ic + 1) * 128, cc * 512:(cc + 1) * 512],
                            t_.ap()[:, ic * D + cc * 512: ic * D + (cc + 1) * 512],
                            b_, src_buf=sbuf)
    p.cast_flush()

    PBt = [p.PS(f"pb{i}", [128, 512]) for i in range(6)]
    PTb, PTb_b = p.PS("ptb", [128, 1024], BF16)
    PSm, PSm_b = p.PS("psm", [128, 512])

    pre_es = ExitStack()

    def Sld(nm, shape, dt, src, src_b, q="sy", scope=None):
        t, b = p.S(nm, shape, dt, scope=scope)
        p.dma(q, t[:, :], src, reads=(src_b,), writes=(b,))
        return t, b

    idf, idf_b = Sld("idf", [128, 128], F32, cst["ident_f32"][0][:, :], cst["ident_f32"][1])
    idb, idb_b = Sld("idb", [128, 128], BF16, cst["ident_bf"][0][:, :], cst["ident_bf"][1], scope=pre_es)
    onc, onc_b = Sld("onc", [128, 1], F32, cst["ones_col"][0][:, :], cst["ones_col"][1])
    onr, onr_b = Sld("onr", [1, 128], F32, cst["ones_row"][0][:, :], cst["ones_row"][1])
    iotaE, iotaE_b = Sld("iotaE", [128, 128], F32, cst["iotaE"][0][:, :], cst["iotaE"][1], scope=pre_es)
    trim, trim_b = Sld("trim", [128, 128], F32, cst["trimask"][0][:, :], cst["trimask"][1], scope=pre_es)
    iotar, iotar_b = Sld("iotar", [1, 128], F32, cst["iotarow"][0][:, :], cst["iotarow"][1], scope=pre_es)
    revc, revc_b = Sld("revc", [128, 1], F32, cst["revcol"][0][:, :], cst["revcol"][1], scope=pre_es)

    bas, bas_b = p.S("bas", [128, FC * NB2], F32, scope=pre_es)
    for ic in range(FC):
        p.dma("sy", bas[:, ic * NB2:(ic + 1) * NB2], basis[ic * 128:(ic + 1) * 128, :],
              reads=(basis_b,), writes=(bas_b,))
    coef = {}
    for nm in ("q", "k", "v", "o"):
        src, srcb = coef_in[nm]
        coef[nm] = Sld("coef" + nm, [128, NB2], F32, src[:, :], srcb, scope=pre_es)
    dscal, dscal_b = Sld("dscal", [1, 1], F32, decay[:, :], decay_b, scope=pre_es)
    osc, osc_b = Sld("osc", [1, 1], F32, oscale[:, :], oscale_b, scope=pre_es)
    winf, winf_b = p.S("winf", [128, FC * MF], F32, scope=pre_es)
    for ic in range(FC):
        p.dma("sy", winf[:, ic * MF:(ic + 1) * MF], Win[ic * 128:(ic + 1) * 128, :],
              reads=(Win_b,), writes=(winf_b,))
    woutf, woutf_b = Sld("woutf", [MF, D], F32, Wout[:, :], Wout_b, scope=pre_es)
    w1c, w1c_b = Sld("w1c", [MF, INNER], F32, W1f[:, :], W1f_b)
    b1c, b1c_b = Sld("b1c", [MF, INNER], F32, b1f[:, :], b1f_b)
    w2c, w2c_b = Sld("w2c", [MF, INNER], F32, W2f[:, :], W2f_b)
    binr, binr_b = Sld("binr", [1, MF], F32, binv[:, :], binv_b, scope=pre_es)
    b2r, b2r_b = Sld("b2r", [1, MF], F32, b2f[:, :], b2f_b, scope=pre_es)
    boutr, boutr_b = Sld("boutr", [1, D], F32, boutv[:, :], boutv_b, scope=pre_es)
    l1br = {}
    for nm in ("b_in", "b_gx", "b_ga", "b_out", "a_param"):
        l1br[nm] = Sld("r" + nm, [1, D], F32, l1b[nm][0][:, :], l1b[nm][1], scope=pre_es)
    wgt, wgt_b = p.S("wgt", [128, FC * NEXP], F32)
    p.dma("sy", wgt[:, :].rearrange("p (ic e) -> p ic e", ic=FC),
          Wg.rearrange("(ic p) e -> p ic e", p=128), reads=(Wg_b,), writes=(wgt_b,))
    bgr, bgr_b = Sld("bgr", [1, NEXP], F32, bg[:, :], bg_b, scope=pre_es)

    def row_to_cols(row_t, row_b, ncols, name, scope=None):
        colt, colb = p.S(name, [128, max(1, ncols // 128)], F32, scope=scope)
        for j in range(max(1, ncols // 128)):
            w = min(128, ncols - j * 128)
            p.op("pe", lambda e, j=j, w=w: e.transpose(PBt[4][0][0:w, 0:1],
                                                       row_t[0:1, j * 128:j * 128 + w], idf[0:1, 0:1]),
                 reads=(row_b, idf_b), writes=(PBt[4][1],))
            p.op("ve", lambda e, j=j, w=w: e.tensor_copy(colt[0:w, j:j + 1], PBt[4][0][0:w, 0:1]),
                 reads=(PBt[4][1],), writes=(colb,))
        return colt, colb

    binc, binc_b = row_to_cols(binr, binr_b, MF, "binc")
    b2cc, b2cc_b = row_to_cols(b2r, b2r_b, MF, "b2cc")
    boutT, boutT_b = row_to_cols(boutr, boutr_b, D, "boutT")
    biasT = {}
    for nm in ("b_in", "b_gx", "b_ga", "b_out"):
        rt_, rb_ = l1br[nm]
        biasT[nm] = row_to_cols(rt_, rb_, D, "bT" + nm)
    apT, apT_b = row_to_cols(l1br["a_param"][0], l1br["a_param"][1], D, "apT", scope=pre_es)
    bgc, bgc_b = row_to_cols(bgr, bgr_b, NEXP, "bgc")

    sp8, sp8_b = p.S("sp8", [128, FC], F32)
    tpb, tpb_b = p.S("pre_t1", [128, FC], F32, scope=pre_es)
    tpc, tpc_b = p.S("pre_t2", [128, FC], F32, scope=pre_es)
    p.act(AF.Abs, PBt[4][0][0:128, 0:FC], apT[:, :], reads=(apT_b,), writes=(PBt[4][1],))
    p.act(AF.Exp, PBt[5][0][0:128, 0:FC], PBt[4][0][0:128, 0:FC], scale=-1.0,
          reads=(PBt[4][1],), writes=(PBt[5][1],))
    p.act(AF.Ln, PBt[4][0][0:128, 0:FC], PBt[5][0][0:128, 0:FC], bias=1.0,
          reads=(PBt[5][1],), writes=(PBt[4][1],))
    p.op("ve", lambda e: e.tensor_scalar_max(tpb[:, :], apT[:, :], 0.0), reads=(apT_b,), writes=(tpb_b,))
    p.fence("ve")
    p.op("ve", lambda e: e.tensor_tensor(tpc[:, :], tpb[:, :], PBt[4][0][0:128, 0:FC], OP.add),
         reads=(tpb_b, PBt[4][1]), writes=(tpc_b,))
    p.fence("ve")
    p.op("ve", lambda e: e.tensor_scalar_mul(sp8[:, :], tpc[:, :], -8.0), reads=(tpc_b,), writes=(sp8_b,))
    p.fence("ve")

    dsig, dsig_b = p.S("dsig", [1, 2], F32, scope=pre_es)
    p.act(AF.Exp, PBt[4][0][0:1, 0:1], dscal[:, :], scale=-1.0, reads=(dscal_b,), writes=(PBt[4][1],))
    p.op("ve", lambda e: e.tensor_scalar_add(dsig[:, 0:1], PBt[4][0][0:1, 0:1], 1.0),
         reads=(PBt[4][1],), writes=(dsig_b,))
    p.fence("ve")
    p.op("ve", lambda e: e.reciprocal(dsig[:, 1:2], dsig[:, 0:1]), reads=(dsig_b,), writes=(dsig_b,))
    p.fence("ve")
    lnd, lnd_b = p.S("lnd", [1, 1], F32, scope=pre_es)
    p.act(AF.Ln, PBt[4][0][0:1, 0:1], dsig[:, 1:2], reads=(dsig_b,), writes=(PBt[4][1],))
    p.op("ve", lambda e: e.tensor_scalar_max(lnd[:, :], PBt[4][0][0:1, 0:1], LND_CLAMP),
         reads=(PBt[4][1],), writes=(lnd_b,))
    p.fence("ve")
    lndc, lndc_b = p.S("lndc", [128, 1], F32, scope=pre_es)
    p.op("pe", lambda e: e.matmul(PBt[4][0][0:128, 0:1], onr[0:1, :], lnd[0:1, 0:1],
                                  start=True, stop=True), reads=(onr_b, lnd_b), writes=(PBt[4][1],))
    p.op("ve", lambda e: e.tensor_copy(lndc[:, 0:1], PBt[4][0][0:128, 0:1]),
         reads=(PBt[4][1],), writes=(lndc_b,))
    p.fence("ve")
    Dmask, Dmask_b = p.S("Dmask", [128, 128], F32)
    p.op("ve", lambda e: e.tensor_scalar_mul(Dmask[:, :], iotaE[:, :], lndc[:, 0:1]),
         reads=(iotaE_b, lndc_b), writes=(Dmask_b,))
    p.act(AF.Exp, PBt[4][0][:, 0:128], Dmask[:, :], reads=(Dmask_b,), writes=(PBt[4][1],))
    p.op("ve", lambda e: e.tensor_tensor(Dmask[:, :], PBt[4][0][:, 0:128], trim[:, :], OP.mult),
         reads=(PBt[4][1], trim_b), writes=(Dmask_b,))
    drow, drow_b = p.S("drow", [1, 128], F32, scope=pre_es)
    p.op("ve", lambda e: e.tensor_scalar_mul(drow[:, :], iotar[:, :], lnd[0:1, 0:1]),
         reads=(iotar_b, lnd_b), writes=(drow_b,))
    p.act(AF.Exp, PBt[5][0][0:1, 0:128], drow[:, :], reads=(drow_b,), writes=(PBt[5][1],))
    p.op("ve", lambda e: e.tensor_copy(drow[:, :], PBt[5][0][0:1, 0:128]),
         reads=(PBt[5][1],), writes=(drow_b,))
    dprow, dprow_b = p.S("dprow", [128, 128], F32)
    p.op("pe", lambda e: e.matmul(PBt[4][0][0:128, 0:128], onr[0:1, :], drow[0:1, :],
                                  start=True, stop=True), reads=(onr_b, drow_b), writes=(PBt[4][1],))
    p.op("ve", lambda e: e.tensor_copy(dprow[:, :], PBt[4][0][0:128, 0:128]),
         reads=(PBt[4][1],), writes=(dprow_b,))
    dpcol, dpcol_b = p.S("dpcol", [128, 1], F32)
    p.op("ve", lambda e: e.tensor_scalar_mul(dpcol[:, :], revc[:, :], lndc[:, 0:1]),
         reads=(revc_b, lndc_b), writes=(dpcol_b,))
    p.act(AF.Exp, PBt[5][0][0:128, 0:1], dpcol[:, :], reads=(dpcol_b,), writes=(PBt[5][1],))
    p.op("ve", lambda e: e.tensor_copy(dpcol[:, :], PBt[5][0][0:128, 0:1]),
         reads=(PBt[5][1],), writes=(dpcol_b,))
    dTc, dTc_b = p.S("dTc", [128, 1], F32)
    p.op("ve", lambda e: e.tensor_scalar_mul(dTc[:, :], lndc[:, 0:1], 128.0),
         reads=(lndc_b,), writes=(dTc_b,))
    p.act(AF.Exp, PBt[5][0][0:128, 0:1], dTc[:, :], reads=(dTc_b,), writes=(PBt[5][1],))
    p.op("ve", lambda e: e.tensor_copy(dTc[:, :], PBt[5][0][0:128, 0:1]),
         reads=(PBt[5][1],), writes=(dTc_b,))

    basT, basT_b = p.S("basT", [NB2, D], F32, scope=pre_es)
    for ic in range(FC):
        p.op("pe", lambda e, ic=ic: e.transpose(PBt[5][0][0:NB2, 0:128],
                                                bas[:, ic * NB2:(ic + 1) * NB2], idf[:, :]),
             reads=(bas_b, idf_b), writes=(PBt[5][1],))
        p.op("ve", lambda e, ic=ic: e.tensor_copy(basT[:, ic * 128:(ic + 1) * 128], PBt[5][0][0:NB2, 0:128]),
             reads=(PBt[5][1],), writes=(basT_b,))

    wnat = {}
    owT = owT_b = None
    for nm in ("q", "k", "v", "o"):
        ct, cb = coef[nm]
        cT, cTb = p.S("cT" + nm, [NB2, 128], F32, scope=pre_es)
        p.op("pe", lambda e, ct=ct: e.transpose(PBt[5][0][0:NB2, 0:128], ct[:, :], idf[:, :]),
             reads=(cb, idf_b), writes=(PBt[5][1],))
        p.op("ve", lambda e, cT=cT: e.tensor_copy(cT[:, :], PBt[5][0][0:NB2, 0:128]),
             reads=(PBt[5][1],), writes=(cTb,))
        wT, wTb = p.S("wT" + nm, [128, D], F32, scope=pre_es)
        for hh in range(2):
            p.op("pe", lambda e, hh=hh, cT=cT: e.matmul(PBt[hh][0][:, :], cT[:, :],
                                                        basT[0:NB2, hh * 512:(hh + 1) * 512],
                                                        start=True, stop=True),
                 reads=(cTb, basT_b), writes=(PBt[hh][1],))
            if nm == "o":
                p.op("ve", lambda e, hh=hh, wT=wT: e.tensor_copy(wT[:, hh * 512:(hh + 1) * 512],
                                                                 PBt[hh][0][:, :]),
                     reads=(PBt[hh][1],), writes=(wTb,))
        if nm == "o":
            oscb, oscb_b = p.S("oscb", [128, 1], F32, scope=pre_es)
            p.op("pe", lambda e: e.matmul(PBt[4][0][0:128, 0:1], onr[0:1, :], osc[0:1, 0:1],
                                          start=True, stop=True), reads=(onr_b, osc_b), writes=(PBt[4][1],))
            p.op("ve", lambda e: e.tensor_copy(oscb[:, 0:1], PBt[4][0][0:128, 0:1]),
                 reads=(PBt[4][1],), writes=(oscb_b,))
            owT, owT_b = p.S("owT", [128, D], BF16)
            p.op("se", lambda e: e.activation(owT[:, :], wT[:, :], AF.Identity, scale=oscb[:, 0:1]),
                 reads=(wTb, oscb_b), writes=(owT_b,))
        else:
            mx, mx_b = p.S("mxw", [128, 4], F32, scope=pre_es)
            p.op("ve", lambda e: e.reduce_max(mx[:, 0:1], PBt[0][0][:, :], axis=AX.X),
                 reads=(PBt[0][1],), writes=(mx_b,))
            p.op("ve", lambda e: e.reduce_max(mx[:, 1:2], PBt[1][0][:, :], axis=AX.X),
                 reads=(PBt[1][1],), writes=(mx_b,))
            p.fence("ve")
            p.op("ve", lambda e: e.tensor_max(mx[:, 2:3], mx[:, 0:1], mx[:, 1:2]), reads=(mx_b,), writes=(mx_b,))
            p.op("se", lambda e: e.mul(mx[:, 3:4], mx[:, 2:3], -1.0), reads=(mx_b,), writes=(mx_b,))
            if DBG and nm == "q":
                p.dump("nmx", mx[:, 3:4], (mx_b,))
            for hh in range(2):
                p.act(AF.Exp, PBt[2 + hh][0][:, :], PBt[hh][0][:, :], bias=mx[:, 3:4],
                      reads=(PBt[hh][1], mx_b), writes=(PBt[2 + hh][1],))
            sm, sm_b = p.S("smw", [128, 4], F32, scope=pre_es)
            p.op("ve", lambda e: e.reduce_sum(sm[:, 0:1], PBt[2][0][:, :], axis=AX.X),
                 reads=(PBt[2][1],), writes=(sm_b,))
            p.op("ve", lambda e: e.reduce_sum(sm[:, 1:2], PBt[3][0][:, :], axis=AX.X),
                 reads=(PBt[3][1],), writes=(sm_b,))
            p.fence("ve")
            p.op("ve", lambda e: e.tensor_add(sm[:, 2:3], sm[:, 0:1], sm[:, 1:2]), reads=(sm_b,), writes=(sm_b,))
            p.fence("ve")
            p.op("ve", lambda e: e.reciprocal(sm[:, 3:4], sm[:, 2:3]), reads=(sm_b,), writes=(sm_b,))
            p.fence("ve")
            if DBG and nm == "q":
                p.dump("smq", sm[:, :], (sm_b,))
                p.dump("mxq", mx[:, :], (mx_b,))
            wTb16, wTb16_b = p.S("wTb" + nm, [128, D], BF16, scope=pre_es)
            for hh in range(2):
                p.op("ve", lambda e, hh=hh: e.tensor_scalar_mul(wTb16[:, hh * 512:(hh + 1) * 512],
                                                                PBt[2 + hh][0][:, :], sm[:, 3:4]),
                     reads=(PBt[2 + hh][1], sm_b), writes=(wTb16_b,))
            nat, nat_b = p.S("wnat" + nm, [128, FC * 128], BF16)
            for ic in range(FC):
                p.op("pe", lambda e, ic=ic: e.transpose(PTb[0:128, 0:128],
                                                        wTb16[:, ic * 128:(ic + 1) * 128], idb[:, :]),
                     reads=(wTb16_b, idb_b), writes=(PTb_b,))
                p.op("ve", lambda e, ic=ic: e.tensor_copy(nat[:, ic * 128:(ic + 1) * 128], PTb[0:128, 0:128]),
                     reads=(PTb_b,), writes=(nat_b,))
            wnat[nm] = (nat, nat_b)
            if DBG and nm == "q":
                p.dump("basT", basT[:, :], (basT_b,))
                p.dump("cTq", cT[:, :], (cTb,))
                p.dump("wTq", wT[:, :], (wTb,))
                p.dump("wTb16q", wTb16[:, :], (wTb16_b,))
                p.dump("wnatq", nat[:, :], (nat_b,))

    winb, winb_b = p.S("winb", [128, FC * MF], BF16)
    p.op("ve", lambda e: e.tensor_copy(winb[:, :], winf[:, :]), reads=(winf_b,), writes=(winb_b,))
    woutb, woutb_b = p.S("woutb", [MF, D], BF16)
    p.op("ve", lambda e: e.tensor_copy(woutb[:, :], woutf[:, :]), reads=(woutf_b,), writes=(woutb_b,))

    pre_es.close()
    p.barrier()

    Mst, Mst_b = p.S("Mst", [128, 128], F32)
    p.op("ve", lambda e: e.memset(Mst[:, :], 0.0), writes=(Mst_b,))
    hcar, hcar_b = p.S("hcar", [128, FC], F32)
    p.op("ve", lambda e: e.memset(hcar[:, :], 0.0), writes=(hcar_b,))

    xab = [p.S("xa", [128, FC * TC], F32)]
    x1buf, x1buf_b = p.S("x1buf", [128, FC * TC], F32)
    xn, xn_b = p.S("xn", [128, FC * TC], BF16)
    s0, s0_b = p.S("s0", [128, FC * TC], F32)
    qT, qT_b = p.S("qT", [128, TC], BF16)
    kT, kT_b = p.S("kT", [128, TC], BF16)
    ktok, ktok_b = p.S("ktok", [128, TC], BF16)
    vtok, vtok_b = p.S("vtok", [128, TC], BF16)
    qtil4, qtil4_b = p.S("qtil4", [128, TC], BF16)
    ktil4, ktil4_b = p.S("ktil4", [128, TC], BF16)
    SpT4, SpT4_b = p.S("SpT4", [128, TC], BF16)
    Mbf4, Mbf4_b = p.S("Mbf4", [128, TC], BF16)
    ret, ret_b = p.S("ret", [128, TC], BF16)
    Mtmp, Mtmp_b = p.S("Mtmp", [128, 128], F32)
    sF, sF_b = p.S("sF", [MF, TC], F32)
    gaccs = [p.S(f"gacc{i}", [MF, TC], F32) for i in range(2)]
    gbf, gbf_b = p.S("gbf", [MF, TC], BF16)
    xi_bf, xi_bf_b = p.S("xibf", [128, FC * TC], BF16)
    pool16a = p.S("p16a", [128, FC * TC], F32)
    pool16b = p.S("p16b", [128, FC * TC], F32)
    h_bf, h_bf_b = p.S("hbf", [128, FC * TC], BF16)
    tp = [p.S(f"tp{i}", [128, TC], F32) for i in range(8)]
    rstage, rstage_b = p.S("rstage", [128, TC], F32)
    tpi = [0]

    def TP():
        t = tp[tpi[0] % len(tp)]
        tpi[0] += 1
        return t

    rows_stats = p.S("rstat", [1, 4 * TC], F32)
    lg, lg_b = p.S("lg", [NEXP, TC], F32)
    ohb, ohb_b = p.S("ohb", [NEXP, TC], BF16)
    ltok, ltok_b = p.S("ltok", [128, 4 * NEXP], F32)
    mxc, mxc_b = p.S("mxc", [128, 4], F32)
    mxr, mxr_b = p.S("mxr", [1, TC], F32)
    wstr = [p.S("wstrA", [128, FC * D], BF16), p.S("wstrB", [128, FC * D], BF16)]

    def ln_apply(xa, xa_b, out_t, out_b, also_f32_cb=None):
        rs, rs_b = rows_stats
        for ic in range(FC):
            sq = TP()
            p.op("se", lambda e, ic=ic, sq=sq: e.square(sq[0][:, :], xa[:, ic * TC:(ic + 1) * TC]),
                 reads=(xa_b,), writes=(sq[1],))
            p.op("pe", lambda e, ic=ic: e.matmul(PBt[2][0][0:1, :], onc[:, 0:1], xa[:, ic * TC:(ic + 1) * TC],
                                                 start=(ic == 0), stop=(ic == FC - 1)),
                 reads=(xa_b, onc_b), writes=(PBt[2][1],))
            p.op("pe", lambda e, ic=ic, sq=sq: e.matmul(PBt[3][0][0:1, :], onc[:, 0:1], sq[0][:, :],
                                                        start=(ic == 0), stop=(ic == FC - 1)),
                 reads=(sq[1], onc_b), writes=(PBt[3][1],))
        p.op("ve", lambda e: e.tensor_scalar_mul(rs[0:1, 0:TC], PBt[2][0][0:1, :], 1.0 / D),
             reads=(PBt[2][1],), writes=(rs_b,))
        p.op("ve", lambda e: e.tensor_scalar_mul(rs[0:1, TC:2 * TC], PBt[3][0][0:1, :], 1.0 / D),
             reads=(PBt[3][1],), writes=(rs_b,))
        p.op("ve", lambda e: e.tensor_tensor(rs[0:1, 2 * TC:3 * TC], rs[0:1, 0:TC], rs[0:1, 0:TC], OP.mult),
             reads=(rs_b,), writes=(rs_b,))
        p.op("ve", lambda e: e.tensor_sub(rs[0:1, 3 * TC:4 * TC], rs[0:1, TC:2 * TC], rs[0:1, 2 * TC:3 * TC]),
             reads=(rs_b,), writes=(rs_b,))
        p.op("ve", lambda e: e.tensor_scalar_add(rs[0:1, 2 * TC:3 * TC], rs[0:1, 3 * TC:4 * TC], 1e-5),
             reads=(rs_b,), writes=(rs_b,))
        p.act(AF.Ln, PBt[2][0][0:1, :], rs[0:1, 2 * TC:3 * TC], reads=(rs_b,), writes=(PBt[2][1],))
        p.act(AF.Exp, PBt[3][0][0:1, :], PBt[2][0][0:1, :], scale=-0.5,
              reads=(PBt[2][1],), writes=(PBt[3][1],))
        p.op("ve", lambda e: e.tensor_copy(rs[0:1, 3 * TC:4 * TC], PBt[3][0][0:1, :]),
             reads=(PBt[3][1],), writes=(rs_b,))
        p.op("pe", lambda e: e.matmul(PBt[2][0][:, :], onr[0:1, :], rs[0:1, 0:TC],
                                      start=True, stop=True), reads=(rs_b, onr_b), writes=(PBt[2][1],))
        p.op("pe", lambda e: e.matmul(PBt[3][0][:, :], onr[0:1, :], rs[0:1, 3 * TC:4 * TC],
                                      start=True, stop=True), reads=(rs_b, onr_b), writes=(PBt[3][1],))
        for ic in range(FC):
            t1 = TP()
            p.op("ve", lambda e, ic=ic, t1=t1: e.tensor_tensor(t1[0][:, :], xa[:, ic * TC:(ic + 1) * TC],
                                                               PBt[2][0][:, :], OP.subtract),
                 reads=(xa_b, PBt[2][1]), writes=(t1[1],))
            if also_f32_cb is None:
                p.op("ve", lambda e, ic=ic, t1=t1: e.tensor_tensor(out_t[:, ic * TC:(ic + 1) * TC],
                                                                   t1[0][:, :], PBt[3][0][:, :], OP.mult),
                     reads=(t1[1], PBt[3][1]), writes=(out_b,))
            else:
                tf = TP()
                p.op("ve", lambda e, t1=t1, tf=tf: e.tensor_tensor(tf[0][:, :], t1[0][:, :],
                                                                   PBt[3][0][:, :], OP.mult),
                     reads=(t1[1], PBt[3][1]), writes=(tf[1],))
                p.op("ve", lambda e, ic=ic, tf=tf: e.tensor_copy(out_t[:, ic * TC:(ic + 1) * TC], tf[0][:, :]),
                     reads=(tf[1],), writes=(out_b,))
                also_f32_cb(ic, tf)

    def load_x(c, slot):
        xt, xb = xab[slot]
        p.dma("sy", xt[:, :].rearrange("p (fc t) -> p fc t", fc=FC),
              xT.rearrange("(fc p) t -> p fc t", p=128)[:, :, c * TC:(c + 1) * TC],
              reads=(xT_b,), writes=(xb,))

    def load_w(nm, slot):
        wt, wb = wstr[slot]
        p.dma("sy", wt[:, :], l1wbf[nm].ap()[:, :],
              reads=(l1wbf_bufs[nm],), writes=(wb,))
        return wt, wb

    xa, xa_b = xab[0]

    def emit_ln0():
        ln_apply(xa, xa_b, xn, xn_b)

    def qkv_units():
        units = []

        def qk_unit(dst, dstb, nm, bk):
            nat, nat_b = wnat[nm]
            for ic in range(FC):
                p.op("pe", lambda e, ic=ic, nat=nat, bk=bk: e.matmul(
                    PBt[bk][0][:, :], nat[:, ic * 128:(ic + 1) * 128],
                    xn[:, ic * TC:(ic + 1) * TC],
                    start=(ic == 0), stop=(ic == FC - 1)),
                     reads=(nat_b, xn_b), writes=(PBt[bk][1],))
            p.op("se", lambda e, dst=dst, bk=bk: e.copy(dst[:, :], PBt[bk][0][:, :]),
                 reads=(PBt[bk][1],), writes=(dstb,))

        def tok_unit(dst, dstb, nm, s):
            nat, nat_b = wnat[nm]
            pb = PBt[s % 2]
            for ic in range(FC):
                p.op("pe", lambda e, s=s, ic=ic, nat=nat, pb=pb: e.matmul(
                    pb[0][:, 0:128], xn[:, ic * TC + s * SUB: ic * TC + (s + 1) * SUB],
                    nat[:, ic * 128:(ic + 1) * 128],
                    start=(ic == 0), stop=(ic == FC - 1)),
                    reads=(nat_b, xn_b), writes=(pb[1],))
            p.op("se", lambda e, s=s, dst=dst, pb=pb: e.copy(dst[:, s * SUB:(s + 1) * SUB],
                                                             pb[0][:, 0:128]),
                 reads=(pb[1],), writes=(dstb,))

        for dst, dstb, nm, bk in ((qT, qT_b, "q", 0), (kT, kT_b, "k", 1)):
            units.append(lambda dst=dst, dstb=dstb, nm=nm, bk=bk: qk_unit(dst, dstb, nm, bk))
        for dst, dstb, nm in ((ktok, ktok_b, "k"), (vtok, vtok_b, "v")):
            for s in range(TC // SUB):
                units.append(lambda dst=dst, dstb=dstb, nm=nm, s=s: tok_unit(dst, dstb, nm, s))
        return units

    def emit_qkv():
        for u in qkv_units():
            u()

    load_x(0, 0)
    emit_ln0()
    emit_qkv()
    for c in range(NCH):
        if DBG and c == 0:
            p.dump("xa0", xa[:, :], (xa_b,))
            p.dump("xn0", xn[:, :], (xn_b,))
        if STAGE < 2:
            continue
        if DBG and c == 0:
            p.dump("qT0", qT[:, :], (qT_b,))
            p.dump("kT0", kT[:, :], (kT_b,))
            p.dump("ktok0", ktok[:, :], (ktok_b,))
            p.dump("vtok0", vtok[:, :], (vtok_b,))
        if STAGE < 5:
            continue
        for ic in range(FC):
            p.op("pe", lambda e, ic=ic: e.matmul(PBt[0][0][0:MF, :], winb[:, ic * MF:(ic + 1) * MF],
                                                 xn[:, ic * TC:(ic + 1) * TC],
                                                 start=(ic == 0), stop=(ic == FC - 1)),
                 reads=(winb_b, xn_b), writes=(PBt[0][1],))
        p.op("se", lambda e: e.activation(sF[:, :], PBt[0][0][0:MF, :], AF.Identity, bias=binc[0:MF, 0:1]),
             reads=(PBt[0][1], binc_b), writes=(sF_b,))
        for s in range(TC // SUB):
            sl = slice(s * SUB, (s + 1) * SUB)
            q4 = slice(s * 128, (s + 1) * 128)
            p.op("pe", lambda e, sl=sl, q4=q4: e.matmul(PBt[5][0][:, q4], kT[:, sl], qT[:, sl],
                                                        start=True, stop=True),
                 reads=(kT_b, qT_b), writes=(PBt[5][1],))
            p.op("ve", lambda e, sl=sl, q4=q4: e.tensor_tensor(qtil4[:, q4], qT[:, sl], dprow[:, :], OP.mult),
                 reads=(qT_b, dprow_b), writes=(qtil4_b,))
            p.op("ve", lambda e, sl=sl, q4=q4: e.tensor_scalar_mul(ktil4[:, q4], ktok[:, sl], dpcol[:, 0:1]),
                 reads=(ktok_b, dpcol_b), writes=(ktil4_b,))
            p.op("ve", lambda e, q4=q4: e.tensor_tensor(SpT4[:, q4], PBt[5][0][:, q4], Dmask[:, :], OP.mult),
                 reads=(PBt[5][1], Dmask_b), writes=(SpT4_b,))
            p.op("pe", lambda e, sl=sl, q4=q4: e.matmul(PBt[4][0][:, q4], ktil4[:, q4], vtok[:, sl],
                                                        start=True, stop=True),
                 reads=(ktil4_b, vtok_b), writes=(PBt[4][1],))
        for s in range(TC // SUB):
            q4 = slice(s * 128, (s + 1) * 128)
            p.op("ve", lambda e, q4=q4: e.tensor_copy(Mbf4[:, q4], Mst[:, :]),
                 reads=(Mst_b,), writes=(Mbf4_b,))
            p.op("ve", lambda e: e.tensor_scalar_mul(Mtmp[:, :], Mst[:, :], dTc[:, 0:1]),
                 reads=(Mst_b, dTc_b), writes=(Mtmp_b,))
            p.op("ve", lambda e, q4=q4: e.tensor_tensor(Mst[:, :], Mtmp[:, :], PBt[4][0][:, q4], OP.add),
                 reads=(Mtmp_b, PBt[4][1]), writes=(Mst_b,))
        for s in range(TC // SUB):
            sl = slice(s * SUB, (s + 1) * SUB)
            q4 = slice(s * 128, (s + 1) * 128)
            p.op("pe", lambda e, sl=sl, q4=q4: e.matmul(PBt[0][0][:, q4], vtok[:, sl], SpT4[:, q4],
                                                        start=True, stop=False),
                 reads=(vtok_b, SpT4_b), writes=(PBt[0][1],))
            p.op("pe", lambda e, q4=q4: e.matmul(PBt[0][0][:, q4], Mbf4[:, q4], qtil4[:, q4],
                                                 start=False, stop=True),
                 reads=(Mbf4_b, qtil4_b), writes=(PBt[0][1],))
            p.op("se", lambda e, sl=sl, q4=q4: e.copy(ret[:, sl], PBt[0][0][:, q4]),
                 reads=(PBt[0][1],), writes=(ret_b,))
        if STAGE < 4.2:
            continue
        def gacc_relu(h):
            p.act(AF.Relu, tp[h % 8][0][0:MF, :], sF[:, :], bias=b1c[:, h:h + 1],
                  scale=w1c[:, h:h + 1], reads=(sF_b, b1c_b, w1c_b), writes=(tp[h % 8][1],))

        def gacc_acc(h):
            ga_o, ga_ob = gaccs[h % 2]
            ga_i, ga_ib = gaccs[(h + 1) % 2]
            if h == 0:
                p.op("ve", lambda e, h=h, ga_o=ga_o: e.tensor_scalar_mul(ga_o[:, :], tp[h % 8][0][0:MF, :],
                                                                         w2c[:, 0:1]),
                     reads=(tp[h % 8][1], w2c_b), writes=(ga_ob,))
            else:
                p.op("ve", lambda e, h=h, ga_o=ga_o, ga_i=ga_i: e.scalar_tensor_tensor(
                    ga_o[:, :], tp[h % 8][0][0:MF, :], w2c[:, h:h + 1], ga_i[:, :], OP.mult, OP.add),
                     reads=(tp[h % 8][1], w2c_b, ga_ib), writes=(ga_ob,))

        for h in range(8):
            gacc_relu(h)
        for h in range(8):
            gacc_acc(h)
            gacc_relu(h + 8)
        for h in range(8, INNER):
            gacc_acc(h)
        if STAGE < 4.3:
            continue
        gfin, gfin_b = gaccs[(INNER - 1) % 2]
        p.act(AF.Identity, gbf[:, :], gfin[:, :], bias=b2cc[0:MF, 0:1],
              reads=(gfin_b, b2cc_b), writes=(gbf_b,))
        if STAGE < 3:
            continue
        if STAGE < 4:
            continue
        for vcB in range(FC):
            pb = PBt[vcB % 2]
            p.op("pe", lambda e, vcB=vcB, pb=pb: e.matmul(pb[0][:, :], owT[:, vcB * 128:(vcB + 1) * 128],
                                                          ret[:, :], start=True, stop=True),
                 reads=(owT_b, ret_b), writes=(pb[1],))
            p.op("se", lambda e, vcB=vcB, pb=pb: e.copy(s0[:, vcB * TC:(vcB + 1) * TC], pb[0][:, :]),
                 reads=(pb[1],), writes=(s0_b,))

        if DBG and c == 0:
            p.dump("ret0", ret[:, :], (ret_b,))
            p.dump("s00", s0[:, :], (s0_b,))
            p.dump("Dmask", Dmask[:, :], (Dmask_b,))
            p.dump("dprow", dprow[:, :], (dprow_b,))
        if STAGE < 4.4:
            continue
        for dc in range(FC):
            pb = PBt[dc % 2]
            p.op("pe", lambda e, dc=dc, pb=pb: e.matmul(pb[0][:, :], woutb[:, dc * 128:(dc + 1) * 128],
                                                        gbf[:, :], start=True, stop=True),
                 reads=(woutb_b, gbf_b), writes=(pb[1],))
            sg = PBt[2 + dc % 2]
            p.act(AF.Silu, sg[0][:, :], pb[0][:, :], bias=boutT[:, dc:dc + 1],
                  reads=(pb[1], boutT_b), writes=(sg[1],))
            tm = TP()
            p.op("ve", lambda e, dc=dc, sg=sg, tm=tm: e.tensor_tensor(tm[0][:, :], s0[:, dc * TC:(dc + 1) * TC],
                                                                      sg[0][:, :], OP.mult),
                 reads=(s0_b, sg[1]), writes=(tm[1],))
            p.op("ve", lambda e, dc=dc, tm=tm: e.tensor_tensor(x1buf[:, dc * TC:(dc + 1) * TC],
                                                               xa[:, dc * TC:(dc + 1) * TC], tm[0][:, :], OP.add),
                 reads=(tm[1], xa_b), writes=(x1buf_b,))
        if STAGE < 4.5:
            continue
        p.dma("se", x1ap.rearrange("(fc p) t -> p fc t", p=128)[:, :, c * TC:(c + 1) * TC],
              x1buf[:, :].rearrange("p (fc t) -> p fc t", fc=FC), reads=(x1buf_b,), writes=(x1scr_b,))
        if c + 1 < NCH:
            load_x(c + 1, 0)
        if DBG and c == 0:
            p.dump("sF0", sF[:, :], (sF_b,))
            p.dump("gacc0", gfin[:, :], (gfin_b,))
            p.dump("x1c0", x1buf[:, :], (x1buf_b,))

        if STAGE < 6:
            continue
        def logits_cb(ic, tf):
            p.op("pe", lambda e, ic=ic, tf=tf: e.matmul(PSm[0:NEXP, :], wgt[:, ic * NEXP:(ic + 1) * NEXP],
                                                        tf[0][:, :], start=(ic == 0), stop=(ic == FC - 1)),
                 reads=(wgt_b, tf[1]), writes=(PSm_b,))

        ln_apply(x1buf, x1buf_b, xn, xn_b, also_f32_cb=logits_cb)
        p.op("ve", lambda e: e.tensor_scalar(lg[:, :], PSm[0:NEXP, :], bgc[0:NEXP, 0:1], None, OP.add),
             reads=(PSm_b, bgc_b), writes=(lg_b,))
        p.dma("se", xn1ap.rearrange("(fc p) t -> p fc t", p=128)[:, :, c * TC:(c + 1) * TC],
              xn[:, :].rearrange("p (fc t) -> p fc t", fc=FC), reads=(xn_b,), writes=(xn1scr_b,))

        if STAGE < 7:
            continue
        for s in range(TC // SUB):
            p.op("pe", lambda e, s=s: e.transpose(PBt[4][0][0:128, 0:NEXP],
                                                  lg[:, s * SUB:(s + 1) * SUB], idf[0:NEXP, 0:NEXP]),
                 reads=(lg_b, idf_b), writes=(PBt[4][1],))
            p.op("ve", lambda e, s=s: e.tensor_copy(ltok[:, s * NEXP:(s + 1) * NEXP], PBt[4][0][0:128, 0:NEXP]),
                 reads=(PBt[4][1],), writes=(ltok_b,))
            p.op("ve", lambda e, s=s: e.reduce_max(mxc[:, s:s + 1], ltok[:, s * NEXP:(s + 1) * NEXP], axis=AX.X),
                 reads=(ltok_b,), writes=(mxc_b,))
            p.op("pe", lambda e, s=s: e.transpose(PBt[4][0][0:1, 0:128], mxc[:, s:s + 1], idf[:, :]),
                 reads=(mxc_b, idf_b), writes=(PBt[4][1],))
            p.op("ve", lambda e, s=s: e.tensor_copy(mxr[0:1, s * SUB:(s + 1) * SUB], PBt[4][0][0:1, 0:128]),
                 reads=(PBt[4][1],), writes=(mxr_b,))
        p.op("pe", lambda e: e.matmul(PBt[4][0][0:NEXP, :], onr[0:1, 0:NEXP], mxr[0:1, :],
                                      start=True, stop=True), reads=(onr_b, mxr_b), writes=(PBt[4][1],))
        p.op("ve", lambda e: e.tensor_tensor(ohb[:, :], lg[:, :], PBt[4][0][0:NEXP, :], OP.is_ge),
             reads=(lg_b, PBt[4][1]), writes=(ohb_b,))
        p.dma("se", ohap[:, c * TC:(c + 1) * TC], ohb[:, :], reads=(ohb_b,), writes=(ohscr_b,))

        if STAGE < 8:
            continue
        xi_f32, xi_f32_b = pool16a
        xg_all, xg_all_b = pool16b
        wt, wb = load_w("W_in", 0)
        for oc2 in range(FC):
            pb = PBt[oc2 % 2]
            for ic in range(FC):
                p.op("pe", lambda e, oc2=oc2, ic=ic, pb=pb, wt=wt: e.matmul(
                    pb[0][:, :], wt[:, ic * D + oc2 * 128: ic * D + (oc2 + 1) * 128],
                    xn[:, ic * TC:(ic + 1) * TC], start=(ic == 0), stop=(ic == FC - 1)),
                    reads=(wb, xn_b), writes=(pb[1],))
            p.act(AF.Identity, xi_f32[:, oc2 * TC:(oc2 + 1) * TC], pb[0][:, :],
                  bias=biasT["b_in"][0][:, oc2:oc2 + 1],
                  reads=(pb[1], biasT["b_in"][1]), writes=(xi_f32_b,))
            p.act(AF.Identity, xi_bf[:, oc2 * TC:(oc2 + 1) * TC], pb[0][:, :],
                  bias=biasT["b_in"][0][:, oc2:oc2 + 1],
                  reads=(pb[1], biasT["b_in"][1]), writes=(xi_bf_b,))
        if STAGE < 9:
            continue
        wt2, wb2 = load_w("W_gx", 1)
        for oc2 in range(FC):
            pb = PBt[oc2 % 2]
            for ic in range(FC):
                p.op("pe", lambda e, oc2=oc2, ic=ic, pb=pb, wt2=wt2: e.matmul(
                    pb[0][:, :], wt2[:, ic * D + oc2 * 128: ic * D + (oc2 + 1) * 128],
                    xi_bf[:, ic * TC:(ic + 1) * TC], start=(ic == 0), stop=(ic == FC - 1)),
                    reads=(wb2, xi_bf_b), writes=(pb[1],))
            sg = PBt[2 + oc2 % 2]
            p.act(AF.Sigmoid, sg[0][:, :], pb[0][:, :], bias=biasT["b_gx"][0][:, oc2:oc2 + 1],
                  reads=(pb[1], biasT["b_gx"][1]), writes=(sg[1],))
            p.op("ve", lambda e, oc2=oc2, sg=sg: e.tensor_tensor(xg_all[:, oc2 * TC:(oc2 + 1) * TC],
                                                                 sg[0][:, :],
                                                                 xi_f32[:, oc2 * TC:(oc2 + 1) * TC], OP.mult),
                 reads=(sg[1], xi_f32_b), writes=(xg_all_b,))
        if STAGE < 10:
            continue
        a_all, a_all_b = pool16a
        ga_all, ga_all_b = a_all, a_all_b
        wt3, wb3 = load_w("W_ga", 0)
        for oc2 in range(FC):
            pb = PBt[oc2 % 2]
            for ic in range(FC):
                p.op("pe", lambda e, oc2=oc2, ic=ic, pb=pb, wt3=wt3: e.matmul(
                    pb[0][:, :], wt3[:, ic * D + oc2 * 128: ic * D + (oc2 + 1) * 128],
                    xi_bf[:, ic * TC:(ic + 1) * TC], start=(ic == 0), stop=(ic == FC - 1)),
                    reads=(wb3, xi_bf_b), writes=(pb[1],))
            sg = PBt[2 + oc2 % 2]
            p.act(AF.Sigmoid, sg[0][:, :], pb[0][:, :], bias=biasT["b_ga"][0][:, oc2:oc2 + 1],
                  reads=(pb[1], biasT["b_ga"][1]), writes=(sg[1],))
            p.op("ve", lambda e, oc2=oc2, sg=sg: e.tensor_scalar(ga_all[:, oc2 * TC:(oc2 + 1) * TC],
                                                                 sg[0][:, :], sp8[:, oc2:oc2 + 1],
                                                                 None, OP.mult),
                 reads=(sg[1], sp8_b), writes=(ga_all_b,))
        if STAGE < 11:
            continue
        if c + 1 < NCH and STAGE >= 99:
            emit_ln0()
            qunits = qkv_units()
        else:
            qunits = []
        for oc2 in range(FC):
            osl = slice(oc2 * TC, (oc2 + 1) * TC)
            p.act(AF.Exp, s0[:, osl], ga_all[:, osl], reads=(ga_all_b,), writes=(s0_b,))
            p.act(AF.Exp, tp[oc2][0][:, :], ga_all[:, osl], scale=2.0,
                  reads=(ga_all_b,), writes=(tp[oc2][1],))
        for oc2 in range(FC):
            osl = slice(oc2 * TC, (oc2 + 1) * TC)
            p.act(AF.Relu, rstage[:, :], tp[oc2][0][:, :], bias=1.0, scale=-1.0,
                  reads=(tp[oc2][1],), writes=(rstage_b,))
            p.act(AF.Sqrt, ga_all[:, osl], rstage[:, :], reads=(rstage_b,), writes=(ga_all_b,))
            p.op("ve", lambda e, osl=osl, oc2=oc2: e.tensor_tensor(tp[oc2][0][:, :], ga_all[:, osl],
                                                                   xg_all[:, osl], OP.mult),
                 reads=(ga_all_b, xg_all_b), writes=(tp[oc2][1],))
            p.op("ve", lambda e, osl=osl, oc2=oc2: e.tensor_tensor_scan(
                ga_all[:, osl], s0[:, osl], tp[oc2][0][:, :],
                hcar[:, oc2:oc2 + 1], OP.mult, OP.add),
                reads=(s0_b, tp[oc2][1], hcar_b), writes=(ga_all_b,))
            p.op("ve", lambda e, oc2=oc2: e.tensor_copy(hcar[:, oc2:oc2 + 1],
                                                        ga_all[:, oc2 * TC + TC - 1:oc2 * TC + TC]),
                 reads=(ga_all_b,), writes=(hcar_b,))
            p.op("se", lambda e, oc2=oc2, osl=osl: e.copy(h_bf[:, osl], ga_all[:, osl]),
                 reads=(ga_all_b,), writes=(h_bf_b,))
        if STAGE < 12:
            continue
        wt4, wb4 = load_w("W_out", 1)
        for u in qunits:
            u()
        for oc2 in range(FC):
            pb = PBt[4 + oc2 % 2]
            for ic in range(FC):
                p.op("pe", lambda e, oc2=oc2, ic=ic, pb=pb, wt4=wt4: e.matmul(
                    pb[0][:, :], wt4[:, ic * D + oc2 * 128: ic * D + (oc2 + 1) * 128],
                    h_bf[:, ic * TC:(ic + 1) * TC], start=(ic == 0), stop=(ic == FC - 1)),
                    reads=(wb4, h_bf_b), writes=(pb[1],))
            s2t = TP()
            p.act(AF.Identity, s2t[0][:, :], pb[0][:, :], bias=biasT["b_out"][0][:, oc2:oc2 + 1],
                  reads=(pb[1], biasT["b_out"][1]), writes=(s2t[1],))
            p.dma("se", s2ap[oc2 * 128:(oc2 + 1) * 128, c * TC:(c + 1) * TC], s2t[0][:, :],
                  reads=(s2t[1],), writes=(s2w[oc2 % 4],))


def build_p2_cast(nc, p, w1, w2):
    w1bf_t, _ = p.DR("w1bfT", [NEXP, 128, (HDIM // 512) * FC * 512], BF16)
    w2bf_t, _ = p.DR("w2bfT", [NEXP, 128, 32 * D], BF16)
    w1bf, w2bf = w1bf_t.ap(), w2bf_t.ap()
    w1bufs = [Buf() for _ in range(NEXP)]
    w2bufs = [Buf() for _ in range(NEXP)]
    cast_tile = p.cast_tile

    for e_ in range(NEXP):
        for r in range(D // 128):
            for cc in range(HDIM // 512):
                dst = w1bf[e_, :, (cc * FC + r) * 512:(cc * FC + r + 1) * 512]
                cast_tile(w1[e_, r * 128:(r + 1) * 128, cc * 512:(cc + 1) * 512],
                          dst, w1bufs[e_])
        for r in range(HDIM // 128):
            for cc in range(D // 512):
                dst = w2bf[e_, :, r * D + cc * 512:r * D + (cc + 1) * 512]
                cast_tile(w2[e_, r * 128:(r + 1) * 128, cc * 512:(cc + 1) * 512],
                          dst, w2bufs[e_])
    p.cast_flush()
    return w1bf, w1bufs, w2bf, w2bufs


def build_p2(nc, p, dram, moebf=None, ext_w=None):

    def ext_in(nm, shape, dt=F32):
        return nc.dram_tensor(nm, shape, dt, kind="ExternalInput").ap(), Buf()

    x1ap, x1b = dram["x1scr"][0].ap(), dram["x1scr"][1]
    s2ap, s2b = dram["s2scr"][0].ap(), dram["s2scr"][1]
    xn1ap, xn1b = dram["xn1scr"][0].ap(), dram["xn1scr"][1]
    ohap, oh_b = dram["ohscr"][0].ap(), dram["ohscr"][1]

    if ext_w is None:
        w1, w1_b = ext_in("L1_ffn_W1", [NEXP, D, HDIM])
        w2, w2_b = ext_in("L1_ffn_W2", [NEXP, HDIM, D])
    else:
        w1, w2 = ext_w
        w1_b, w2_b = Buf(), Buf()
    out, out_b = nc.dram_tensor("xT2", [D, T], F32, kind="ExternalOutput").ap(), Buf()

    if moebf is None:
        w1bf, w1bufs, w2bf, w2bufs = build_p2_cast(nc, p, w1, w2)
    else:
        w1bf, w1bufs, w2bf, w2bufs = moebf

    onrb2, onrb2_b = p.S("onrb2", [128, 128], BF16)
    orow = nc.dram_tensor("ones_row_bf2", [1, 128], BF16, kind="ExternalInput").ap()
    p.dma("sy", onrb2[0:1, :], orow[:, :], writes=(onrb2_b,))
    p.dma("sy", onrb2[64:65, :], orow[:, :], writes=(onrb2_b,))

    PB = [p.PS(f"p2pb{i}", [128, 512]) for i in range(6)]

    w2sb, w2sb_b = p.S("w2sb", [128, 32 * D], BF16)
    w2qb = [Buf() for _ in range(4)]
    ybufs = [p.S("ybufA", [128, FC * (T // 2)], BF16), p.S("ybufB", [128, FC * (T // 2)], BF16)]
    h_fm, h_fm_b = p.S("hfm", [128, 32 * 512], BF16)
    w1t = [p.S("w1tA", [128, FC * HC], BF16), p.S("w1tB", [128, FC * HC], BF16)]
    xn1s, xn1s_b = p.S("xn1s", [128, FC * 512], BF16)
    xnm, xnm_b = p.S("xnm", [128, FC * 512], BF16)
    mrowt, _mb = p.S("mrow", [128, 512], BF16)
    mrow = [(mrowt[0:1, :], Buf()), (mrowt[64:65, :], Buf())]
    x1t = [p.S("x1tA", [128, 512], F32)]
    s2t = [p.S("s2tA", [128, 512], F32)]
    ctp = [p.S(f"ctp{i}", [128, 512], F32) for i in range(2)]
    outw = [Buf() for _ in range(4)]

    NTS = T // 512
    units = [(th, e_, ts) for th in range(2) for e_ in range(NEXP)
             for ts in range(th * (NTS // 2), (th + 1) * (NTS // 2))]

    def emit_w2load(e_):
        for qq in range(4):
            p.dma("sy", w2sb[:, qq * 8 * D:(qq + 1) * 8 * D],
                  w2bf[e_, :, qq * 8 * D:(qq + 1) * 8 * D],
                  reads=(w2bufs[e_],), writes=(w2qb[qq],))

    def emit_front(u):
        th, e_, ts = units[u]
        tsl = slice(ts * 512, (ts + 1) * 512)
        p.dma("sy", xn1s[:, :].rearrange("p (fc t) -> p fc t", fc=FC),
              xn1ap.rearrange("(fc p) t -> p fc t", p=128)[:, :, tsl],
              reads=(xn1b,), writes=(xn1s_b,))
        mr, mr_b = mrow[u % 2]
        p.dma("sy", mr[:, :], ohap[e_:e_ + 1, tsl], reads=(oh_b,), writes=(mr_b,))
        obase = 64 * (u % 2)
        p.op("pe", lambda e, mr=mr, obase=obase: e.matmul(
            PB[4][0][:, :], onrb2[obase:obase + 1, :], mr[0:1, :],
            start=True, stop=True),
             reads=(onrb2_b, mr_b), writes=(PB[4][1],))
        for ic in range(FC):
            p.op("ve", lambda e, ic=ic: e.tensor_tensor(xnm[:, ic * 512:(ic + 1) * 512],
                                                        xn1s[:, ic * 512:(ic + 1) * 512],
                                                        PB[4][0][:, :], OP.mult),
                 reads=(xn1s_b, PB[4][1]), writes=(xnm_b,))
        wt, wb = w1t[0]
        p.dma("se", wt[:, :], w1bf[e_, :, 0:FC * 512], reads=(w1bufs[e_],), writes=(wb,))

    for u, (th, e_, ts) in enumerate(units):
        tsl = slice(ts * 512, (ts + 1) * 512)
        if u == 0:
            emit_front(0)
        if ts == th * (NTS // 2):
            emit_w2load(e_)
        for hc in range(HDIM // HC):
            wt, wb = w1t[hc % 2]
            if hc + 1 < HDIM // HC:
                nwt, nwb = w1t[(hc + 1) % 2]
                p.dma("se", nwt[:, :], w1bf[e_, :, (hc + 1) * FC * 512:(hc + 2) * FC * 512],
                      reads=(w1bufs[e_],), writes=(nwb,))
            for hs in range(HC // 128):
                pb = PB[(hc * 4 + hs) % 2]
                for ic in range(FC):
                    p.op("pe", lambda e, hc=hc, hs=hs, ic=ic, pb=pb, wt=wt: e.matmul(
                        pb[0][:, :], wt[:, ic * HC + hs * 128: ic * HC + (hs + 1) * 128],
                        xnm[:, ic * 512:(ic + 1) * 512],
                        start=(ic == 0), stop=(ic == FC - 1)),
                        reads=(wb, xnm_b), writes=(pb[1],))
                p.act(AF.Gelu, h_fm[:, (hc * 4 + hs) * 512:(hc * 4 + hs + 1) * 512], pb[0][:, :],
                      reads=(pb[1],), writes=(h_fm_b,))
        if u + 1 < len(units):
            emit_front(u + 1)
        for dc in range(FC):
            pb = PB[2 + dc % 2]
            for ht in range(32):
                p.op("pe", lambda e, dc=dc, ht=ht, pb=pb: e.matmul(
                    pb[0][:, :], w2sb[:, ht * D + dc * 128: ht * D + (dc + 1) * 128],
                    h_fm[:, ht * 512:(ht + 1) * 512],
                    start=(ht == 0), stop=(ht == 31)),
                    reads=(w2qb[ht // 8], h_fm_b), writes=(pb[1],))
            yofs = dc * (T // 2) + (ts % (NTS // 2)) * 512
            ydst, ydst_b = ybufs[e_ % 2]
            ysrc, ysrc_b = ybufs[(e_ + 1) % 2]
            ysl = ydst[:, yofs: yofs + 512]
            if e_ == 0:
                p.op("ve", lambda e, ysl=ysl, pb=pb: e.tensor_copy(ysl, pb[0][:, :]),
                     reads=(pb[1],), writes=(ydst_b,))
            else:
                p.op("ve", lambda e, ysl=ysl, pb=pb, ysrc=ysrc, yofs=yofs: e.tensor_tensor(
                    ysl, pb[0][:, :], ysrc[:, yofs: yofs + 512], OP.add),
                     reads=(pb[1], ysrc_b), writes=(ydst_b,))
            if e_ == NEXP - 1:
                xt, xtb = x1t[0]
                st, stb = s2t[0]
                p.dma("sy", xt[:, :], x1ap[dc * 128:(dc + 1) * 128, tsl], reads=(x1b,), writes=(xtb,))
                p.dma("sy", st[:, :], s2ap[dc * 128:(dc + 1) * 128, tsl], reads=(s2b,), writes=(stb,))
                c0 = ctp[0]
                c1 = ctp[1]
                p.act(AF.Tanh, PB[5][0][:, :], ysl, scale=0.5, reads=(ydst_b,), writes=(PB[5][1],))
                p.op("ve", lambda e, ysl=ysl, c0=c0, st=st: e.tensor_tensor(c0[0][:, :], st[:, :],
                                                                            ysl, OP.mult),
                     reads=(ydst_b, stb), writes=(c0[1],))
                p.op("ve", lambda e, c0=c0, c1=c1: e.tensor_tensor(c1[0][:, :], c0[0][:, :],
                                                                   PB[5][0][:, :], OP.mult),
                     reads=(c0[1], PB[5][1]), writes=(c1[1],))
                p.op("ve", lambda e, c0=c0, st=st, xt=xt: e.scalar_tensor_tensor(st[:, :], c0[0][:, :],
                                                                                 0.5, xt[:, :],
                                                                                 OP.mult, OP.add),
                     reads=(c0[1], xtb), writes=(stb,))
                p.op("ve", lambda e, c1=c1, st=st, c0=c0: e.scalar_tensor_tensor(c0[0][:, :], c1[0][:, :],
                                                                                 0.5, st[:, :],
                                                                                 OP.mult, OP.add),
                     reads=(c1[1], stb), writes=(c0[1],))
                p.dma("sy", out[dc * 128:(dc + 1) * 128, tsl], c0[0][:, :],
                      reads=(c0[1],), writes=(outw[dc % 4],))


def _mk_dram(p, external):
    kind = "ExternalOutput" if external == "out" else ("ExternalInput" if external == "in" else "Internal")
    dram = {}
    specs = (("x1scr", [D, T], F32), ("s2scr", [D, T], F32),
             ("xn1scr", [D, T], BF16), ("ohscr", [NEXP, T], BF16))
    for nm, shape, dt_ in specs:
        if external == "internal":
            t, b = p.DR(nm, shape, dt_, kind="Internal")
        else:
            t = p.nc.dram_tensor(nm, shape, dt_, kind=kind)
            b = Buf()
        dram[nm] = (t, b)
    return dram


def _build(mode):
    nc = bass.Bass("TRN2", target_bir_lowering=False)
    p = Prog(nc)
    p.setup_act_guard()
    if mode == "p1":
        dram = _mk_dram(p, "out")
        build_p1(nc, p, dram)
    elif mode == "p2":
        dram = _mk_dram(p, "in")
        build_p2(nc, p, dram)
    else:
        dram = _mk_dram(p, "internal")
        scope1 = ExitStack()
        p.default_scope = scope1
        build_p1(nc, p, dram)
        p.default_scope = None
        p.fence("po")
        snap = {s: v for s, v in p.cnt.items() if v > 0}
        for e in Prog.ENGS:
            if e == "po":
                continue
            for s, v in snap.items():
                p._wait(e, s, v)
        w1e = nc.dram_tensor("L1_ffn_W1", [NEXP, D, HDIM], F32, kind="ExternalInput").ap()
        w2e = nc.dram_tensor("L1_ffn_W2", [NEXP, HDIM, D], F32, kind="ExternalInput").ap()
        moebf = build_p2_cast(nc, p, w1e, w2e)
        scope1.close()
        p.extra_dep = dict(snap)
        build_p2(nc, p, dram, moebf=moebf, ext_w=(w1e, w2e))
    p.final_wait()
    p.es.close()
    return nc


P1_KEYS = ("xT", "L0_attn_basis", "L0_attn_q_coeffs", "L0_attn_k_coeffs",
           "L0_attn_v_coeffs", "L0_attn_o_coeffs", "L0_attn_decay",
           "L0_attn_out_scale", "L0_ffn_Win", "L0_ffn_bin_", "L0_ffn_W1",
           "L0_ffn_b1", "L0_ffn_W2", "L0_ffn_b2", "L0_ffn_Wout", "L0_ffn_bout",
           "L1_attn_W_in", "L1_attn_b_in", "L1_attn_W_gx", "L1_attn_b_gx",
           "L1_attn_W_ga", "L1_attn_b_ga", "L1_attn_a_param", "L1_attn_W_out",
           "L1_attn_b_out", "L1_ffn_Wg", "L1_ffn_bg",
           "ident_f32", "ident_bf", "ones_col", "ones_row", "ones_row_bfs",
           "iotaE", "trimask", "iotarow", "revcol")
P2_KEYS = ("L1_ffn_W1", "L1_ffn_W2", "ones_row_bf2")


def _input_maps(inputs):
    consts = _consts_np()
    x = np.asarray(inputs["L0_attn_x"])
    base = {}
    for k, v in inputs.items():
        if k == "L0_attn_x":
            continue
        a = np.asarray(v, dtype=np.float32)
        if k in ("L0_attn_decay", "L0_attn_out_scale"):
            a = a.reshape(1, 1)
        if a.ndim == 1:
            a = a.reshape(1, -1)
        base[k] = np.ascontiguousarray(a)
    for k, v in consts.items():
        base[k] = v
    base["ones_row_bf2"] = consts["ones_row_bfs"]
    maps = []
    for b in range(B):
        m = dict(base)
        m["xT"] = np.ascontiguousarray(x[b].T.astype(np.float32))
        maps.append(m)
    return maps


FUSED = True


def kernel(**inputs):
    maps = _input_maps(inputs)
    if FUSED:
        nc = _build("full")
        keys = tuple(set(P1_KEYS) | set(P2_KEYS))
        inm = [{k: m[k] for k in keys} for m in maps]
        rr = run_bass_kernel_spmd(nc, inm, core_ids=list(range(B)))
        outs = []
        for b in range(B):
            xt2 = np.asarray(rr.results[b]["xT2"], dtype=np.float32)
            outs.append(np.ascontiguousarray(xt2.T))
        return np.stack(outs, axis=0)
    nc1 = _build("p1")
    in1 = [{k: m[k] for k in P1_KEYS} for m in maps]
    r1 = run_bass_kernel_spmd(nc1, in1, core_ids=list(range(B)))
    r1m = [{k: np.array(v) for k, v in r.items()} for r in r1.results]
    nc2 = _build("p2")
    in2 = []
    for b in range(B):
        m = {k: maps[b][k] for k in P2_KEYS}
        for nm in ("x1scr", "s2scr", "xn1scr", "ohscr"):
            m[nm] = r1m[b][nm]
        in2.append(m)
    r2 = run_bass_kernel_spmd(nc2, in2, core_ids=list(range(B)))
    outs = []
    for b in range(B):
        xt2 = np.asarray(r2.results[b]["xT2"], dtype=np.float32)
        outs.append(np.ascontiguousarray(xt2.T))
    return np.stack(outs, axis=0)
